# version 02
# baseline (speedup 1.0000x reference)
import numpy as np
from contextlib import ExitStack

import concourse.bass as bass
import concourse.mybir as mybir
from concourse.bass import broadcast_tensor_aps
from concourse.bass_utils import run_bass_kernel_spmd

F32 = mybir.dt.float32
BF16 = mybir.dt.bfloat16
AF = mybir.ActivationFunctionType
ALU = mybir.AluOpType
AX = mybir.AxisListType

T = 2048
D = 1024
H = 16
E = 8
FF = 4096
TB = 1024
EPS = 1e-5
NCORES = 8

ENG2BASS = {"ten": "tensor", "act": "scalar", "dve": "vector", "pool": "gpsimd", "sp": "sync"}
LATE_METHODS = {"bn_stats", "bn_aggr", "tensor_reduce", "reduce_max", "reduce_sum",
                "reciprocal", "dma_start", "memset"}


class Rec:
    __slots__ = ("eng", "method", "args", "kw", "waits", "need_inc", "inc_amt", "val", "late", "idx")

    def __init__(self, eng, method, args, kw, late, idx):
        self.eng = eng
        self.method = method
        self.args = args
        self.kw = kw
        self.waits = []
        self.need_inc = False
        self.inc_amt = 16 if method == "dma_start" else 1
        self.val = None
        self.late = late
        self.idx = idx


class Res:
    __slots__ = ("name", "w", "rs")

    def __init__(self, name="r"):
        self.name = name
        self.w = None
        self.rs = []


class B:
    def __init__(self, nc):
        self.nc = nc
        self.q = {e: [] for e in ENG2BASS}
        self.pending = {e: [] for e in ENG2BASS}
        self.nidx = 0
        self.ctr = {e: 0 for e in ENG2BASS}
        self.hwm = {e: {} for e in ENG2BASS}
        self.all_res = []
        self.sems = None

    def res(self, name="r"):
        r = Res(name)
        self.all_res.append(r)
        return r

    def op(self, eng, method, *args, reads=(), writes=(), late=None, **kw):
        if late is None:
            late = method in LATE_METHODS
            if not late and eng in ("dve", "act", "pool") and args:
                try:
                    shp = args[0].shape
                    n_free = 1
                    for d in shp[1:]:
                        n_free *= d
                    if n_free < 512:
                        late = True
                except Exception:
                    pass
        r = Rec(eng, method, args, kw, late, self.nidx)
        self.nidx += 1
        if method == "dma_start":
            r.need_inc = True
        deps = []
        for res in reads:
            if res.w is not None:
                deps.append(res.w)
        for res in writes:
            if res.w is not None:
                deps.append(res.w)
            deps.extend(res.rs)
        deps.extend(self.pending[eng])
        self.pending[eng] = []
        best = {}
        for d in deps:
            if d.eng == eng and not d.late:
                continue
            cur = best.get(d.eng)
            if cur is None or d.idx > cur.idx:
                best[d.eng] = d
        for d in best.values():
            d.need_inc = True
            r.waits.append(d)
        for res in writes:
            res.w = r
            res.rs = []
        for res in reads:
            res.rs.append(r)
        self.q[eng].append(r)
        return r

    def hard_barrier(self):
        lasts = [self.q[e][-1] for e in ENG2BASS if self.q[e]]
        for d in lasts:
            d.need_inc = True
        for e in ENG2BASS:
            self.pending[e] = list(lasts)
        for r in self.all_res:
            r.w = None
            r.rs = []

    def finalize_range(self, lo, hi):
        for e in ENG2BASS:
            for r in self.q[e]:
                if lo <= r.idx < hi and r.need_inc:
                    self.ctr[e] += r.inc_amt
                    r.val = self.ctr[e]

    def emit_range(self, block, lo, hi):
        sems = self.sems
        for e, bname in ENG2BASS.items():
            recs = [r for r in self.q[e] if lo <= r.idx < hi]
            if not recs:
                continue

            def emit(h, recs=recs, e=e):
                hw = self.hwm[e]
                for r in recs:
                    for d in r.waits:
                        assert d.val is not None, (r.method, d.method, d.eng)
                        if hw.get(d.eng, 0) >= d.val:
                            continue
                        hw[d.eng] = d.val
                        h.wait_ge(sems[d.eng], d.val)
                    if r.method == "wait_only":
                        continue
                    ins = getattr(h, r.method)(*r.args, **r.kw)
                    if r.need_inc:
                        ins.then_inc(sems[r.eng], r.inc_amt)

            getattr(block, bname)(emit)


def build_program(P_SEQ=4, dbg=()):
    NTOK = P_SEQ * T
    NTB = NTOK // TB
    NHALF = T // TB
    nc = bass.Bass("TRN2", target_bir_lowering=False)

    def dram(name, shape, dtype, kind):
        if name in dbg:
            kind = "ExternalOutput"
        return nc.dram_tensor(name, shape, dtype, kind=kind).ap()

    x_in = dram("L0_attn_x", [P_SEQ, T, D], F32, "ExternalInput")
    ga_in = dram("L0_attn_ln_gamma", [D], F32, "ExternalInput")
    ba_in = dram("L0_attn_ln_beta", [D], F32, "ExternalInput")
    wqkv_in = dram("L0_attn_w_qkv", [D, 3 * D], F32, "ExternalInput")
    bqkv_in = dram("L0_attn_b_qkv", [3 * D], F32, "ExternalInput")
    wout_in = dram("L0_attn_w_out", [D, D], F32, "ExternalInput")
    bout_in = dram("L0_attn_b_out", [D], F32, "ExternalInput")
    gf_in = dram("L0_ffn_gamma", [D], F32, "ExternalInput")
    bf_in = dram("L0_ffn_beta", [D], F32, "ExternalInput")
    wg_in = dram("L0_ffn_Wg", [E, D], F32, "ExternalInput")
    w1_in = dram("L0_ffn_W1", [E, D, FF], F32, "ExternalInput")
    b1_in = dram("L0_ffn_b1", [E, FF], F32, "ExternalInput")
    w2_in = dram("L0_ffn_W2", [E, FF, D], F32, "ExternalInput")
    b2_in = dram("L0_ffn_b2", [E, D], F32, "ExternalInput")
    out_d = dram("out", [P_SEQ, T, D], F32, "ExternalOutput")

    wqk_scr = dram("wqk_scr", [16, 8, 128, 128], BF16, "Internal")
    wv_scr = dram("wv_scr", [8, 128, D], BF16, "Internal")
    wout_scr = dram("wout_scr", [8, 128, D], BF16, "Internal")
    w1_scr = dram("w1_scr", [E, 32, 8, 128, 128], BF16, "Internal")
    w2_scr = dram("w2_scr", [E, 2, 32, 128, 512], BF16, "Internal")
    qk_park = dram("qk_park", [2, 8, 128, T], BF16, "Internal")
    y1_park = dram("y1_park", [NTOK, D], F32, "Internal")
    xn_park = dram("xn_park", [NTOK // 128, 128, 8 * 128], BF16, "Internal")
    gtm_park = dram("gtm_park", [P_SEQ, 128, 16 * E], F32, "Internal")
    gfm_park = dram("gfm_park", [P_SEQ, E, T], F32, "Internal")

    b = B(nc)

    with ExitStack() as es:
        b.sems = {e: es.enter_context(nc.semaphore(f"sem_{e}")) for e in ENG2BASS}

        def sb(name, shape, dtype):
            return es.enter_context(nc.sbuf_tensor(name, shape, dtype))

        pb = [es.enter_context(nc.psum_tensor(f"pb{i}", [128, 512], F32)) for i in range(8)]
        pres = [b.res(f"pb{i}") for i in range(8)]

        idn_f = sb("idn_f", [128, 128], F32)
        ones_c64 = sb("ones_c64", [64, 1], BF16)
        ones_r128 = sb("ones_r128", [1, 128], F32)
        ones_r64f = sb("ones_r64f", [1, 64], F32)
        b1_col = sb("b1_col", [128, 32 * E], F32)
        b2_sb = sb("b2_sb", [E, D], F32)
        wg_col = sb("wg_col", [128, 8 * E], BF16)
        bq_col = sb("bq_col", [128, 16], F32)
        ga_bc = sb("ga_bc", [128, D], BF16)
        ba_bc = sb("ba_bc", [128, D], BF16)
        gf_bc = sb("gf_bc", [128, D], BF16)
        bf_bc = sb("bf_bc", [128, D], BF16)
        bout_bc = sb("bout_bc", [128, D], BF16)
        bvr_bc = sb("bvr_bc", [128, D], BF16)

        r_consts = b.res("consts")
        r_wqk_scr = b.res(); r_wv_scr = b.res(); r_wout_scr = b.res()
        r_w1_scr = [b.res() for _ in range(E)]
        r_w2_scr = [b.res() for _ in range(E)]
        r_qkpark = b.res()
        r_y1park = b.res(); r_xnpark = b.res(); r_gtmpark = b.res(); r_gfmpark = b.res()
        r_out = b.res()

        block = es.enter_context(nc.Block())

        for oc in range(16):
            src = wqkv_in[:, oc * 128:(oc + 1) * 128].rearrange("(dc p) c -> dc p c", p=128)
            b.op("pool", "dma_start", out=wqk_scr[oc], in_=src, writes=[r_wqk_scr])
        for dc in range(8):
            b.op("pool", "dma_start", out=wv_scr[dc],
                 in_=wqkv_in[dc * 128:(dc + 1) * 128, 2 * D:3 * D], writes=[r_wv_scr])
            b.op("pool", "dma_start", out=wout_scr[dc],
                 in_=wout_in[dc * 128:(dc + 1) * 128, :], writes=[r_wout_scr])

        b.op("pool", "memset", idn_f[:], 0.0, writes=[r_consts])
        b.op("pool", "affine_select", out=idn_f[:], in_=idn_f[:], compare_op=ALU.not_equal,
             fill=1.0, base=0, pattern=[[-1, 128]], channel_multiplier=1,
             reads=[r_consts], writes=[r_consts])
        b.op("pool", "memset", ones_c64[:], 1.0, writes=[r_consts])
        b.op("pool", "memset", ones_r128[:], 1.0, writes=[r_consts])
        b.op("pool", "memset", ones_r64f[:], 1.0, writes=[r_consts])

        with ExitStack() as es_pro:
            rowbuf = es_pro.enter_context(nc.sbuf_tensor("rowbuf", [1, 3 * D], F32))
            b1_sb8 = es_pro.enter_context(nc.sbuf_tensor("b1_sb8", [E, FF], F32))
            wg_sb8 = es_pro.enter_context(nc.sbuf_tensor("wg_sb8", [E, D], F32))
            row_r = b.res(); s8_r = b.res()

            def bcast_row(dst, ofs, n):
                for j0 in range(0, n, 512):
                    w = min(512, n - j0)
                    b.op("ten", "matmul", pb[7][0:128, 0:w], ones_r128[:, :],
                         rowbuf[0:1, ofs + j0: ofs + j0 + w], start=True, stop=True,
                         reads=[r_consts, row_r], writes=[pres[7]])
                    b.op("act", "activation", dst[:, j0:j0 + w], pb[7][0:128, 0:w], AF.Identity,
                         reads=[pres[7]], writes=[r_consts])

            for src_t, dst in [(ga_in, ga_bc), (ba_in, ba_bc), (gf_in, gf_bc),
                               (bf_in, bf_bc), (bout_in, bout_bc)]:
                b.op("sp", "dma_start", out=rowbuf[0:1, 0:D],
                     in_=src_t.rearrange("(a d) -> a d", a=1), writes=[row_r])
                bcast_row(dst, 0, D)
            b.op("sp", "dma_start", out=rowbuf[0:1, 0:3 * D],
                 in_=bqkv_in.rearrange("(a d) -> a d", a=1), writes=[row_r])
            for oc in range(16):
                b.op("ten", "transpose", pb[6][0:128, 0:1],
                     rowbuf[0:1, oc * 128:(oc + 1) * 128], idn_f[0:1, 0:1],
                     reads=[row_r, r_consts], writes=[pres[6]])
                b.op("act", "activation", bq_col[:, oc:oc + 1], pb[6][0:128, 0:1], AF.Identity,
                     reads=[pres[6]], writes=[r_consts])
            bcast_row(bvr_bc, 2 * D, D)
            b.op("sp", "dma_start", out=b1_sb8[:, :], in_=b1_in[:, :], writes=[s8_r])
            for fc in range(32):
                b.op("ten", "transpose", pb[6][0:128, 0:8],
                     b1_sb8[:, fc * 128:(fc + 1) * 128], idn_f[0:8, 0:8],
                     reads=[s8_r, r_consts], writes=[pres[6]])
                b.op("act", "activation", b1_col[:, fc * 8:(fc + 1) * 8], pb[6][0:128, 0:8],
                     AF.Identity, reads=[pres[6]], writes=[r_consts])
            b.op("sp", "dma_start", out=wg_sb8[:, :], in_=wg_in[:, :], writes=[s8_r])
            for dc in range(8):
                b.op("ten", "transpose", pb[6][0:128, 0:8],
                     wg_sb8[:, dc * 128:(dc + 1) * 128], idn_f[0:8, 0:8],
                     reads=[s8_r, r_consts], writes=[pres[6]])
                b.op("dve", "tensor_copy", wg_col[:, dc * 8:(dc + 1) * 8], pb[6][0:128, 0:8],
                     reads=[pres[6]], writes=[r_consts])
            b.op("sp", "dma_start", out=b2_sb[:, :], in_=b2_in[:, :], writes=[r_consts])

            b.hard_barrier()
            pro_end = b.nidx
            b.finalize_range(0, pro_end)
            b.emit_range(block, 0, pro_end)

        with ExitStack() as es_a:
            def sba(name, shape, dtype):
                return es_a.enter_context(nc.sbuf_tensor(name, shape, dtype))

            u_fm = sba("u_fm", [128, 8 * T], BF16); r_ufm = b.res("ufm")
            x_t = [sba(f"x_t{i}", [128, D], F32) for i in range(2)]
            r_xt = [b.res() for _ in range(2)]
            u0 = [sba(f"u0_{i}", [128, D], F32) for i in range(2)]
            r_u0 = [b.res() for _ in range(2)]
            bn6 = [sba(f"bn6_{i}", [128, 12], F32) for i in range(2)]
            r_bn6 = [b.res() for _ in range(2)]
            mv = [sba(f"mv_{i}", [128, 2], F32) for i in range(2)]
            r_mv = [b.res() for _ in range(2)]
            vpe = [sba(f"vpe_{i}", [128, 1], F32) for i in range(2)]
            r_vpe = [b.res() for _ in range(2)]
            ngm = [sba(f"ngm_{i}", [128, 1], F32) for i in range(2)]
            r_ngm = [b.res() for _ in range(2)]
            rsd = [sba(f"rsd_{i}", [128, 1], F32) for i in range(2)]
            r_rsd = [b.res() for _ in range(2)]
            wqk_buf = [sba(f"wqk{i}", [128, 8 * 128], BF16) for i in range(2)]
            r_wqk = [b.res() for _ in range(2)]
            wvo_sb = sba("wvo_sb", [128, 8 * D], BF16); r_wvo = b.res()
            qk_stage = [sba("qks0", [128, T], BF16)] * 2
            _rq = b.res()
            r_qks = [_rq, _rq]
            v_full = sba("v_full", [128, 16 * H * 65], BF16); r_vfull = b.res("vfull")
            qa = [sba(f"qa{i}", [65, 512], BF16) for i in range(2)]
            r_qa_d = [b.res() for _ in range(2)]; r_qa_b = [b.res() for _ in range(2)]
            ka = [sba(f"ka{i}", [65, T], BF16) for i in range(2)]
            r_ka_d = [b.res() for _ in range(2)]; r_ka_o = [b.res() for _ in range(2)]
            sqt = [sba(f"sqt{i}", [64, 512], BF16) for i in range(2)]
            r_sqt = [b.res() for _ in range(2)]
            kmx = sba("kmx", [1, 16], F32); r_kmx = b.res()
            sc2 = [sba(f"sc2_{i}", [1, 1], F32) for i in range(2)]
            r_sc2 = [b.res() for _ in range(2)]
            lnq = [sba(f"lnq{i}", [1, 512], F32) for i in range(2)]
            r_lnq = [b.res() for _ in range(2)]
            P_sb = sba("P_sb", [128, 16 * 512], BF16)
            r_P = [b.res() for _ in range(16)]
            rcp = [sba(f"rcp{i}", [1, 512], F32) for i in range(2)]
            r_rcp = [b.res() for _ in range(2)]
            bcn = [sba(f"bcn{i}", [64, 512], F32) for i in range(2)]
            r_bcn = [b.res() for _ in range(2)]
            att = sba("att", [128, 8 * 512], BF16); r_att = b.res()
            _at = sba("attmp0", [64, 512], BF16)
            att_tmp = [_at, _at]
            _ra = b.res()
            r_attmp = [_ra, _ra]
            zt = [sba(f"zt{i}", [128, D], F32) for i in range(2)]
            r_zt = [b.res() for _ in range(2)]
            y1_t = [sba(f"y1t{i}", [128, D], F32) for i in range(2)]
            r_y1t = [b.res() for _ in range(2)]
            xn0 = [sba(f"xn0_{i}", [128, D], F32) for i in range(2)]
            r_xn0 = [b.res() for _ in range(2)]
            xn_st = [sba(f"xnst{i}", [128, 8 * 128], BF16) for i in range(2)]
            r_xnst = [b.res() for _ in range(2)]
            lg_t = [sba(f"lgt{i}", [8, 128], F32) for i in range(2)]
            r_lgt = [b.res() for _ in range(2)]
            gt0_all = sba("gt0_all", [128, 16 * E], F32); r_gt0a = b.res()
            gsub = sba("gsub", [128, 16 * E], F32); r_gsub = b.res()
            gp8 = sba("gp8", [128, 16 * E], F32); r_gp8 = b.res()
            gprob = sba("gprob", [128, 16 * E], F32); r_gprob = b.res()
            gam = sba("gam", [128, 16 * E], F32); r_gam = b.res()
            gbm = sba("gbm", [128, 16 * E], F32); r_gbm = b.res()
            gis1 = sba("gis1", [128, 16 * E], F32); r_gis1 = b.res()
            gpmx, r_gpmx = gp8, r_gp8
            gmsk, r_gmsk = gt0_all, r_gt0a
            gtmp, r_gtmp = gbm, r_gbm
            gm1 = sba("gm1", [128, 16], F32); r_gm1 = b.res()
            gsum = sba("gsum", [128, 16], F32); r_gsum = b.res()
            grs = sba("grs", [128, 16], F32); r_grs = b.res()
            gt1 = sba("gt1", [128, 16], F32); r_gt1 = b.res()
            gt2 = sba("gt2", [128, 16], F32); r_gt2 = b.res()
            gden = sba("gden", [128, 16], F32); r_gden = b.res()
            gtm_st, r_gtmst = gprob, r_gprob
            gfm_st = sba("gfm_st", [8, 512], F32); r_gfmst = b.res()

            gv = lambda t: t[:, :].rearrange("p (g e) -> p g e", e=E)
            g1 = lambda t: t[:, :].rearrange("p (g o) -> p g o", o=1)

            def bTT(out_t, rout, a_t, ra, b_t, rb, op):
                a2, b2 = broadcast_tensor_aps(gv(a_t), g1(b_t))
                b.op("dve", "tensor_tensor", gv(out_t), a2, b2, op,
                     reads=[ra, rb], writes=[rout])

            def plain_ln(pi, src, rsrc, dst, rdst):
                b.op("dve", "bn_stats", bn6[pi][:, 0:6], src[:, 0:512],
                     reads=[rsrc], writes=[r_bn6[pi]])
                b.op("dve", "bn_stats", bn6[pi][:, 6:12], src[:, 512:1024],
                     reads=[rsrc], writes=[r_bn6[pi]])
                b.op("dve", "bn_aggr", mv[pi][:], bn6[pi][:, 0:12],
                     reads=[r_bn6[pi]], writes=[r_mv[pi]])
                b.op("dve", "tensor_scalar", ngm[pi][:], mv[pi][:, 0:1], -1.0, None, ALU.mult,
                     reads=[r_mv[pi]], writes=[r_ngm[pi]])
                b.op("dve", "tensor_scalar", vpe[pi][:], mv[pi][:, 1:2], EPS, None, ALU.add,
                     reads=[r_mv[pi]], writes=[r_vpe[pi]])
                b.op("act", "activation", rsd[pi][:], vpe[pi][:], AF.Ln,
                     reads=[r_vpe[pi]], writes=[r_rsd[pi]])
                b.op("act", "activation", rsd[pi][:], rsd[pi][:], AF.Exp, scale=-0.5,
                     reads=[r_rsd[pi]], writes=[r_rsd[pi]])
                b.op("act", "activation", dst[:], src[:], AF.Identity, bias=ngm[pi][:],
                     reads=[rsrc, r_ngm[pi]], writes=[rdst])
                b.op("act", "activation", dst[:], dst[:], AF.Identity, scale=rsd[pi][:],
                     reads=[rdst, r_rsd[pi]], writes=[rdst])

            for s in range(P_SEQ):
                for dc in range(8):
                    b.op("pool", "dma_start", out=wvo_sb[:, dc * D:(dc + 1) * D],
                         in_=wv_scr[dc], reads=[r_wv_scr], writes=[r_wvo])
                if s == 0:
                    for tc in range(16):
                        b.op("dve", "memset",
                             v_full[:, tc * H * 65:(tc + 1) * H * 65].rearrange(
                                 "p (h c) -> p h c", c=65)[:, :, 64:65],
                             1.0, writes=[r_vfull], late=True)
                for tt in range(16):
                    pi = tt % 2
                    xb = x_t[pi]; rx = r_xt[pi]
                    b.op("sp", "dma_start", out=xb[:],
                         in_=x_in[s, tt * 128:(tt + 1) * 128, :], writes=[rx])
                    plain_ln(pi, xb, rx, u0[pi], r_u0[pi])
                    b.op("dve", "tensor_tensor", u0[pi][:], u0[pi][:], ga_bc[:], ALU.mult,
                         reads=[r_u0[pi], r_consts], writes=[r_u0[pi]])
                    b.op("dve", "tensor_tensor", u0[pi][:], u0[pi][:], ba_bc[:], ALU.add,
                         reads=[r_u0[pi], r_consts], writes=[r_u0[pi]])
                    for half in range(2):
                        bank = 6 + half
                        for j in range(4):
                            dc = half * 4 + j
                            b.op("ten", "transpose", pb[bank][0:128, j * 128:(j + 1) * 128],
                                 u0[pi][:, dc * 128:(dc + 1) * 128], idn_f[:],
                                 reads=[r_u0[pi], r_consts], writes=[pres[bank]])
                        dst = u_fm[:, 0:8 * T].rearrange("p (dc t) -> p dc t", dc=8)[
                            :, half * 4:(half + 1) * 4, tt * 128:(tt + 1) * 128]
                        b.op("dve", "tensor_copy", dst,
                             pb[bank][0:128, 0:512].rearrange("p (dc t) -> p dc t", dc=4),
                             reads=[pres[bank]], writes=[r_ufm])
                for oc in range(16):
                    wb = wqk_buf[oc % 2]; rw = r_wqk[oc % 2]
                    b.op("pool", "dma_start",
                         out=wb[:].rearrange("p (dc c) -> p dc c", dc=8),
                         in_=wqk_scr[oc].rearrange("dc p c -> p dc c"),
                         reads=[r_wqk_scr], writes=[rw])
                    stg = qk_stage[oc % 2]; rstg = r_qks[oc % 2]
                    for ncn in range(4):
                        bank = ncn % 3
                        for dc in range(8):
                            b.op("ten", "matmul", pb[bank][0:128, 0:512],
                                 wb[:, dc * 128:(dc + 1) * 128],
                                 u_fm[:, dc * T + ncn * 512: dc * T + (ncn + 1) * 512],
                                 start=(dc == 0), stop=(dc == 7),
                                 reads=[rw, r_ufm], writes=[pres[bank]])
                        for hh in range(2):
                            b.op("act", "activation",
                                 stg[64 * hh:64 * (hh + 1), ncn * 512:(ncn + 1) * 512],
                                 pb[bank][64 * hh:64 * (hh + 1), 0:512], AF.Identity,
                                 bias=bq_col[64 * hh:64 * (hh + 1), oc:oc + 1],
                                 reads=[pres[bank], r_consts], writes=[rstg])
                    b.op("sp", "dma_start", out=qk_park[oc // 8, oc % 8], in_=stg[:],
                         reads=[rstg], writes=[r_qkpark])
                for tc in range(16):
                    for dh in range(2):
                        bank = 3 + dh
                        for dc in range(8):
                            b.op("ten", "matmul", pb[bank][0:128, 0:512],
                                 u_fm[:, dc * T + tc * 128: dc * T + (tc + 1) * 128],
                                 wvo_sb[:, dc * D + dh * 512: dc * D + (dh + 1) * 512],
                                 start=(dc == 0), stop=(dc == 7),
                                 reads=[r_ufm, r_wvo], writes=[pres[bank]])
                        for j in range(8):
                            h = dh * 8 + j
                            b.op("dve", "tensor_tensor",
                                 v_full[:, tc * H * 65 + h * 65: tc * H * 65 + h * 65 + 64],
                                 pb[bank][0:128, j * 64:(j + 1) * 64],
                                 bvr_bc[:, dh * 512 + j * 64: dh * 512 + (j + 1) * 64],
                                 ALU.add, reads=[pres[bank], r_consts], writes=[r_vfull])
                for dc in range(8):
                    b.op("pool", "dma_start", out=wvo_sb[:, dc * D:(dc + 1) * D],
                         in_=wout_scr[dc], reads=[r_wout_scr], writes=[r_wvo])

                if s == 0:
                    for e in range(E):
                        for fc in range(32):
                            src = w1_in[e, :, fc * 128:(fc + 1) * 128].rearrange(
                                "(dc p) f -> dc p f", p=128)
                            b.op("pool", "dma_start", out=w1_scr[e, fc], in_=src,
                                 writes=[r_w1_scr[e]])
                        for fc in range(32):
                            src = w2_in[e, fc * 128:(fc + 1) * 128, :].rearrange(
                                "p (dh j) -> dh p j", j=512)
                            b.op("pool", "dma_start", out=w2_scr[e, :, fc], in_=src,
                                 writes=[r_w2_scr[e]])

                for qc in range(4):
                    for h in range(H):
                        pi = h % 2
                        qb = qa[pi]; kb = ka[pi]
                        nb = 5 + pi
                        pvbank = 3 + pi
                        b.op("sp", "dma_start", out=qb[0:64, :],
                             in_=qk_park[0, h // 2, 64 * (h % 2):64 * (h % 2) + 64,
                                         qc * 512:(qc + 1) * 512],
                             reads=[r_qkpark], writes=[r_qa_d[pi]])
                        b.op("sp", "dma_start", out=kb[0:64, :],
                             in_=qk_park[1, h // 2, 64 * (h % 2):64 * (h % 2) + 64, :],
                             reads=[r_qkpark], writes=[r_ka_d[pi]])
                        if s == 0 and qc == 0 and h < 2:
                            b.op("dve", "memset", kb[64:65, :], 1.0, writes=[r_ka_o[pi]],
                                 late=True)
                        if qc == 0:
                            b.op("dve", "memset", kmx[0:1, h:h + 1], -1e30, writes=[r_kmx])
                            for i in range(4):
                                b.op("dve", "tensor_tensor", sqt[pi][:, :],
                                     kb[0:64, i * 512:(i + 1) * 512],
                                     kb[0:64, i * 512:(i + 1) * 512], ALU.mult,
                                     reads=[r_ka_d[pi]], writes=[r_sqt[pi]])
                                b.op("ten", "matmul", pb[nb][0:1, 0:512], ones_c64[:],
                                     sqt[pi][:], start=True, stop=True,
                                     reads=[r_sqt[pi], r_consts], writes=[pres[nb]])
                                b.op("dve", "tensor_reduce", sc2[pi][:], pb[nb][0:1, 0:512],
                                     AX.X, ALU.max, reads=[pres[nb]], writes=[r_sc2[pi]])
                                b.op("dve", "tensor_tensor", kmx[0:1, h:h + 1],
                                     kmx[0:1, h:h + 1], sc2[pi][:], ALU.max,
                                     reads=[r_sc2[pi], r_kmx], writes=[r_kmx])
                            b.op("act", "activation", kmx[0:1, h:h + 1], kmx[0:1, h:h + 1],
                                 AF.Ln, reads=[r_kmx], writes=[r_kmx])
                            b.op("dve", "tensor_scalar", kmx[0:1, h:h + 1],
                                 kmx[0:1, h:h + 1], 0.5, None, ALU.mult,
                                 reads=[r_kmx], writes=[r_kmx])
                        b.op("dve", "tensor_tensor", sqt[pi][:, :], qb[0:64, :], qb[0:64, :],
                             ALU.mult, reads=[r_qa_d[pi]], writes=[r_sqt[pi]])
                        b.op("ten", "matmul", pb[nb][0:1, 0:512], ones_c64[:], sqt[pi][:],
                             start=True, stop=True, reads=[r_sqt[pi], r_consts],
                             writes=[pres[nb]])
                        b.op("act", "activation", lnq[pi][:], pb[nb][0:1, 0:512], AF.Ln,
                             reads=[pres[nb]], writes=[r_lnq[pi]])
                        b.op("act", "activation", lnq[pi][:], lnq[pi][:], AF.Exp,
                             scale=0.5, bias=kmx[0:1, h:h + 1],
                             reads=[r_lnq[pi], r_kmx], writes=[r_lnq[pi]])
                        b.op("dve", "tensor_scalar", qb[64:65, :], lnq[pi][:], -1.0, None,
                             ALU.mult, reads=[r_lnq[pi]], writes=[r_qa_b[pi]])

                        def plan_scores(kc):
                            bank = kc % 3
                            b.op("ten", "matmul", pb[bank][0:128, 0:512],
                                 kb[:, kc * 128:(kc + 1) * 128], qb[:, :],
                                 start=True, stop=True,
                                 reads=[r_ka_d[pi], r_ka_o[pi], r_qa_d[pi], r_qa_b[pi]],
                                 writes=[pres[bank]])

                        def plan_exp(kc):
                            bank = kc % 3
                            b.op("act", "activation", P_sb[:, kc * 512:(kc + 1) * 512],
                                 pb[bank][0:128, 0:512], AF.Exp, scale=0.125,
                                 reads=[pres[bank]], writes=[r_P[kc]])

                        def plan_pv(kc):
                            b.op("ten", "matmul", pb[pvbank][0:65, 0:512],
                                 v_full[:, kc * H * 65 + h * 65: kc * H * 65 + (h + 1) * 65],
                                 P_sb[:, kc * 512:(kc + 1) * 512],
                                 start=(kc == 0), stop=(kc == 15),
                                 reads=[r_vfull, r_P[kc]], writes=[pres[pvbank]])

                        plan_scores(0)
                        for kc in range(1, 16):
                            plan_scores(kc)
                            plan_exp(kc - 1)
                            plan_pv(kc - 1)
                        plan_exp(15)
                        plan_pv(15)
                        b.op("dve", "reciprocal", rcp[pi][:], pb[pvbank][64:65, 0:512],
                             reads=[pres[pvbank]], writes=[r_rcp[pi]])
                        b.op("ten", "matmul", pb[7][0:64, 0:512], ones_r64f[:], rcp[pi][:],
                             start=True, stop=True, reads=[r_rcp[pi], r_consts],
                             writes=[pres[7]])
                        b.op("act", "activation", bcn[pi][:], pb[7][0:64, 0:512], AF.Identity,
                             reads=[pres[7]], writes=[r_bcn[pi]])
                        if h % 2 == 0:
                            b.op("dve", "tensor_tensor",
                                 att[0:64, (h // 2) * 512:(h // 2 + 1) * 512],
                                 pb[pvbank][0:64, 0:512], bcn[pi][:], ALU.mult,
                                 reads=[pres[pvbank], r_bcn[pi]], writes=[r_att])
                        else:
                            at = att_tmp[(h // 2) % 2]; rat = r_attmp[(h // 2) % 2]
                            b.op("dve", "tensor_tensor", at[:],
                                 pb[pvbank][0:64, 0:512], bcn[pi][:], ALU.mult,
                                 reads=[pres[pvbank], r_bcn[pi]], writes=[rat])
                            b.op("sp", "dma_start",
                                 out=att[64:128, (h // 2) * 512:(h // 2 + 1) * 512],
                                 in_=at[:], reads=[rat], writes=[r_att])
                    for tt4 in range(4):
                        tt = qc * 4 + tt4
                        gtok = s * T + tt * 128
                        p2 = tt4 % 2
                        xb = x_t[p2]; rx = r_xt[p2]
                        b.op("sp", "dma_start", out=xb[:],
                             in_=x_in[s, tt * 128:(tt + 1) * 128, :], writes=[rx])
                        for dh in range(2):
                            bank = dh
                            for dc in range(8):
                                b.op("ten", "matmul", pb[bank][0:128, 0:512],
                                     att[:, dc * 512 + tt4 * 128: dc * 512 + (tt4 + 1) * 128],
                                     wvo_sb[:, dc * D + dh * 512: dc * D + (dh + 1) * 512],
                                     start=(dc == 0), stop=(dc == 7),
                                     reads=[r_att, r_wvo], writes=[pres[bank]])
                            b.op("act", "activation", zt[p2][:, dh * 512:(dh + 1) * 512],
                                 pb[bank][0:128, 0:512], AF.Identity,
                                 reads=[pres[bank]], writes=[r_zt[p2]])
                        b.op("dve", "tensor_tensor", zt[p2][:], zt[p2][:], bout_bc[:],
                             ALU.add, reads=[r_zt[p2], r_consts], writes=[r_zt[p2]])
                        b.op("dve", "scalar_tensor_tensor", zt[p2][:], xb[:], 2.0, zt[p2][:],
                             ALU.mult, ALU.add, reads=[rx, r_zt[p2]], writes=[r_zt[p2]])
                        plain_ln(p2, zt[p2], r_zt[p2], y1_t[p2], r_y1t[p2])
                        b.op("sp", "dma_start", out=y1_park[gtok:gtok + 128, :],
                             in_=y1_t[p2][:], reads=[r_y1t[p2]], writes=[r_y1park])
                        plain_ln(p2, y1_t[p2], r_y1t[p2], xn0[p2], r_xn0[p2])
                        b.op("dve", "tensor_tensor", xn0[p2][:], xn0[p2][:], gf_bc[:],
                             ALU.mult, reads=[r_xn0[p2], r_consts], writes=[r_xn0[p2]])
                        b.op("dve", "tensor_tensor", xn0[p2][:], xn0[p2][:], bf_bc[:],
                             ALU.add, reads=[r_xn0[p2], r_consts], writes=[r_xn0[p2]])
                        for half in range(2):
                            bank = 6 + half
                            for j in range(4):
                                dc = half * 4 + j
                                b.op("ten", "transpose",
                                     pb[bank][0:128, j * 128:(j + 1) * 128],
                                     xn0[p2][:, dc * 128:(dc + 1) * 128], idn_f[:],
                                     reads=[r_xn0[p2], r_consts], writes=[pres[bank]])
                            b.op("dve", "tensor_copy",
                                 xn_st[p2][:, half * 512:(half + 1) * 512].rearrange(
                                     "p (dc t) -> p dc t", dc=4),
                                 pb[bank][0:128, 0:512].rearrange("p (dc t) -> p dc t", dc=4),
                                 reads=[pres[bank]], writes=[r_xnst[p2]])
                        b.op("sp", "dma_start", out=xn_park[gtok // 128],
                             in_=xn_st[p2][:], reads=[r_xnst[p2]], writes=[r_xnpark])
                        for dc in range(8):
                            b.op("ten", "matmul", pb[5][0:8, 0:128],
                                 wg_col[:, dc * 8:(dc + 1) * 8],
                                 xn_st[p2][:, dc * 128:(dc + 1) * 128],
                                 start=(dc == 0), stop=(dc == 7),
                                 reads=[r_consts, r_xnst[p2]], writes=[pres[5]])
                        b.op("act", "activation", lg_t[p2][:], pb[5][0:8, 0:128], AF.Identity,
                             reads=[pres[5]], writes=[r_lgt[p2]])
                        b.op("ten", "transpose", pb[5][0:128, 0:8], lg_t[p2][:],
                             idn_f[0:8, 0:8], reads=[r_lgt[p2], r_consts], writes=[pres[5]])
                        b.op("act", "activation", gt0_all[:, tt * E:(tt + 1) * E],
                             pb[5][0:128, 0:8], AF.Identity,
                             reads=[pres[5]], writes=[r_gt0a])
                b.op("dve", "tensor_reduce", gm1[:], gv(gt0_all), AX.X, ALU.max,
                     reads=[r_gt0a], writes=[r_gm1])
                bTT(gsub, r_gsub, gt0_all, r_gt0a, gm1, r_gm1, ALU.subtract)
                b.op("act", "activation", gp8[:], gsub[:], AF.Exp,
                     reads=[r_gsub], writes=[r_gp8])
                b.op("dve", "tensor_reduce", gsum[:], gv(gp8), AX.X, ALU.add,
                     reads=[r_gp8], writes=[r_gsum])
                b.op("dve", "reciprocal", grs[:], gsum[:], reads=[r_gsum], writes=[r_grs])
                bTT(gprob, r_gprob, gp8, r_gp8, grs, r_grs, ALU.mult)
                b.op("dve", "tensor_reduce", gt1[:], gv(gprob), AX.X, ALU.max,
                     reads=[r_gprob], writes=[r_gt1])
                b.op("dve", "memset", gpmx[:], -1e30, writes=[r_gpmx])
                for j in range(1, E):
                    b.op("dve", "tensor_tensor", gv(gpmx)[:, :, j:j + 1],
                         gv(gpmx)[:, :, j - 1:j], gv(gprob)[:, :, j - 1:j], ALU.max,
                         reads=[r_gpmx, r_gprob], writes=[r_gpmx], late=True)
                bTT(gam, r_gam, gprob, r_gprob, gt1, r_gt1, ALU.is_ge)
                bTT(gbm, r_gbm, gpmx, r_gpmx, gt1, r_gt1, ALU.is_lt)
                b.op("dve", "tensor_tensor", gis1[:], gam[:], gbm[:], ALU.mult,
                     reads=[r_gam, r_gbm], writes=[r_gis1])
                b.op("dve", "scalar_tensor_tensor", gmsk[:], gis1[:], -1e30, gprob[:],
                     ALU.mult, ALU.add, reads=[r_gis1, r_gprob], writes=[r_gmsk])
                b.op("dve", "tensor_reduce", gt2[:], gv(gmsk), AX.X, ALU.max,
                     reads=[r_gmsk], writes=[r_gt2])
                b.op("dve", "memset", gpmx[:], -1e30, writes=[r_gpmx])
                for j in range(1, E):
                    b.op("dve", "tensor_tensor", gv(gpmx)[:, :, j:j + 1],
                         gv(gpmx)[:, :, j - 1:j], gv(gmsk)[:, :, j - 1:j], ALU.max,
                         reads=[r_gpmx, r_gmsk], writes=[r_gpmx], late=True)
                bTT(gam, r_gam, gmsk, r_gmsk, gt2, r_gt2, ALU.is_ge)
                bTT(gbm, r_gbm, gpmx, r_gpmx, gt2, r_gt2, ALU.is_lt)
                b.op("dve", "tensor_tensor", gsub[:], gam[:], gbm[:], ALU.mult,
                     reads=[r_gam, r_gbm], writes=[r_gsub])
                b.op("dve", "tensor_tensor", gden[:], gt1[:], gt2[:], ALU.add,
                     reads=[r_gt1, r_gt2], writes=[r_gden])
                b.op("dve", "tensor_scalar", gden[:], gden[:], 1e-9, None, ALU.add,
                     reads=[r_gden], writes=[r_gden])
                b.op("dve", "reciprocal", grs[:], gden[:], reads=[r_gden], writes=[r_grs])
                b.op("dve", "tensor_tensor", gt1[:], gt1[:], grs[:], ALU.mult,
                     reads=[r_gt1, r_grs], writes=[r_gt1])
                b.op("dve", "tensor_tensor", gt2[:], gt2[:], grs[:], ALU.mult,
                     reads=[r_gt2, r_grs], writes=[r_gt2])
                bTT(gtmp, r_gtmp, gsub, r_gsub, gt2, r_gt2, ALU.mult)
                bTT(gam, r_gam, gis1, r_gis1, gt1, r_gt1, ALU.mult)
                b.op("dve", "tensor_tensor", gtm_st[:], gam[:], gtmp[:], ALU.add,
                     reads=[r_gam, r_gtmp], writes=[r_gtmst])
                b.op("sp", "dma_start", out=gtm_park[s], in_=gtm_st[:],
                     reads=[r_gtmst], writes=[r_gtmpark])
                for tt in range(16):
                    b.op("ten", "transpose", pb[5][0:8, 0:128],
                         gtm_st[:, tt * E:(tt + 1) * E], idn_f[:],
                         reads=[r_gtmst, r_consts], writes=[pres[5]])
                    b.op("dve", "tensor_copy", gfm_st[:, (tt % 4) * 128:(tt % 4 + 1) * 128],
                         pb[5][0:8, 0:128], reads=[pres[5]], writes=[r_gfmst])
                    if tt % 4 == 3:
                        b.op("sp", "dma_start",
                             out=gfm_park[s, :, (tt // 4) * 512:(tt // 4 + 1) * 512],
                             in_=gfm_st[:], reads=[r_gfmst], writes=[r_gfmpark])

            b.hard_barrier()
            attn_end = b.nidx
            b.finalize_range(pro_end, attn_end)
            b.emit_range(block, pro_end, attn_end)

        with ExitStack() as es_m:
            def sbm(name, shape, dtype):
                return es_m.enter_context(nc.sbuf_tensor(name, shape, dtype))

            xn_blk = [sbm(f"xnb{i}", [128, 8 * TB], BF16) for i in range(2)]
            r_xnb = [b.res() for _ in range(2)]
            h_sb = sbm("h_sb", [128, 32 * TB], BF16)
            r_h = [b.res() for _ in range(32)]
            acc = sbm("acc", [128, 8 * D], F32)
            acc_r = [b.res(f"acc{i}") for i in range(8)]
            w1_buf = [sbm(f"w1b{i}", [128, 4 * 8 * 128], BF16) for i in range(2)]
            r_w1b = [b.res() for _ in range(2)]
            w2_buf = [sbm(f"w2b{i}", [128, 4 * 512], BF16) for i in range(2)]
            r_w2b = [b.res() for _ in range(2)]
            gtm_sb = [sbm(f"gtmsb{i}", [128, 8 * E], F32) for i in range(2)]
            r_gtm = [b.res() for _ in range(2)]
            gfm_sb = [sbm(f"gfmsb{i}", [E, TB], F32) for i in range(2)]
            r_gfm = [b.res() for _ in range(2)]
            ytmp = [sbm(f"ytmp{i}", [128, 512], F32) for i in range(2)]
            r_yt = [b.res() for _ in range(2)]
            y1b = [sbm(f"y1b{i}", [128, D], F32) for i in range(2)]
            r_y1b = [b.res() for _ in range(2)]
            outb = [sbm(f"outb{i}", [128, D], F32) for i in range(2)]
            r_outb = [b.res() for _ in range(2)]
            bn6m = [sbm(f"bn6m{i}", [128, 12], F32) for i in range(2)]
            r_bn6m = [b.res() for _ in range(2)]
            mvm = [sbm(f"mvm{i}", [128, 2], F32) for i in range(2)]
            r_mvm = [b.res() for _ in range(2)]
            vpem = [sbm(f"vpem{i}", [128, 1], F32) for i in range(2)]
            r_vpem = [b.res() for _ in range(2)]
            ngmm = [sbm(f"ngmm{i}", [128, 1], F32) for i in range(2)]
            r_ngmm = [b.res() for _ in range(2)]
            rsdm = [sbm(f"rsdm{i}", [128, 1], F32) for i in range(2)]
            r_rsdm = [b.res() for _ in range(2)]

            ntc = TB // 128
            for tb in range(NTB):
                s = tb // NHALF
                half = tb % NHALF
                tpi = tb % 2
                xnb = xn_blk[tpi]; rxnb = r_xnb[tpi]
                xview = xnb[:, :].rearrange("p (ti dc j) -> p ti dc j", ti=8, dc=8)
                for ti in range(8):
                    b.op("sp", "dma_start", out=xnb[:, ti * 1024:(ti + 1) * 1024],
                         in_=xn_park[tb * 8 + ti], reads=[r_xnpark], writes=[rxnb])
                b.op("sp", "dma_start", out=gtm_sb[tpi][:],
                     in_=gtm_park[s][:, half * 8 * E:(half + 1) * 8 * E],
                     reads=[r_gtmpark], writes=[r_gtm[tpi]])
                b.op("sp", "dma_start", out=gfm_sb[tpi][:],
                     in_=gfm_park[s][:, half * TB:(half + 1) * TB],
                     reads=[r_gfmpark], writes=[r_gfm[tpi]])
                for dh in range(2):
                    for tc in range(ntc):
                        b.op("ten", "matmul", pb[tc][0:128, 0:512],
                             gfm_sb[tpi][:, tc * 128:(tc + 1) * 128],
                             b2_sb[:, dh * 512:(dh + 1) * 512], start=True, stop=True,
                             reads=[r_gfm[tpi], r_consts], writes=[pres[tc]])
                        b.op("act", "activation",
                             acc[:, tc * D + dh * 512: tc * D + (dh + 1) * 512],
                             pb[tc][0:128, 0:512], AF.Identity,
                             reads=[pres[tc]], writes=[acc_r[tc]])
                for e in range(E):
                    for fcg in range(8):
                        wb = w1_buf[fcg % 2]; rw = r_w1b[fcg % 2]
                        b.op("pool", "dma_start",
                             out=wb[:].rearrange("p (fi dc f) -> p fi dc f", fi=4, dc=8),
                             in_=w1_scr[e, fcg * 4:(fcg + 1) * 4].rearrange(
                                 "fi dc p f -> p fi dc f"),
                             reads=[r_w1_scr[e]], writes=[rw])
                        for fi in range(4):
                            fc = fcg * 4 + fi
                            for th in range(2):
                                bank = (fc * 2 + th) % 3
                                for dc in range(8):
                                    b.op("ten", "matmul", pb[bank][0:128, 0:512],
                                         wb[:, (fi * 8 + dc) * 128:(fi * 8 + dc + 1) * 128],
                                         xview[:, th * 4:(th + 1) * 4, dc, :],
                                         start=(dc == 0), stop=(dc == 7),
                                         reads=[rw, rxnb], writes=[pres[bank]])
                                b.op("act", "activation",
                                     h_sb[:, fc * TB + th * 512: fc * TB + (th + 1) * 512],
                                     pb[bank][0:128, 0:512], AF.Gelu_apprx_tanh,
                                     bias=b1_col[:, fc * 8 + e: fc * 8 + e + 1],
                                     reads=[pres[bank], r_consts], writes=[r_h[fc]])
                    for dh in range(2):
                        for fcg in range(8):
                            wb = w2_buf[fcg % 2]; rw = r_w2b[fcg % 2]
                            b.op("pool", "dma_start",
                                 out=wb[:].rearrange("p (fi j) -> p fi j", fi=4),
                                 in_=w2_scr[e, dh, fcg * 4:(fcg + 1) * 4].rearrange(
                                     "fi p j -> p fi j"),
                                 reads=[r_w2_scr[e]], writes=[rw])
                            for fi in range(4):
                                fc = fcg * 4 + fi
                                for tc in range(ntc):
                                    b.op("ten", "matmul", pb[tc][0:128, 0:512],
                                         h_sb[:, fc * TB + tc * 128: fc * TB + (tc + 1) * 128],
                                         wb[:, fi * 512:(fi + 1) * 512],
                                         start=(fc == 0), stop=(fc == 31),
                                         reads=[rw, r_h[fc]], writes=[pres[tc]])
                        for tc in range(ntc):
                            yb = ytmp[tc % 2]; ry = r_yt[tc % 2]
                            b.op("act", "activation", yb[:], pb[tc][0:128, 0:512], AF.Identity,
                                 scale=gtm_sb[tpi][:, tc * E + e: tc * E + e + 1],
                                 reads=[pres[tc], r_gtm[tpi]], writes=[ry])
                            b.op("dve", "tensor_tensor",
                                 acc[:, tc * D + dh * 512: tc * D + (dh + 1) * 512],
                                 acc[:, tc * D + dh * 512: tc * D + (dh + 1) * 512],
                                 yb[:], ALU.add, reads=[ry, acc_r[tc]], writes=[acc_r[tc]])
                for tc in range(ntc):
                    gtok = tb * TB + tc * 128
                    p2 = tc % 2
                    yb = y1b[p2]; ry = r_y1b[p2]
                    ob = outb[p2]; ro = r_outb[p2]
                    b.op("sp", "dma_start", out=yb[:], in_=y1_park[gtok:gtok + 128, :],
                         reads=[r_y1park], writes=[ry])
                    b.op("dve", "scalar_tensor_tensor", ob[:], yb[:], 2.0,
                         acc[:, tc * D:(tc + 1) * D], ALU.mult, ALU.add,
                         reads=[ry, acc_r[tc]], writes=[ro])
                    b.op("dve", "bn_stats", bn6m[p2][:, 0:6], ob[:, 0:512],
                         reads=[ro], writes=[r_bn6m[p2]])
                    b.op("dve", "bn_stats", bn6m[p2][:, 6:12], ob[:, 512:1024],
                         reads=[ro], writes=[r_bn6m[p2]])
                    b.op("dve", "bn_aggr", mvm[p2][:], bn6m[p2][:, 0:12],
                         reads=[r_bn6m[p2]], writes=[r_mvm[p2]])
                    b.op("dve", "tensor_scalar", ngmm[p2][:], mvm[p2][:, 0:1], -1.0, None,
                         ALU.mult, reads=[r_mvm[p2]], writes=[r_ngmm[p2]])
                    b.op("dve", "tensor_scalar", vpem[p2][:], mvm[p2][:, 1:2], EPS, None,
                         ALU.add, reads=[r_mvm[p2]], writes=[r_vpem[p2]])
                    b.op("act", "activation", rsdm[p2][:], vpem[p2][:], AF.Ln,
                         reads=[r_vpem[p2]], writes=[r_rsdm[p2]])
                    b.op("act", "activation", rsdm[p2][:], rsdm[p2][:], AF.Exp, scale=-0.5,
                         reads=[r_rsdm[p2]], writes=[r_rsdm[p2]])
                    b.op("act", "activation", ob[:], ob[:], AF.Identity, bias=ngmm[p2][:],
                         reads=[ro, r_ngmm[p2]], writes=[ro])
                    b.op("act", "activation", ob[:], ob[:], AF.Identity, scale=rsdm[p2][:],
                         reads=[ro, r_rsdm[p2]], writes=[ro])
                    b.op("sp", "dma_start",
                         out=out_d[s, half * TB + tc * 128: half * TB + (tc + 1) * 128, :],
                         in_=ob[:], reads=[ro], writes=[r_out])
            b.op("sp", "wait_only", reads=[r_out])

            moe_end = b.nidx
            b.finalize_range(attn_end, moe_end)
            b.emit_range(block, attn_end, moe_end)
    return nc


_NC_CACHE = {}


def _get_nc():
    if "nc" not in _NC_CACHE:
        _NC_CACHE["nc"] = build_program()
    return _NC_CACHE["nc"]


def kernel(**inputs):
    nc = _get_nc()
    x = np.ascontiguousarray(np.asarray(inputs["L0_attn_x"], dtype=np.float32))
    names = ["L0_attn_ln_gamma", "L0_attn_ln_beta", "L0_attn_w_qkv", "L0_attn_b_qkv",
             "L0_attn_w_out", "L0_attn_b_out", "L0_ffn_gamma", "L0_ffn_beta",
             "L0_ffn_Wg", "L0_ffn_W1", "L0_ffn_b1", "L0_ffn_W2", "L0_ffn_b2"]
    shared = {n: np.ascontiguousarray(np.asarray(inputs[n], dtype=np.float32)) for n in names}
    in_maps = []
    for c in range(NCORES):
        m = dict(shared)
        m["L0_attn_x"] = np.ascontiguousarray(x[c * 4:(c + 1) * 4])
        in_maps.append(m)
    res = run_bass_kernel_spmd(nc, in_maps, core_ids=list(range(NCORES)))
    out = np.concatenate([r["out"] for r in res.results], axis=0)
    return out


# version 03
# speedup vs baseline: 1.1554x; 1.1554x over previous
import numpy as np
from contextlib import ExitStack

import concourse.bass as bass
import concourse.mybir as mybir
from concourse.bass import broadcast_tensor_aps
from concourse.bass_utils import run_bass_kernel_spmd

F32 = mybir.dt.float32
BF16 = mybir.dt.bfloat16
AF = mybir.ActivationFunctionType
ALU = mybir.AluOpType
AX = mybir.AxisListType

T = 2048
D = 1024
H = 16
E = 8
FF = 4096
TB = 1024
EPS = 1e-5
NCORES = 8

ENG2BASS = {"ten": "tensor", "act": "scalar", "dve": "vector", "pool": "gpsimd", "sp": "sync"}
LATE_METHODS = {"bn_stats", "bn_aggr", "tensor_reduce", "reduce_max", "reduce_sum",
                "reciprocal", "dma_start", "memset"}


class Rec:
    __slots__ = ("eng", "method", "args", "kw", "waits", "need_inc", "inc_amt", "val", "late", "idx")

    def __init__(self, eng, method, args, kw, late, idx):
        self.eng = eng
        self.method = method
        self.args = args
        self.kw = kw
        self.waits = []
        self.need_inc = False
        self.inc_amt = 16 if method == "dma_start" else 1
        self.val = None
        self.late = late
        self.idx = idx


class Res:
    __slots__ = ("name", "w", "rs")

    def __init__(self, name="r"):
        self.name = name
        self.w = None
        self.rs = []


class B:
    def __init__(self, nc):
        self.nc = nc
        self.q = {e: [] for e in ENG2BASS}
        self.pending = {e: [] for e in ENG2BASS}
        self.nidx = 0
        self.ctr = {e: 0 for e in ENG2BASS}
        self.hwm = {e: {} for e in ENG2BASS}
        self.all_res = []
        self.sems = None

    def res(self, name="r"):
        r = Res(name)
        self.all_res.append(r)
        return r

    def op(self, eng, method, *args, reads=(), writes=(), late=None, **kw):
        if late is None:
            late = method in LATE_METHODS or eng in ("dve", "act", "pool")
        r = Rec(eng, method, args, kw, late, self.nidx)
        self.nidx += 1
        if method == "dma_start":
            r.need_inc = True
        deps = []
        for res in reads:
            if res.w is not None:
                deps.append(res.w)
        for res in writes:
            if res.w is not None:
                deps.append(res.w)
            deps.extend(res.rs)
        deps.extend(self.pending[eng])
        self.pending[eng] = []
        best = {}
        for d in deps:
            if d.eng == eng and not d.late:
                continue
            cur = best.get(d.eng)
            if cur is None or d.idx > cur.idx:
                best[d.eng] = d
        for d in best.values():
            d.need_inc = True
            r.waits.append(d)
        for res in writes:
            res.w = r
            res.rs = []
        for res in reads:
            res.rs.append(r)
        self.q[eng].append(r)
        return r

    def hard_barrier(self):
        lasts = [self.q[e][-1] for e in ENG2BASS if self.q[e]]
        for d in lasts:
            d.need_inc = True
        for e in ENG2BASS:
            self.pending[e] = list(lasts)
        for r in self.all_res:
            r.w = None
            r.rs = []

    def finalize_range(self, lo, hi):
        for e in ENG2BASS:
            for r in self.q[e]:
                if lo <= r.idx < hi and r.need_inc:
                    self.ctr[e] += r.inc_amt
                    r.val = self.ctr[e]

    def emit_range(self, block, lo, hi):
        sems = self.sems
        for e, bname in ENG2BASS.items():
            recs = [r for r in self.q[e] if lo <= r.idx < hi]
            if not recs:
                continue

            def emit(h, recs=recs, e=e):
                hw = self.hwm[e]
                for r in recs:
                    for d in r.waits:
                        assert d.val is not None, (r.method, d.method, d.eng)
                        if hw.get(d.eng, 0) >= d.val:
                            continue
                        hw[d.eng] = d.val
                        h.wait_ge(sems[d.eng], d.val)
                    if r.method == "wait_only":
                        continue
                    ins = getattr(h, r.method)(*r.args, **r.kw)
                    if r.need_inc:
                        ins.then_inc(sems[r.eng], r.inc_amt)

            getattr(block, bname)(emit)


def build_program(P_SEQ=4, dbg=()):
    NTOK = P_SEQ * T
    NTB = NTOK // TB
    NHALF = T // TB
    nc = bass.Bass("TRN2", target_bir_lowering=False)

    def dram(name, shape, dtype, kind):
        if name in dbg:
            kind = "ExternalOutput"
        return nc.dram_tensor(name, shape, dtype, kind=kind).ap()

    x_in = dram("L0_attn_x", [P_SEQ, T, D], F32, "ExternalInput")
    ga_in = dram("L0_attn_ln_gamma", [D], F32, "ExternalInput")
    ba_in = dram("L0_attn_ln_beta", [D], F32, "ExternalInput")
    wqkv_in = dram("L0_attn_w_qkv", [D, 3 * D], F32, "ExternalInput")
    bqkv_in = dram("L0_attn_b_qkv", [3 * D], F32, "ExternalInput")
    wout_in = dram("L0_attn_w_out", [D, D], F32, "ExternalInput")
    bout_in = dram("L0_attn_b_out", [D], F32, "ExternalInput")
    gf_in = dram("L0_ffn_gamma", [D], F32, "ExternalInput")
    bf_in = dram("L0_ffn_beta", [D], F32, "ExternalInput")
    wg_in = dram("L0_ffn_Wg", [E, D], F32, "ExternalInput")
    w1_in = dram("L0_ffn_W1", [E, D, FF], F32, "ExternalInput")
    b1_in = dram("L0_ffn_b1", [E, FF], F32, "ExternalInput")
    w2_in = dram("L0_ffn_W2", [E, FF, D], F32, "ExternalInput")
    b2_in = dram("L0_ffn_b2", [E, D], F32, "ExternalInput")
    out_d = dram("out", [P_SEQ, T, D], F32, "ExternalOutput")

    wqk_scr = dram("wqk_scr", [16, 8, 128, 128], BF16, "Internal")
    wv_scr = dram("wv_scr", [8, 128, D], BF16, "Internal")
    wout_scr = dram("wout_scr", [8, 128, D], BF16, "Internal")
    w1_scr = dram("w1_scr", [E, 32, 8, 128, 128], BF16, "Internal")
    w2_scr = dram("w2_scr", [E, 2, 32, 128, 512], BF16, "Internal")
    qk_park = dram("qk_park", [2, 8, 128, T], BF16, "Internal")
    y1_park = dram("y1_park", [NTOK, D], F32, "Internal")
    xn_park = dram("xn_park", [NTOK // 128, 128, 8 * 128], BF16, "Internal")
    gtm_park = dram("gtm_park", [P_SEQ, 128, 16 * E], F32, "Internal")
    acc_dump = dram("acc_dump", [NTOK // TB, 128, 8 * D], F32, "Internal")
    h_dump = dram("h_dump", [128, 32 * TB], BF16, "Internal")
    gfm_park = dram("gfm_park", [P_SEQ, E, T], F32, "Internal")

    b = B(nc)

    with ExitStack() as es:
        b.sems = {e: es.enter_context(nc.semaphore(f"sem_{e}")) for e in ENG2BASS}

        def sb(name, shape, dtype):
            return es.enter_context(nc.sbuf_tensor(name, shape, dtype))

        pb = [es.enter_context(nc.psum_tensor(f"pb{i}", [128, 512], F32)) for i in range(8)]
        pres = [b.res(f"pb{i}") for i in range(8)]

        idn_f = sb("idn_f", [128, 128], F32)
        ones_c64 = sb("ones_c64", [64, 1], BF16)
        ones_r128 = sb("ones_r128", [1, 128], F32)
        ones_r64f = sb("ones_r64f", [1, 64], F32)
        b1_col = sb("b1_col", [128, 32 * E], F32)
        b2_sb = sb("b2_sb", [E, D], F32)
        wg_col = sb("wg_col", [128, 8 * E], BF16)
        bq_col = sb("bq_col", [128, 16], F32)
        ga_bc = sb("ga_bc", [128, D], BF16)
        ba_bc = sb("ba_bc", [128, D], BF16)
        gf_bc = sb("gf_bc", [128, D], BF16)
        bf_bc = sb("bf_bc", [128, D], BF16)
        bout_bc = sb("bout_bc", [128, D], BF16)
        bvr_bc = sb("bvr_bc", [128, D], BF16)

        r_consts = b.res("consts")
        r_wqk_scr = b.res(); r_wv_scr = b.res(); r_wout_scr = b.res()
        r_w1_scr = [b.res() for _ in range(E)]
        r_w2_scr = [b.res() for _ in range(E)]
        r_qkpark = b.res()
        r_y1park = b.res(); r_xnpark = b.res(); r_gtmpark = b.res(); r_gfmpark = b.res()
        r_out = b.res()

        block = es.enter_context(nc.Block())

        for oc in range(16):
            src = wqkv_in[:, oc * 128:(oc + 1) * 128].rearrange("(dc p) c -> dc p c", p=128)
            b.op("pool", "dma_start", out=wqk_scr[oc], in_=src, writes=[r_wqk_scr])
        for dc in range(8):
            b.op("pool", "dma_start", out=wv_scr[dc],
                 in_=wqkv_in[dc * 128:(dc + 1) * 128, 2 * D:3 * D], writes=[r_wv_scr])
            b.op("pool", "dma_start", out=wout_scr[dc],
                 in_=wout_in[dc * 128:(dc + 1) * 128, :], writes=[r_wout_scr])

        b.op("pool", "memset", idn_f[:], 0.0, writes=[r_consts])
        b.op("pool", "affine_select", out=idn_f[:], in_=idn_f[:], compare_op=ALU.not_equal,
             fill=1.0, base=0, pattern=[[-1, 128]], channel_multiplier=1,
             reads=[r_consts], writes=[r_consts])
        b.op("pool", "memset", ones_c64[:], 1.0, writes=[r_consts])
        b.op("pool", "memset", ones_r128[:], 1.0, writes=[r_consts])
        b.op("pool", "memset", ones_r64f[:], 1.0, writes=[r_consts])

        with ExitStack() as es_pro:
            rowbuf = es_pro.enter_context(nc.sbuf_tensor("rowbuf", [1, 3 * D], F32))
            b1_sb8 = es_pro.enter_context(nc.sbuf_tensor("b1_sb8", [E, FF], F32))
            wg_sb8 = es_pro.enter_context(nc.sbuf_tensor("wg_sb8", [E, D], F32))
            row_r = b.res(); s8_r = b.res()

            def bcast_row(dst, ofs, n):
                for j0 in range(0, n, 512):
                    w = min(512, n - j0)
                    b.op("ten", "matmul", pb[7][0:128, 0:w], ones_r128[:, :],
                         rowbuf[0:1, ofs + j0: ofs + j0 + w], start=True, stop=True,
                         reads=[r_consts, row_r], writes=[pres[7]])
                    b.op("act", "activation", dst[:, j0:j0 + w], pb[7][0:128, 0:w], AF.Identity,
                         reads=[pres[7]], writes=[r_consts])

            for src_t, dst in [(ga_in, ga_bc), (ba_in, ba_bc), (gf_in, gf_bc),
                               (bf_in, bf_bc), (bout_in, bout_bc)]:
                b.op("sp", "dma_start", out=rowbuf[0:1, 0:D],
                     in_=src_t.rearrange("(a d) -> a d", a=1), writes=[row_r])
                bcast_row(dst, 0, D)
            b.op("sp", "dma_start", out=rowbuf[0:1, 0:3 * D],
                 in_=bqkv_in.rearrange("(a d) -> a d", a=1), writes=[row_r])
            for oc in range(16):
                b.op("ten", "transpose", pb[6][0:128, 0:1],
                     rowbuf[0:1, oc * 128:(oc + 1) * 128], idn_f[0:1, 0:1],
                     reads=[row_r, r_consts], writes=[pres[6]])
                b.op("act", "activation", bq_col[:, oc:oc + 1], pb[6][0:128, 0:1], AF.Identity,
                     reads=[pres[6]], writes=[r_consts])
            bcast_row(bvr_bc, 2 * D, D)
            b.op("sp", "dma_start", out=b1_sb8[:, :], in_=b1_in[:, :], writes=[s8_r])
            for fc in range(32):
                b.op("ten", "transpose", pb[6][0:128, 0:8],
                     b1_sb8[:, fc * 128:(fc + 1) * 128], idn_f[0:8, 0:8],
                     reads=[s8_r, r_consts], writes=[pres[6]])
                b.op("act", "activation", b1_col[:, fc * 8:(fc + 1) * 8], pb[6][0:128, 0:8],
                     AF.Identity, reads=[pres[6]], writes=[r_consts])
            b.op("sp", "dma_start", out=wg_sb8[:, :], in_=wg_in[:, :], writes=[s8_r])
            for dc in range(8):
                b.op("ten", "transpose", pb[6][0:128, 0:8],
                     wg_sb8[:, dc * 128:(dc + 1) * 128], idn_f[0:8, 0:8],
                     reads=[s8_r, r_consts], writes=[pres[6]])
                b.op("dve", "tensor_copy", wg_col[:, dc * 8:(dc + 1) * 8], pb[6][0:128, 0:8],
                     reads=[pres[6]], writes=[r_consts])
            b.op("sp", "dma_start", out=b2_sb[:, :], in_=b2_in[:, :], writes=[r_consts])

            b.hard_barrier()
            pro_end = b.nidx
            b.finalize_range(0, pro_end)
            b.emit_range(block, 0, pro_end)

        with ExitStack() as es_a:
            def sba(name, shape, dtype):
                return es_a.enter_context(nc.sbuf_tensor(name, shape, dtype))

            u_fm = sba("u_fm", [128, 8 * T], BF16); r_ufm = b.res("ufm")
            x_t = [sba(f"x_t{i}", [128, D], F32) for i in range(2)]
            r_xt = [b.res() for _ in range(2)]
            u0 = [sba(f"u0_{i}", [128, D], F32) for i in range(2)]
            r_u0 = [b.res() for _ in range(2)]
            bn6 = [sba(f"bn6_{i}", [128, 12], F32) for i in range(2)]
            r_bn6 = [b.res() for _ in range(2)]
            mv = [sba(f"mv_{i}", [128, 2], F32) for i in range(2)]
            r_mv = [b.res() for _ in range(2)]
            vpe = [sba(f"vpe_{i}", [128, 1], F32) for i in range(2)]
            r_vpe = [b.res() for _ in range(2)]
            ngm = [sba(f"ngm_{i}", [128, 1], F32) for i in range(2)]
            r_ngm = [b.res() for _ in range(2)]
            rsd = [sba(f"rsd_{i}", [128, 1], F32) for i in range(2)]
            r_rsd = [b.res() for _ in range(2)]
            wqk_buf = [sba(f"wqk{i}", [128, 8 * 128], BF16) for i in range(2)]
            r_wqk = [b.res() for _ in range(2)]
            wvo_sb = sba("wvo_sb", [128, 8 * D], BF16); r_wvo = b.res()
            qk_stage = [sba("qks0", [128, T], BF16)] * 2
            _rq = b.res()
            r_qks = [_rq, _rq]
            v_full = sba("v_full", [128, 16 * H * 65], BF16); r_vfull = b.res("vfull")
            qa = [sba(f"qa{i}", [65, 512], BF16) for i in range(2)]
            r_qa_d = [b.res() for _ in range(2)]; r_qa_b = [b.res() for _ in range(2)]
            ka = [sba(f"ka{i}", [65, T], BF16) for i in range(2)]
            r_ka_d = [b.res() for _ in range(2)]; r_ka_o = [b.res() for _ in range(2)]
            sqt = [sba(f"sqt{i}", [64, 512], BF16) for i in range(2)]
            r_sqt = [b.res() for _ in range(2)]
            kmx = sba("kmx", [1, 16], F32); r_kmx = b.res()
            sc2 = [sba(f"sc2_{i}", [1, 1], F32) for i in range(2)]
            r_sc2 = [b.res() for _ in range(2)]
            lnq = [sba(f"lnq{i}", [1, 512], F32) for i in range(2)]
            r_lnq = [b.res() for _ in range(2)]
            P_sb = sba("P_sb", [128, 16 * 512], BF16)
            r_P = [b.res() for _ in range(16)]
            rcp = [sba(f"rcp{i}", [1, 512], F32) for i in range(2)]
            r_rcp = [b.res() for _ in range(2)]
            bcn = [sba(f"bcn{i}", [64, 512], F32) for i in range(2)]
            r_bcn = [b.res() for _ in range(2)]
            att = sba("att", [128, 8 * 512], BF16); r_att = b.res()
            _at = sba("attmp0", [64, 512], BF16)
            att_tmp = [_at, _at]
            _ra = b.res()
            r_attmp = [_ra, _ra]
            zt = [sba(f"zt{i}", [128, D], F32) for i in range(2)]
            r_zt = [b.res() for _ in range(2)]
            y1_t = [sba(f"y1t{i}", [128, D], F32) for i in range(2)]
            r_y1t = [b.res() for _ in range(2)]
            xn0 = [sba(f"xn0_{i}", [128, D], F32) for i in range(2)]
            r_xn0 = [b.res() for _ in range(2)]
            xn_st = [sba(f"xnst{i}", [128, 8 * 128], BF16) for i in range(2)]
            r_xnst = [b.res() for _ in range(2)]
            lg_t = [sba(f"lgt{i}", [8, 128], F32) for i in range(2)]
            r_lgt = [b.res() for _ in range(2)]
            gt0_all = sba("gt0_all", [128, 16 * E], F32); r_gt0a = b.res()
            gsub = sba("gsub", [128, 16 * E], F32); r_gsub = b.res()
            gp8 = sba("gp8", [128, 16 * E], F32); r_gp8 = b.res()
            gprob = sba("gprob", [128, 16 * E], F32); r_gprob = b.res()
            gam = sba("gam", [128, 16 * E], F32); r_gam = b.res()
            gbm = sba("gbm", [128, 16 * E], F32); r_gbm = b.res()
            gis1 = sba("gis1", [128, 16 * E], F32); r_gis1 = b.res()
            gpmx, r_gpmx = gp8, r_gp8
            gmsk, r_gmsk = gt0_all, r_gt0a
            gtmp, r_gtmp = gbm, r_gbm
            gm1 = sba("gm1", [128, 16], F32); r_gm1 = b.res()
            gsum = sba("gsum", [128, 16], F32); r_gsum = b.res()
            grs = sba("grs", [128, 16], F32); r_grs = b.res()
            gt1 = sba("gt1", [128, 16], F32); r_gt1 = b.res()
            gt2 = sba("gt2", [128, 16], F32); r_gt2 = b.res()
            gden = sba("gden", [128, 16], F32); r_gden = b.res()
            gtm_st, r_gtmst = gprob, r_gprob
            gfm_st = sba("gfm_st", [8, 512], F32); r_gfmst = b.res()

            gv = lambda t: t[:, :].rearrange("p (g e) -> p g e", e=E)
            g1 = lambda t: t[:, :].rearrange("p (g o) -> p g o", o=1)

            def bTT(out_t, rout, a_t, ra, b_t, rb, op):
                a2, b2 = broadcast_tensor_aps(gv(a_t), g1(b_t))
                b.op("dve", "tensor_tensor", gv(out_t), a2, b2, op,
                     reads=[ra, rb], writes=[rout])

            def plain_ln(pi, src, rsrc, dst, rdst):
                b.op("dve", "bn_stats", bn6[pi][:, 0:6], src[:, 0:512],
                     reads=[rsrc], writes=[r_bn6[pi]])
                b.op("dve", "bn_stats", bn6[pi][:, 6:12], src[:, 512:1024],
                     reads=[rsrc], writes=[r_bn6[pi]])
                b.op("dve", "bn_aggr", mv[pi][:], bn6[pi][:, 0:12],
                     reads=[r_bn6[pi]], writes=[r_mv[pi]])
                b.op("dve", "tensor_scalar", ngm[pi][:], mv[pi][:, 0:1], -1.0, None, ALU.mult,
                     reads=[r_mv[pi]], writes=[r_ngm[pi]])
                b.op("dve", "tensor_scalar", vpe[pi][:], mv[pi][:, 1:2], EPS, None, ALU.add,
                     reads=[r_mv[pi]], writes=[r_vpe[pi]])
                b.op("act", "activation", rsd[pi][:], vpe[pi][:], AF.Ln,
                     reads=[r_vpe[pi]], writes=[r_rsd[pi]])
                b.op("act", "activation", rsd[pi][:], rsd[pi][:], AF.Exp, scale=-0.5,
                     reads=[r_rsd[pi]], writes=[r_rsd[pi]])
                b.op("act", "activation", dst[:], src[:], AF.Identity, bias=ngm[pi][:],
                     reads=[rsrc, r_ngm[pi]], writes=[rdst])
                b.op("act", "activation", dst[:], dst[:], AF.Identity, scale=rsd[pi][:],
                     reads=[rdst, r_rsd[pi]], writes=[rdst])

            for s in range(P_SEQ):
                for dc in range(8):
                    b.op("pool", "dma_start", out=wvo_sb[:, dc * D:(dc + 1) * D],
                         in_=wv_scr[dc], reads=[r_wv_scr], writes=[r_wvo])
                if s == 0:
                    for tc in range(16):
                        b.op("dve", "memset",
                             v_full[:, tc * H * 65:(tc + 1) * H * 65].rearrange(
                                 "p (h c) -> p h c", c=65)[:, :, 64:65],
                             1.0, writes=[r_vfull], late=True)
                for tt in range(16):
                    pi = tt % 2
                    xb = x_t[pi]; rx = r_xt[pi]
                    b.op("sp", "dma_start", out=xb[:],
                         in_=x_in[s, tt * 128:(tt + 1) * 128, :], writes=[rx])
                    plain_ln(pi, xb, rx, u0[pi], r_u0[pi])
                    b.op("dve", "tensor_tensor", u0[pi][:], u0[pi][:], ga_bc[:], ALU.mult,
                         reads=[r_u0[pi], r_consts], writes=[r_u0[pi]])
                    b.op("dve", "tensor_tensor", u0[pi][:], u0[pi][:], ba_bc[:], ALU.add,
                         reads=[r_u0[pi], r_consts], writes=[r_u0[pi]])
                    for half in range(2):
                        bank = 6 + half
                        for j in range(4):
                            dc = half * 4 + j
                            b.op("ten", "transpose", pb[bank][0:128, j * 128:(j + 1) * 128],
                                 u0[pi][:, dc * 128:(dc + 1) * 128], idn_f[:],
                                 reads=[r_u0[pi], r_consts], writes=[pres[bank]])
                        dst = u_fm[:, 0:8 * T].rearrange("p (dc t) -> p dc t", dc=8)[
                            :, half * 4:(half + 1) * 4, tt * 128:(tt + 1) * 128]
                        b.op("dve", "tensor_copy", dst,
                             pb[bank][0:128, 0:512].rearrange("p (dc t) -> p dc t", dc=4),
                             reads=[pres[bank]], writes=[r_ufm])
                for oc in range(16):
                    wb = wqk_buf[oc % 2]; rw = r_wqk[oc % 2]
                    b.op("pool", "dma_start",
                         out=wb[:].rearrange("p (dc c) -> p dc c", dc=8),
                         in_=wqk_scr[oc].rearrange("dc p c -> p dc c"),
                         reads=[r_wqk_scr], writes=[rw])
                    stg = qk_stage[oc % 2]; rstg = r_qks[oc % 2]
                    for ncn in range(4):
                        bank = ncn % 3
                        for dc in range(8):
                            b.op("ten", "matmul", pb[bank][0:128, 0:512],
                                 wb[:, dc * 128:(dc + 1) * 128],
                                 u_fm[:, dc * T + ncn * 512: dc * T + (ncn + 1) * 512],
                                 start=(dc == 0), stop=(dc == 7),
                                 reads=[rw, r_ufm], writes=[pres[bank]])
                        for hh in range(2):
                            b.op("act", "activation",
                                 stg[64 * hh:64 * (hh + 1), ncn * 512:(ncn + 1) * 512],
                                 pb[bank][64 * hh:64 * (hh + 1), 0:512], AF.Identity,
                                 bias=bq_col[64 * hh:64 * (hh + 1), oc:oc + 1],
                                 reads=[pres[bank], r_consts], writes=[rstg])
                    b.op("sp", "dma_start", out=qk_park[oc // 8, oc % 8], in_=stg[:],
                         reads=[rstg], writes=[r_qkpark])
                for tc in range(16):
                    for dh in range(2):
                        bank = 3 + dh
                        for dc in range(8):
                            b.op("ten", "matmul", pb[bank][0:128, 0:512],
                                 u_fm[:, dc * T + tc * 128: dc * T + (tc + 1) * 128],
                                 wvo_sb[:, dc * D + dh * 512: dc * D + (dh + 1) * 512],
                                 start=(dc == 0), stop=(dc == 7),
                                 reads=[r_ufm, r_wvo], writes=[pres[bank]])
                        for j in range(8):
                            h = dh * 8 + j
                            b.op("dve", "tensor_tensor",
                                 v_full[:, tc * H * 65 + h * 65: tc * H * 65 + h * 65 + 64],
                                 pb[bank][0:128, j * 64:(j + 1) * 64],
                                 bvr_bc[:, dh * 512 + j * 64: dh * 512 + (j + 1) * 64],
                                 ALU.add, reads=[pres[bank], r_consts], writes=[r_vfull])
                for dc in range(8):
                    b.op("pool", "dma_start", out=wvo_sb[:, dc * D:(dc + 1) * D],
                         in_=wout_scr[dc], reads=[r_wout_scr], writes=[r_wvo])

                if True:
                    _eper = E // P_SEQ
                    for e in range(_eper * s, _eper * (s + 1)):
                        for fc in range(32):
                            src = w1_in[e, :, fc * 128:(fc + 1) * 128].rearrange(
                                "(dc p) f -> dc p f", p=128)
                            b.op("pool", "dma_start", out=w1_scr[e, fc], in_=src,
                                 writes=[r_w1_scr[e]])
                        for fc in range(32):
                            src = w2_in[e, fc * 128:(fc + 1) * 128, :].rearrange(
                                "p (dh j) -> dh p j", j=512)
                            b.op("pool", "dma_start", out=w2_scr[e, :, fc], in_=src,
                                 writes=[r_w2_scr[e]])

                def prep_loads(qc, h):
                    pi = h % 2
                    qb = qa[pi]; kb = ka[pi]
                    b.op("sp", "dma_start", out=qb[0:64, :],
                         in_=qk_park[0, h // 2, 64 * (h % 2):64 * (h % 2) + 64,
                                     qc * 512:(qc + 1) * 512],
                         reads=[r_qkpark], writes=[r_qa_d[pi]])
                    b.op("sp", "dma_start", out=kb[0:64, :],
                         in_=qk_park[1, h // 2, 64 * (h % 2):64 * (h % 2) + 64, :],
                         reads=[r_qkpark], writes=[r_ka_d[pi]])
                    if s == 0 and qc == 0 and h < 2:
                        b.op("dve", "memset", kb[64:65, :], 1.0, writes=[r_ka_o[pi]],
                             late=True)
                    b.op("dve", "tensor_tensor", sqt[pi][:, :], qb[0:64, :], qb[0:64, :],
                         ALU.mult, reads=[r_qa_d[pi]], writes=[r_sqt[pi]])

                def prep_bound(qc, h):
                    pi = h % 2
                    qb = qa[pi]; kb = ka[pi]
                    nb = 5 + pi
                    if qc == 0:
                        b.op("dve", "memset", kmx[0:1, h:h + 1], -1e30, writes=[r_kmx])
                        for i in range(4):
                            b.op("dve", "tensor_tensor", sqt[pi][:, :],
                                 kb[0:64, i * 512:(i + 1) * 512],
                                 kb[0:64, i * 512:(i + 1) * 512], ALU.mult,
                                 reads=[r_ka_d[pi]], writes=[r_sqt[pi]])
                            b.op("ten", "matmul", pb[nb][0:1, 0:512], ones_c64[:],
                                 sqt[pi][:], start=True, stop=True,
                                 reads=[r_sqt[pi], r_consts], writes=[pres[nb]])
                            b.op("dve", "tensor_reduce", sc2[pi][:], pb[nb][0:1, 0:512],
                                 AX.X, ALU.max, reads=[pres[nb]], writes=[r_sc2[pi]])
                            b.op("dve", "tensor_tensor", kmx[0:1, h:h + 1],
                                 kmx[0:1, h:h + 1], sc2[pi][:], ALU.max,
                                 reads=[r_sc2[pi], r_kmx], writes=[r_kmx])
                        b.op("act", "activation", kmx[0:1, h:h + 1], kmx[0:1, h:h + 1],
                             AF.Ln, reads=[r_kmx], writes=[r_kmx])
                        b.op("dve", "tensor_scalar", kmx[0:1, h:h + 1],
                             kmx[0:1, h:h + 1], 0.5, None, ALU.mult,
                             reads=[r_kmx], writes=[r_kmx])
                        b.op("dve", "tensor_tensor", sqt[pi][:, :], qb[0:64, :], qb[0:64, :],
                             ALU.mult, reads=[r_qa_d[pi]], writes=[r_sqt[pi]])
                    b.op("ten", "matmul", pb[nb][0:1, 0:512], ones_c64[:], sqt[pi][:],
                         start=True, stop=True, reads=[r_sqt[pi], r_consts],
                         writes=[pres[nb]])
                    b.op("act", "activation", lnq[pi][:], pb[nb][0:1, 0:512], AF.Ln,
                         reads=[pres[nb]], writes=[r_lnq[pi]])
                    b.op("act", "activation", lnq[pi][:], lnq[pi][:], AF.Exp,
                         scale=0.5, bias=kmx[0:1, h:h + 1],
                         reads=[r_lnq[pi], r_kmx], writes=[r_lnq[pi]])
                    b.op("dve", "tensor_scalar", qb[64:65, :], lnq[pi][:], -1.0, None,
                         ALU.mult, reads=[r_lnq[pi]], writes=[r_qa_b[pi]])

                def core(qc, h, inject):
                    pi = h % 2
                    qb = qa[pi]; kb = ka[pi]
                    pvbank = 3 + pi

                    def plan_scores(kc):
                        bank = kc % 3
                        b.op("ten", "matmul", pb[bank][0:128, 0:512],
                             kb[:, kc * 128:(kc + 1) * 128], qb[:, :],
                             start=True, stop=True,
                             reads=[r_ka_d[pi], r_ka_o[pi], r_qa_d[pi], r_qa_b[pi]],
                             writes=[pres[bank]])

                    def plan_exp(kc):
                        bank = kc % 3
                        b.op("act", "activation", P_sb[:, kc * 512:(kc + 1) * 512],
                             pb[bank][0:128, 0:512], AF.Exp, scale=0.125,
                             reads=[pres[bank]], writes=[r_P[kc]])

                    def plan_pv(kc):
                        b.op("ten", "matmul", pb[pvbank][0:65, 0:512],
                             v_full[:, kc * H * 65 + h * 65: kc * H * 65 + (h + 1) * 65],
                             P_sb[:, kc * 512:(kc + 1) * 512],
                             start=(kc == 0), stop=(kc == 15),
                             reads=[r_vfull, r_P[kc]], writes=[pres[pvbank]])

                    plan_scores(0)
                    plan_scores(1)
                    plan_scores(2)
                    for kc in range(16):
                        plan_exp(kc)
                        if kc + 3 < 16:
                            plan_scores(kc + 3)
                        if kc == 6 and inject is not None:
                            inject()
                        plan_pv(kc)
                    b.op("dve", "reciprocal", rcp[pi][:], pb[pvbank][64:65, 0:512],
                         reads=[pres[pvbank]], writes=[r_rcp[pi]])

                def norm_tail(h):
                    pi = h % 2
                    pvbank = 3 + pi
                    b.op("ten", "matmul", pb[7][0:64, 0:512], ones_r64f[:], rcp[pi][:],
                         start=True, stop=True, reads=[r_rcp[pi], r_consts],
                         writes=[pres[7]])
                    b.op("act", "activation", bcn[pi][:], pb[7][0:64, 0:512], AF.Identity,
                         reads=[pres[7]], writes=[r_bcn[pi]])
                    if h % 2 == 0:
                        b.op("dve", "tensor_tensor",
                             att[0:64, (h // 2) * 512:(h // 2 + 1) * 512],
                             pb[pvbank][0:64, 0:512], bcn[pi][:], ALU.mult,
                             reads=[pres[pvbank], r_bcn[pi]], writes=[r_att])
                    else:
                        at = att_tmp[(h // 2) % 2]; rat = r_attmp[(h // 2) % 2]
                        b.op("dve", "tensor_tensor", at[:],
                             pb[pvbank][0:64, 0:512], bcn[pi][:], ALU.mult,
                             reads=[pres[pvbank], r_bcn[pi]], writes=[rat])
                        b.op("sp", "dma_start",
                             out=att[64:128, (h // 2) * 512:(h // 2 + 1) * 512],
                             in_=at[:], reads=[rat], writes=[r_att])

                for qc in range(4):
                    prep_loads(qc, 0)
                    prep_bound(qc, 0)
                    for h in range(H):
                        if h + 1 < H:
                            prep_loads(qc, h + 1)
                            inject = (lambda hh=h + 1, qq=qc: prep_bound(qq, hh))
                        else:
                            inject = None
                        core(qc, h, inject)
                        if h > 0:
                            norm_tail(h - 1)
                    norm_tail(H - 1)
                    for tt4 in range(4):
                        tt = qc * 4 + tt4
                        gtok = s * T + tt * 128
                        p2 = tt4 % 2
                        xb = x_t[p2]; rx = r_xt[p2]
                        b.op("sp", "dma_start", out=xb[:],
                             in_=x_in[s, tt * 128:(tt + 1) * 128, :], writes=[rx])
                        for dh in range(2):
                            bank = dh
                            for dc in range(8):
                                b.op("ten", "matmul", pb[bank][0:128, 0:512],
                                     att[:, dc * 512 + tt4 * 128: dc * 512 + (tt4 + 1) * 128],
                                     wvo_sb[:, dc * D + dh * 512: dc * D + (dh + 1) * 512],
                                     start=(dc == 0), stop=(dc == 7),
                                     reads=[r_att, r_wvo], writes=[pres[bank]])
                            b.op("act", "activation", zt[p2][:, dh * 512:(dh + 1) * 512],
                                 pb[bank][0:128, 0:512], AF.Identity,
                                 reads=[pres[bank]], writes=[r_zt[p2]])
                        b.op("dve", "tensor_tensor", zt[p2][:], zt[p2][:], bout_bc[:],
                             ALU.add, reads=[r_zt[p2], r_consts], writes=[r_zt[p2]])
                        b.op("dve", "scalar_tensor_tensor", zt[p2][:], xb[:], 2.0, zt[p2][:],
                             ALU.mult, ALU.add, reads=[rx, r_zt[p2]], writes=[r_zt[p2]])
                        plain_ln(p2, zt[p2], r_zt[p2], y1_t[p2], r_y1t[p2])
                        b.op("sp", "dma_start", out=y1_park[gtok:gtok + 128, :],
                             in_=y1_t[p2][:], reads=[r_y1t[p2]], writes=[r_y1park])
                        plain_ln(p2, y1_t[p2], r_y1t[p2], xn0[p2], r_xn0[p2])
                        b.op("dve", "tensor_tensor", xn0[p2][:], xn0[p2][:], gf_bc[:],
                             ALU.mult, reads=[r_xn0[p2], r_consts], writes=[r_xn0[p2]])
                        b.op("dve", "tensor_tensor", xn0[p2][:], xn0[p2][:], bf_bc[:],
                             ALU.add, reads=[r_xn0[p2], r_consts], writes=[r_xn0[p2]])
                        for half in range(2):
                            bank = 6 + half
                            for j in range(4):
                                dc = half * 4 + j
                                b.op("ten", "transpose",
                                     pb[bank][0:128, j * 128:(j + 1) * 128],
                                     xn0[p2][:, dc * 128:(dc + 1) * 128], idn_f[:],
                                     reads=[r_xn0[p2], r_consts], writes=[pres[bank]])
                            b.op("dve", "tensor_copy",
                                 xn_st[p2][:, half * 512:(half + 1) * 512].rearrange(
                                     "p (dc t) -> p dc t", dc=4),
                                 pb[bank][0:128, 0:512].rearrange("p (dc t) -> p dc t", dc=4),
                                 reads=[pres[bank]], writes=[r_xnst[p2]])
                        b.op("sp", "dma_start", out=xn_park[gtok // 128],
                             in_=xn_st[p2][:], reads=[r_xnst[p2]], writes=[r_xnpark])
                        for dc in range(8):
                            b.op("ten", "matmul", pb[5][0:8, 0:128],
                                 wg_col[:, dc * 8:(dc + 1) * 8],
                                 xn_st[p2][:, dc * 128:(dc + 1) * 128],
                                 start=(dc == 0), stop=(dc == 7),
                                 reads=[r_consts, r_xnst[p2]], writes=[pres[5]])
                        b.op("act", "activation", lg_t[p2][:], pb[5][0:8, 0:128], AF.Identity,
                             reads=[pres[5]], writes=[r_lgt[p2]])
                        b.op("ten", "transpose", pb[5][0:128, 0:8], lg_t[p2][:],
                             idn_f[0:8, 0:8], reads=[r_lgt[p2], r_consts], writes=[pres[5]])
                        b.op("act", "activation", gt0_all[:, tt * E:(tt + 1) * E],
                             pb[5][0:128, 0:8], AF.Identity,
                             reads=[pres[5]], writes=[r_gt0a])
                b.op("dve", "tensor_reduce", gm1[:], gv(gt0_all), AX.X, ALU.max,
                     reads=[r_gt0a], writes=[r_gm1])
                bTT(gsub, r_gsub, gt0_all, r_gt0a, gm1, r_gm1, ALU.subtract)
                b.op("act", "activation", gp8[:], gsub[:], AF.Exp,
                     reads=[r_gsub], writes=[r_gp8])
                b.op("dve", "tensor_reduce", gsum[:], gv(gp8), AX.X, ALU.add,
                     reads=[r_gp8], writes=[r_gsum])
                b.op("dve", "reciprocal", grs[:], gsum[:], reads=[r_gsum], writes=[r_grs])
                bTT(gprob, r_gprob, gp8, r_gp8, grs, r_grs, ALU.mult)
                b.op("dve", "tensor_reduce", gt1[:], gv(gprob), AX.X, ALU.max,
                     reads=[r_gprob], writes=[r_gt1])
                b.op("dve", "memset", gpmx[:], -1e30, writes=[r_gpmx])
                for j in range(1, E):
                    b.op("dve", "tensor_tensor", gv(gpmx)[:, :, j:j + 1],
                         gv(gpmx)[:, :, j - 1:j], gv(gprob)[:, :, j - 1:j], ALU.max,
                         reads=[r_gpmx, r_gprob], writes=[r_gpmx], late=True)
                bTT(gam, r_gam, gprob, r_gprob, gt1, r_gt1, ALU.is_ge)
                bTT(gbm, r_gbm, gpmx, r_gpmx, gt1, r_gt1, ALU.is_lt)
                b.op("dve", "tensor_tensor", gis1[:], gam[:], gbm[:], ALU.mult,
                     reads=[r_gam, r_gbm], writes=[r_gis1])
                b.op("dve", "scalar_tensor_tensor", gmsk[:], gis1[:], -1e30, gprob[:],
                     ALU.mult, ALU.add, reads=[r_gis1, r_gprob], writes=[r_gmsk])
                b.op("dve", "tensor_reduce", gt2[:], gv(gmsk), AX.X, ALU.max,
                     reads=[r_gmsk], writes=[r_gt2])
                b.op("dve", "memset", gpmx[:], -1e30, writes=[r_gpmx])
                for j in range(1, E):
                    b.op("dve", "tensor_tensor", gv(gpmx)[:, :, j:j + 1],
                         gv(gpmx)[:, :, j - 1:j], gv(gmsk)[:, :, j - 1:j], ALU.max,
                         reads=[r_gpmx, r_gmsk], writes=[r_gpmx], late=True)
                bTT(gam, r_gam, gmsk, r_gmsk, gt2, r_gt2, ALU.is_ge)
                bTT(gbm, r_gbm, gpmx, r_gpmx, gt2, r_gt2, ALU.is_lt)
                b.op("dve", "tensor_tensor", gsub[:], gam[:], gbm[:], ALU.mult,
                     reads=[r_gam, r_gbm], writes=[r_gsub])
                b.op("dve", "tensor_tensor", gden[:], gt1[:], gt2[:], ALU.add,
                     reads=[r_gt1, r_gt2], writes=[r_gden])
                b.op("dve", "tensor_scalar", gden[:], gden[:], 1e-9, None, ALU.add,
                     reads=[r_gden], writes=[r_gden])
                b.op("dve", "reciprocal", grs[:], gden[:], reads=[r_gden], writes=[r_grs])
                b.op("dve", "tensor_tensor", gt1[:], gt1[:], grs[:], ALU.mult,
                     reads=[r_gt1, r_grs], writes=[r_gt1])
                b.op("dve", "tensor_tensor", gt2[:], gt2[:], grs[:], ALU.mult,
                     reads=[r_gt2, r_grs], writes=[r_gt2])
                bTT(gtmp, r_gtmp, gsub, r_gsub, gt2, r_gt2, ALU.mult)
                bTT(gam, r_gam, gis1, r_gis1, gt1, r_gt1, ALU.mult)
                b.op("dve", "tensor_tensor", gtm_st[:], gam[:], gtmp[:], ALU.add,
                     reads=[r_gam, r_gtmp], writes=[r_gtmst])
                b.op("sp", "dma_start", out=gtm_park[s], in_=gtm_st[:],
                     reads=[r_gtmst], writes=[r_gtmpark])
                for tt in range(16):
                    b.op("ten", "transpose", pb[5][0:8, 0:128],
                         gtm_st[:, tt * E:(tt + 1) * E], idn_f[:],
                         reads=[r_gtmst, r_consts], writes=[pres[5]])
                    b.op("dve", "tensor_copy", gfm_st[:, (tt % 4) * 128:(tt % 4 + 1) * 128],
                         pb[5][0:8, 0:128], reads=[pres[5]], writes=[r_gfmst])
                    if tt % 4 == 3:
                        b.op("sp", "dma_start",
                             out=gfm_park[s, :, (tt // 4) * 512:(tt // 4 + 1) * 512],
                             in_=gfm_st[:], reads=[r_gfmst], writes=[r_gfmpark])

            b.hard_barrier()
            attn_end = b.nidx
            b.finalize_range(pro_end, attn_end)
            b.emit_range(block, pro_end, attn_end)

        with ExitStack() as es_m:
            def sbm(name, shape, dtype):
                return es_m.enter_context(nc.sbuf_tensor(name, shape, dtype))

            xn_blk = [sbm(f"xnb{i}", [128, 8 * TB], BF16) for i in range(2)]
            r_xnb = [b.res() for _ in range(2)]
            h_sb = sbm("h_sb", [128, 32 * TB], BF16)
            r_h = [b.res() for _ in range(32)]
            acc = sbm("acc", [128, 8 * D], F32)
            acc_r = [b.res(f"acc{i}") for i in range(8)]
            w1_buf = [sbm(f"w1b{i}", [128, 4 * 8 * 128], BF16) for i in range(2)]
            r_w1b = [b.res() for _ in range(2)]
            w2_buf = [sbm(f"w2b{i}", [128, 4 * 512], BF16) for i in range(2)]
            r_w2b = [b.res() for _ in range(2)]
            gtm_sb = [sbm(f"gtmsb{i}", [128, 8 * E], F32) for i in range(2)]
            r_gtm = [b.res() for _ in range(2)]
            gfm_sb = [sbm(f"gfmsb{i}", [E, TB], F32) for i in range(2)]
            r_gfm = [b.res() for _ in range(2)]
            ytmp = [sbm(f"ytmp{i}", [128, 512], F32) for i in range(2)]
            r_yt = [b.res() for _ in range(2)]
            y1b = [sbm(f"y1b{i}", [128, D], F32) for i in range(2)]
            r_y1b = [b.res() for _ in range(2)]
            outb = [sbm(f"outb{i}", [128, D], F32) for i in range(2)]
            r_outb = [b.res() for _ in range(2)]
            bn6m = [sbm(f"bn6m{i}", [128, 12], F32) for i in range(2)]
            r_bn6m = [b.res() for _ in range(2)]
            mvm = [sbm(f"mvm{i}", [128, 2], F32) for i in range(2)]
            r_mvm = [b.res() for _ in range(2)]
            vpem = [sbm(f"vpem{i}", [128, 1], F32) for i in range(2)]
            r_vpem = [b.res() for _ in range(2)]
            ngmm = [sbm(f"ngmm{i}", [128, 1], F32) for i in range(2)]
            r_ngmm = [b.res() for _ in range(2)]
            rsdm = [sbm(f"rsdm{i}", [128, 1], F32) for i in range(2)]
            r_rsdm = [b.res() for _ in range(2)]

            ntc = TB // 128
            for tb in range(NTB):
                s = tb // NHALF
                half = tb % NHALF
                tpi = tb % 2
                xnb = xn_blk[tpi]; rxnb = r_xnb[tpi]
                xview = xnb[:, :].rearrange("p (ti dc j) -> p ti dc j", ti=8, dc=8)
                for ti in range(8):
                    b.op("sp", "dma_start", out=xnb[:, ti * 1024:(ti + 1) * 1024],
                         in_=xn_park[tb * 8 + ti], reads=[r_xnpark], writes=[rxnb])
                b.op("sp", "dma_start", out=gtm_sb[tpi][:],
                     in_=gtm_park[s][:, half * 8 * E:(half + 1) * 8 * E],
                     reads=[r_gtmpark], writes=[r_gtm[tpi]])
                b.op("sp", "dma_start", out=gfm_sb[tpi][:],
                     in_=gfm_park[s][:, half * TB:(half + 1) * TB],
                     reads=[r_gfmpark], writes=[r_gfm[tpi]])
                def plan_acc_init():
                    for dh in range(2):
                        for tc in range(ntc):
                            b.op("ten", "matmul", pb[tc][0:128, 0:512],
                                 gfm_sb[tpi][:, tc * 128:(tc + 1) * 128],
                                 b2_sb[:, dh * 512:(dh + 1) * 512], start=True, stop=True,
                                 reads=[r_gfm[tpi], r_consts], writes=[pres[tc]])
                            b.op("act", "activation",
                                 acc[:, tc * D + dh * 512: tc * D + (dh + 1) * 512],
                                 pb[tc][0:128, 0:512], AF.Identity,
                                 reads=[pres[tc]], writes=[acc_r[tc]])
                for e in range(E):
                    for fcg in range(8):
                        wb = w1_buf[fcg % 2]; rw = r_w1b[fcg % 2]
                        b.op("pool", "dma_start",
                             out=wb[:].rearrange("p (fi dc f) -> p fi dc f", fi=4, dc=8),
                             in_=w1_scr[e, fcg * 4:(fcg + 1) * 4].rearrange(
                                 "fi dc p f -> p fi dc f"),
                             reads=[r_w1_scr[e]], writes=[rw])
                        for fi in range(4):
                            fc = fcg * 4 + fi
                            for th in range(2):
                                bank = (fc * 2 + th) % 3
                                for dc in range(8):
                                    b.op("ten", "matmul", pb[bank][0:128, 0:512],
                                         wb[:, (fi * 8 + dc) * 128:(fi * 8 + dc + 1) * 128],
                                         xview[:, th * 4:(th + 1) * 4, dc, :],
                                         start=(dc == 0), stop=(dc == 7),
                                         reads=[rw, rxnb], writes=[pres[bank]])
                                b.op("act", "activation",
                                     h_sb[:, fc * TB + th * 512: fc * TB + (th + 1) * 512],
                                     pb[bank][0:128, 0:512], AF.Gelu_apprx_tanh,
                                     bias=b1_col[:, fc * 8 + e: fc * 8 + e + 1],
                                     reads=[pres[bank], r_consts], writes=[r_h[fc]])
                    if "h_dump" in dbg and tb == 0 and e == 0:
                        b.op("sp", "dma_start", out=h_dump[:, :], in_=h_sb[:, :],
                             reads=[r_h[0], r_h[31]], writes=[r_out])
                    if e == 0:
                        plan_acc_init()
                    for dh in range(2):
                        for fcg in range(8):
                            wb = w2_buf[fcg % 2]; rw = r_w2b[fcg % 2]
                            b.op("pool", "dma_start",
                                 out=wb[:].rearrange("p (fi j) -> p fi j", fi=4),
                                 in_=w2_scr[e, dh, fcg * 4:(fcg + 1) * 4].rearrange(
                                     "fi p j -> p fi j"),
                                 reads=[r_w2_scr[e]], writes=[rw])
                            for fi in range(4):
                                fc = fcg * 4 + fi
                                for tc in range(ntc):
                                    b.op("ten", "matmul", pb[tc][0:128, 0:512],
                                         h_sb[:, fc * TB + tc * 128: fc * TB + (tc + 1) * 128],
                                         wb[:, fi * 512:(fi + 1) * 512],
                                         start=(fc == 0), stop=(fc == 31),
                                         reads=[rw, r_h[fc]], writes=[pres[tc]])
                        for tc in range(ntc):
                            yb = ytmp[tc % 2]; ry = r_yt[tc % 2]
                            b.op("act", "activation", yb[:], pb[tc][0:128, 0:512], AF.Identity,
                                 scale=gtm_sb[tpi][:, tc * E + e: tc * E + e + 1],
                                 reads=[pres[tc], r_gtm[tpi]], writes=[ry])
                            b.op("dve", "tensor_tensor",
                                 acc[:, tc * D + dh * 512: tc * D + (dh + 1) * 512],
                                 acc[:, tc * D + dh * 512: tc * D + (dh + 1) * 512],
                                 yb[:], ALU.add, reads=[ry, acc_r[tc]], writes=[acc_r[tc]])
                if "acc_dump" in dbg:
                    b.op("sp", "dma_start", out=acc_dump[tb], in_=acc[:, :],
                         reads=[acc_r[0], acc_r[1], acc_r[2], acc_r[3], acc_r[4],
                                acc_r[5], acc_r[6], acc_r[7]], writes=[r_out])
                for tc in range(ntc):
                    gtok = tb * TB + tc * 128
                    p2 = tc % 2
                    yb = y1b[p2]; ry = r_y1b[p2]
                    ob = outb[p2]; ro = r_outb[p2]
                    b.op("sp", "dma_start", out=yb[:], in_=y1_park[gtok:gtok + 128, :],
                         reads=[r_y1park], writes=[ry])
                    b.op("dve", "scalar_tensor_tensor", ob[:], yb[:], 2.0,
                         acc[:, tc * D:(tc + 1) * D], ALU.mult, ALU.add,
                         reads=[ry, acc_r[tc]], writes=[ro])
                    b.op("dve", "bn_stats", bn6m[p2][:, 0:6], ob[:, 0:512],
                         reads=[ro], writes=[r_bn6m[p2]])
                    b.op("dve", "bn_stats", bn6m[p2][:, 6:12], ob[:, 512:1024],
                         reads=[ro], writes=[r_bn6m[p2]])
                    b.op("dve", "bn_aggr", mvm[p2][:], bn6m[p2][:, 0:12],
                         reads=[r_bn6m[p2]], writes=[r_mvm[p2]])
                    b.op("dve", "tensor_scalar", ngmm[p2][:], mvm[p2][:, 0:1], -1.0, None,
                         ALU.mult, reads=[r_mvm[p2]], writes=[r_ngmm[p2]])
                    b.op("dve", "tensor_scalar", vpem[p2][:], mvm[p2][:, 1:2], EPS, None,
                         ALU.add, reads=[r_mvm[p2]], writes=[r_vpem[p2]])
                    b.op("act", "activation", rsdm[p2][:], vpem[p2][:], AF.Ln,
                         reads=[r_vpem[p2]], writes=[r_rsdm[p2]])
                    b.op("act", "activation", rsdm[p2][:], rsdm[p2][:], AF.Exp, scale=-0.5,
                         reads=[r_rsdm[p2]], writes=[r_rsdm[p2]])
                    b.op("act", "activation", ob[:], ob[:], AF.Identity, bias=ngmm[p2][:],
                         reads=[ro, r_ngmm[p2]], writes=[ro])
                    b.op("act", "activation", ob[:], ob[:], AF.Identity, scale=rsdm[p2][:],
                         reads=[ro, r_rsdm[p2]], writes=[ro])
                    b.op("sp", "dma_start",
                         out=out_d[s, half * TB + tc * 128: half * TB + (tc + 1) * 128, :],
                         in_=ob[:], reads=[ro], writes=[r_out])
            b.op("sp", "wait_only", reads=[r_out])

            moe_end = b.nidx
            b.finalize_range(attn_end, moe_end)
            b.emit_range(block, attn_end, moe_end)
    return nc


_NC_CACHE = {}


def _get_nc():
    if "nc" not in _NC_CACHE:
        _NC_CACHE["nc"] = build_program()
    return _NC_CACHE["nc"]


def kernel(**inputs):
    nc = _get_nc()
    x = np.ascontiguousarray(np.asarray(inputs["L0_attn_x"], dtype=np.float32))
    names = ["L0_attn_ln_gamma", "L0_attn_ln_beta", "L0_attn_w_qkv", "L0_attn_b_qkv",
             "L0_attn_w_out", "L0_attn_b_out", "L0_ffn_gamma", "L0_ffn_beta",
             "L0_ffn_Wg", "L0_ffn_W1", "L0_ffn_b1", "L0_ffn_W2", "L0_ffn_b2"]
    shared = {n: np.ascontiguousarray(np.asarray(inputs[n], dtype=np.float32)) for n in names}
    in_maps = []
    for c in range(NCORES):
        m = dict(shared)
        m["L0_attn_x"] = np.ascontiguousarray(x[c * 4:(c + 1) * 4])
        in_maps.append(m)
    res = run_bass_kernel_spmd(nc, in_maps, core_ids=list(range(NCORES)))
    out = np.concatenate([r["out"] for r in res.results], axis=0)
    return out


# version 04
# speedup vs baseline: 1.1870x; 1.0274x over previous
import numpy as np
from contextlib import ExitStack

import concourse.bass as bass
import concourse.mybir as mybir
from concourse.bass import broadcast_tensor_aps
from concourse.bass_utils import run_bass_kernel_spmd

F32 = mybir.dt.float32
BF16 = mybir.dt.bfloat16
AF = mybir.ActivationFunctionType
ALU = mybir.AluOpType
AX = mybir.AxisListType

T = 2048
D = 1024
H = 16
E = 8
FF = 4096
TB = 1024
EPS = 1e-5
NCORES = 8

ENG2BASS = {"ten": "tensor", "act": "scalar", "dve": "vector", "pool": "gpsimd", "sp": "sync"}
LATE_METHODS = {"bn_stats", "bn_aggr", "tensor_reduce", "reduce_max", "reduce_sum",
                "reciprocal", "dma_start", "memset"}


class Rec:
    __slots__ = ("eng", "method", "args", "kw", "waits", "need_inc", "inc_amt", "val", "late", "idx")

    def __init__(self, eng, method, args, kw, late, idx):
        self.eng = eng
        self.method = method
        self.args = args
        self.kw = kw
        self.waits = []
        self.need_inc = False
        self.inc_amt = 16 if method == "dma_start" else 1
        self.val = None
        self.late = late
        self.idx = idx


class Res:
    __slots__ = ("name", "w", "rs")

    def __init__(self, name="r"):
        self.name = name
        self.w = None
        self.rs = []


class B:
    def __init__(self, nc):
        self.nc = nc
        self.q = {e: [] for e in ENG2BASS}
        self.pending = {e: [] for e in ENG2BASS}
        self.nidx = 0
        self.ctr = {e: 0 for e in ENG2BASS}
        self.hwm = {e: {} for e in ENG2BASS}
        self.all_res = []
        self.sems = None

    def res(self, name="r"):
        r = Res(name)
        self.all_res.append(r)
        return r

    def op(self, eng, method, *args, reads=(), writes=(), late=None, **kw):
        if late is None:
            late = method in LATE_METHODS or eng in ("dve", "act", "pool")
        r = Rec(eng, method, args, kw, late, self.nidx)
        self.nidx += 1
        if method == "dma_start":
            r.need_inc = True
        deps = []
        for res in reads:
            if res.w is not None:
                deps.append(res.w)
        for res in writes:
            if res.w is not None:
                deps.append(res.w)
            deps.extend(res.rs)
        deps.extend(self.pending[eng])
        self.pending[eng] = []
        best = {}
        for d in deps:
            if d.eng == eng and not d.late:
                continue
            cur = best.get(d.eng)
            if cur is None or d.idx > cur.idx:
                best[d.eng] = d
        for d in best.values():
            d.need_inc = True
            r.waits.append(d)
        for res in writes:
            res.w = r
            res.rs = []
        for res in reads:
            res.rs.append(r)
        self.q[eng].append(r)
        return r

    def hard_barrier(self):
        lasts = [self.q[e][-1] for e in ENG2BASS if self.q[e]]
        for d in lasts:
            d.need_inc = True
        for e in ENG2BASS:
            self.pending[e] = list(lasts)
        for r in self.all_res:
            r.w = None
            r.rs = []

    def finalize_range(self, lo, hi):
        for e in ENG2BASS:
            for r in self.q[e]:
                if lo <= r.idx < hi and r.need_inc:
                    self.ctr[e] += r.inc_amt
                    r.val = self.ctr[e]

    def emit_range(self, block, lo, hi):
        sems = self.sems
        for e, bname in ENG2BASS.items():
            recs = [r for r in self.q[e] if lo <= r.idx < hi]
            if not recs:
                continue

            def emit(h, recs=recs, e=e):
                hw = self.hwm[e]
                for r in recs:
                    for d in r.waits:
                        assert d.val is not None, (r.method, d.method, d.eng)
                        if hw.get(d.eng, 0) >= d.val:
                            continue
                        hw[d.eng] = d.val
                        h.wait_ge(sems[d.eng], d.val)
                    if r.method == "wait_only":
                        continue
                    ins = getattr(h, r.method)(*r.args, **r.kw)
                    if r.need_inc:
                        ins.then_inc(sems[r.eng], r.inc_amt)

            getattr(block, bname)(emit)


def build_program(P_SEQ=4, dbg=()):
    NTOK = P_SEQ * T
    NTB = NTOK // TB
    NHALF = T // TB
    nc = bass.Bass("TRN2", target_bir_lowering=False)

    def dram(name, shape, dtype, kind):
        if name in dbg:
            kind = "ExternalOutput"
        return nc.dram_tensor(name, shape, dtype, kind=kind).ap()

    x_in = dram("L0_attn_x", [P_SEQ, T, D], F32, "ExternalInput")
    ga_in = dram("L0_attn_ln_gamma", [D], F32, "ExternalInput")
    ba_in = dram("L0_attn_ln_beta", [D], F32, "ExternalInput")
    wqkv_in = dram("L0_attn_w_qkv", [D, 3 * D], F32, "ExternalInput")
    bqkv_in = dram("L0_attn_b_qkv", [3 * D], F32, "ExternalInput")
    wout_in = dram("L0_attn_w_out", [D, D], F32, "ExternalInput")
    bout_in = dram("L0_attn_b_out", [D], F32, "ExternalInput")
    gf_in = dram("L0_ffn_gamma", [D], F32, "ExternalInput")
    bf_in = dram("L0_ffn_beta", [D], F32, "ExternalInput")
    wg_in = dram("L0_ffn_Wg", [E, D], F32, "ExternalInput")
    w1_in = dram("L0_ffn_W1", [E, D, FF], F32, "ExternalInput")
    b1_in = dram("L0_ffn_b1", [E, FF], F32, "ExternalInput")
    w2_in = dram("L0_ffn_W2", [E, FF, D], F32, "ExternalInput")
    b2_in = dram("L0_ffn_b2", [E, D], F32, "ExternalInput")
    out_d = dram("out", [P_SEQ, T, D], F32, "ExternalOutput")

    wqk_scr = dram("wqk_scr", [16, 8, 128, 128], BF16, "Internal")
    wv_scr = dram("wv_scr", [8, 128, D], BF16, "Internal")
    wout_scr = dram("wout_scr", [8, 128, D], BF16, "Internal")
    w1_scr = dram("w1_scr", [E, 32, 8, 128, 128], BF16, "Internal")
    w2_scr = dram("w2_scr", [E, 2, 32, 128, 512], BF16, "Internal")
    qk_park = dram("qk_park", [2, 8, 128, T], BF16, "Internal")
    y1_park = dram("y1_park", [NTOK, D], F32, "Internal")
    xn_park = dram("xn_park", [NTOK // 128, 128, 8 * 128], BF16, "Internal")
    gtm_park = dram("gtm_park", [P_SEQ, 128, 16 * E], F32, "Internal")
    acc_dump = dram("acc_dump", [NTOK // TB, 128, 8 * D], F32, "Internal")
    h_dump = dram("h_dump", [128, 32 * TB], BF16, "Internal")
    gfm_park = dram("gfm_park", [P_SEQ, E, T], F32, "Internal")

    b = B(nc)

    with ExitStack() as es:
        b.sems = {e: es.enter_context(nc.semaphore(f"sem_{e}")) for e in ENG2BASS}

        def sb(name, shape, dtype):
            return es.enter_context(nc.sbuf_tensor(name, shape, dtype))

        pb = [es.enter_context(nc.psum_tensor(f"pb{i}", [128, 512], F32)) for i in range(8)]
        pres = [b.res(f"pb{i}") for i in range(8)]

        idn_f = sb("idn_f", [128, 128], F32)
        ones_c64 = sb("ones_c64", [64, 1], BF16)
        ones_r128 = sb("ones_r128", [1, 128], F32)
        ones_r64f = sb("ones_r64f", [1, 64], F32)
        b1_col = sb("b1_col", [128, 32 * E], F32)
        b2_sb = sb("b2_sb", [E, D], F32)
        wg_col = sb("wg_col", [128, 8 * E], BF16)
        bq_col = sb("bq_col", [128, 16], F32)
        ga_bc = sb("ga_bc", [128, D], BF16)
        ba_bc = sb("ba_bc", [128, D], BF16)
        gf_bc = sb("gf_bc", [128, D], BF16)
        bf_bc = sb("bf_bc", [128, D], BF16)
        bout_bc = sb("bout_bc", [128, D], BF16)
        bvr_bc = sb("bvr_bc", [128, D], BF16)

        r_consts = b.res("consts")
        r_wqk_scr = b.res(); r_wv_scr = b.res(); r_wout_scr = b.res()
        r_w1_scr = [b.res() for _ in range(E)]
        r_w2_scr = [b.res() for _ in range(E)]
        r_qkpark = b.res()
        r_y1park = b.res(); r_xnpark = b.res(); r_gtmpark = b.res(); r_gfmpark = b.res()
        r_out = b.res()

        block = es.enter_context(nc.Block())

        for oc in range(16):
            src = wqkv_in[:, oc * 128:(oc + 1) * 128].rearrange("(dc p) c -> dc p c", p=128)
            b.op("pool", "dma_start", out=wqk_scr[oc], in_=src, writes=[r_wqk_scr])
        for dc in range(8):
            b.op("pool", "dma_start", out=wv_scr[dc],
                 in_=wqkv_in[dc * 128:(dc + 1) * 128, 2 * D:3 * D], writes=[r_wv_scr])
            b.op("pool", "dma_start", out=wout_scr[dc],
                 in_=wout_in[dc * 128:(dc + 1) * 128, :], writes=[r_wout_scr])

        b.op("pool", "memset", idn_f[:], 0.0, writes=[r_consts])
        b.op("pool", "affine_select", out=idn_f[:], in_=idn_f[:], compare_op=ALU.not_equal,
             fill=1.0, base=0, pattern=[[-1, 128]], channel_multiplier=1,
             reads=[r_consts], writes=[r_consts])
        b.op("pool", "memset", ones_c64[:], 1.0, writes=[r_consts])
        b.op("pool", "memset", ones_r128[:], 1.0, writes=[r_consts])
        b.op("pool", "memset", ones_r64f[:], 1.0, writes=[r_consts])

        with ExitStack() as es_pro:
            rowbuf = es_pro.enter_context(nc.sbuf_tensor("rowbuf", [1, 3 * D], F32))
            b1_sb8 = es_pro.enter_context(nc.sbuf_tensor("b1_sb8", [E, FF], F32))
            wg_sb8 = es_pro.enter_context(nc.sbuf_tensor("wg_sb8", [E, D], F32))
            row_r = b.res(); s8_r = b.res()

            def bcast_row(dst, ofs, n):
                for j0 in range(0, n, 512):
                    w = min(512, n - j0)
                    b.op("ten", "matmul", pb[7][0:128, 0:w], ones_r128[:, :],
                         rowbuf[0:1, ofs + j0: ofs + j0 + w], start=True, stop=True,
                         reads=[r_consts, row_r], writes=[pres[7]])
                    b.op("act", "activation", dst[:, j0:j0 + w], pb[7][0:128, 0:w], AF.Identity,
                         reads=[pres[7]], writes=[r_consts])

            for src_t, dst in [(ga_in, ga_bc), (ba_in, ba_bc), (gf_in, gf_bc),
                               (bf_in, bf_bc), (bout_in, bout_bc)]:
                b.op("sp", "dma_start", out=rowbuf[0:1, 0:D],
                     in_=src_t.rearrange("(a d) -> a d", a=1), writes=[row_r])
                bcast_row(dst, 0, D)
            b.op("sp", "dma_start", out=rowbuf[0:1, 0:3 * D],
                 in_=bqkv_in.rearrange("(a d) -> a d", a=1), writes=[row_r])
            for oc in range(16):
                b.op("ten", "transpose", pb[6][0:128, 0:1],
                     rowbuf[0:1, oc * 128:(oc + 1) * 128], idn_f[0:1, 0:1],
                     reads=[row_r, r_consts], writes=[pres[6]])
                b.op("act", "activation", bq_col[:, oc:oc + 1], pb[6][0:128, 0:1], AF.Identity,
                     reads=[pres[6]], writes=[r_consts])
            bcast_row(bvr_bc, 2 * D, D)
            b.op("sp", "dma_start", out=b1_sb8[:, :], in_=b1_in[:, :], writes=[s8_r])
            for fc in range(32):
                b.op("ten", "transpose", pb[6][0:128, 0:8],
                     b1_sb8[:, fc * 128:(fc + 1) * 128], idn_f[0:8, 0:8],
                     reads=[s8_r, r_consts], writes=[pres[6]])
                b.op("act", "activation", b1_col[:, fc * 8:(fc + 1) * 8], pb[6][0:128, 0:8],
                     AF.Identity, reads=[pres[6]], writes=[r_consts])
            b.op("sp", "dma_start", out=wg_sb8[:, :], in_=wg_in[:, :], writes=[s8_r])
            for dc in range(8):
                b.op("ten", "transpose", pb[6][0:128, 0:8],
                     wg_sb8[:, dc * 128:(dc + 1) * 128], idn_f[0:8, 0:8],
                     reads=[s8_r, r_consts], writes=[pres[6]])
                b.op("dve", "tensor_copy", wg_col[:, dc * 8:(dc + 1) * 8], pb[6][0:128, 0:8],
                     reads=[pres[6]], writes=[r_consts])
            b.op("sp", "dma_start", out=b2_sb[:, :], in_=b2_in[:, :], writes=[r_consts])

            b.hard_barrier()
            pro_end = b.nidx
            b.finalize_range(0, pro_end)
            b.emit_range(block, 0, pro_end)

        with ExitStack() as es_a:
            def sba(name, shape, dtype):
                return es_a.enter_context(nc.sbuf_tensor(name, shape, dtype))

            u_fm = sba("u_fm", [128, 8 * T], BF16); r_ufm = b.res("ufm")
            x_t = [sba(f"x_t{i}", [128, D], F32) for i in range(2)]
            r_xt = [b.res() for _ in range(2)]
            u0 = [sba(f"u0_{i}", [128, D], F32) for i in range(2)]
            r_u0 = [b.res() for _ in range(2)]
            bn6 = [sba(f"bn6_{i}", [128, 12], F32) for i in range(2)]
            r_bn6 = [b.res() for _ in range(2)]
            mv = [sba(f"mv_{i}", [128, 2], F32) for i in range(2)]
            r_mv = [b.res() for _ in range(2)]
            vpe = [sba(f"vpe_{i}", [128, 1], F32) for i in range(2)]
            r_vpe = [b.res() for _ in range(2)]
            ngm = [sba(f"ngm_{i}", [128, 1], F32) for i in range(2)]
            r_ngm = [b.res() for _ in range(2)]
            rsd = [sba(f"rsd_{i}", [128, 1], F32) for i in range(2)]
            r_rsd = [b.res() for _ in range(2)]
            wqk_buf = [sba(f"wqk{i}", [128, 8 * 128], BF16) for i in range(2)]
            r_wqk = [b.res() for _ in range(2)]
            wvo_sb = sba("wvo_sb", [128, 8 * D], BF16); r_wvo = b.res()
            qk_stage = [sba("qks0", [128, T], BF16)] * 2
            _rq = b.res()
            r_qks = [_rq, _rq]
            v_full = sba("v_full", [128, 16 * H * 65], BF16); r_vfull = b.res("vfull")
            qa = [sba(f"qa{i}", [65, 512], BF16) for i in range(2)]
            r_qa_d = [b.res() for _ in range(2)]; r_qa_b = [b.res() for _ in range(2)]
            ka = [sba(f"ka{i}", [65, T], BF16) for i in range(2)]
            r_ka_d = [b.res() for _ in range(2)]; r_ka_o = [b.res() for _ in range(2)]
            sqt = [sba(f"sqt{i}", [64, 512], BF16) for i in range(2)]
            r_sqt = [b.res() for _ in range(2)]
            kmx = sba("kmx", [1, 16], F32); r_kmx = b.res()
            sc2 = [sba(f"sc2_{i}", [1, 1], F32) for i in range(2)]
            r_sc2 = [b.res() for _ in range(2)]
            lnq = [sba(f"lnq{i}", [1, 512], F32) for i in range(2)]
            r_lnq = [b.res() for _ in range(2)]
            P_sb = sba("P_sb", [128, 16 * 512], BF16)
            r_P = [b.res() for _ in range(16)]
            rcp = [sba(f"rcp{i}", [1, 512], F32) for i in range(2)]
            r_rcp = [b.res() for _ in range(2)]
            bcn = [sba(f"bcn{i}", [64, 512], F32) for i in range(2)]
            r_bcn = [b.res() for _ in range(2)]
            att = sba("att", [128, 8 * 512], BF16); r_att = b.res()
            _at = sba("attmp0", [64, 512], BF16)
            att_tmp = [_at, _at]
            _ra = b.res()
            r_attmp = [_ra, _ra]
            zt = [sba(f"zt{i}", [128, D], F32) for i in range(2)]
            r_zt = [b.res() for _ in range(2)]
            y1_t = [sba(f"y1t{i}", [128, D], F32) for i in range(2)]
            r_y1t = [b.res() for _ in range(2)]
            xn0 = [sba(f"xn0_{i}", [128, D], F32) for i in range(2)]
            r_xn0 = [b.res() for _ in range(2)]
            xn_st = [sba(f"xnst{i}", [128, 8 * 128], BF16) for i in range(2)]
            r_xnst = [b.res() for _ in range(2)]
            lg_t = [sba(f"lgt{i}", [8, 128], F32) for i in range(2)]
            r_lgt = [b.res() for _ in range(2)]
            gt0_all = sba("gt0_all", [128, 16 * E], F32); r_gt0a = b.res()
            gsub = sba("gsub", [128, 16 * E], F32); r_gsub = b.res()
            gp8 = sba("gp8", [128, 16 * E], F32); r_gp8 = b.res()
            gprob = sba("gprob", [128, 16 * E], F32); r_gprob = b.res()
            gam = sba("gam", [128, 16 * E], F32); r_gam = b.res()
            gbm = sba("gbm", [128, 16 * E], F32); r_gbm = b.res()
            gis1 = sba("gis1", [128, 16 * E], F32); r_gis1 = b.res()
            gpmx, r_gpmx = gp8, r_gp8
            gmsk, r_gmsk = gt0_all, r_gt0a
            gtmp, r_gtmp = gbm, r_gbm
            gm1 = sba("gm1", [128, 16], F32); r_gm1 = b.res()
            gsum = sba("gsum", [128, 16], F32); r_gsum = b.res()
            grs = sba("grs", [128, 16], F32); r_grs = b.res()
            gt1 = sba("gt1", [128, 16], F32); r_gt1 = b.res()
            gt2 = sba("gt2", [128, 16], F32); r_gt2 = b.res()
            gden = sba("gden", [128, 16], F32); r_gden = b.res()
            gtm_st, r_gtmst = gprob, r_gprob
            gfm_st = sba("gfm_st", [8, 512], F32); r_gfmst = b.res()

            gv = lambda t: t[:, :].rearrange("p (g e) -> p g e", e=E)
            g1 = lambda t: t[:, :].rearrange("p (g o) -> p g o", o=1)

            def bTT(out_t, rout, a_t, ra, b_t, rb, op):
                a2, b2 = broadcast_tensor_aps(gv(a_t), g1(b_t))
                b.op("dve", "tensor_tensor", gv(out_t), a2, b2, op,
                     reads=[ra, rb], writes=[rout])

            def plain_ln(pi, src, rsrc, dst, rdst):
                b.op("dve", "bn_stats", bn6[pi][:, 0:6], src[:, 0:512],
                     reads=[rsrc], writes=[r_bn6[pi]])
                b.op("dve", "bn_stats", bn6[pi][:, 6:12], src[:, 512:1024],
                     reads=[rsrc], writes=[r_bn6[pi]])
                b.op("dve", "bn_aggr", mv[pi][:], bn6[pi][:, 0:12],
                     reads=[r_bn6[pi]], writes=[r_mv[pi]])
                b.op("dve", "tensor_scalar", ngm[pi][:], mv[pi][:, 0:1], -1.0, None, ALU.mult,
                     reads=[r_mv[pi]], writes=[r_ngm[pi]])
                b.op("dve", "tensor_scalar", vpe[pi][:], mv[pi][:, 1:2], EPS, None, ALU.add,
                     reads=[r_mv[pi]], writes=[r_vpe[pi]])
                b.op("act", "activation", rsd[pi][:], vpe[pi][:], AF.Ln,
                     reads=[r_vpe[pi]], writes=[r_rsd[pi]])
                b.op("act", "activation", rsd[pi][:], rsd[pi][:], AF.Exp, scale=-0.5,
                     reads=[r_rsd[pi]], writes=[r_rsd[pi]])
                b.op("act", "activation", dst[:], src[:], AF.Identity, bias=ngm[pi][:],
                     reads=[rsrc, r_ngm[pi]], writes=[rdst])
                b.op("act", "activation", dst[:], dst[:], AF.Identity, scale=rsd[pi][:],
                     reads=[rdst, r_rsd[pi]], writes=[rdst])

            for s in range(P_SEQ):
                for dc in range(8):
                    b.op("pool", "dma_start", out=wvo_sb[:, dc * D:(dc + 1) * D],
                         in_=wv_scr[dc], reads=[r_wv_scr], writes=[r_wvo])
                if s == 0:
                    for tc in range(16):
                        b.op("dve", "memset",
                             v_full[:, tc * H * 65:(tc + 1) * H * 65].rearrange(
                                 "p (h c) -> p h c", c=65)[:, :, 64:65],
                             1.0, writes=[r_vfull], late=True)
                def a1_ln(tt):
                    pi = tt % 2
                    xb = x_t[pi]; rx = r_xt[pi]
                    b.op("sp", "dma_start", out=xb[:],
                         in_=x_in[s, tt * 128:(tt + 1) * 128, :], writes=[rx])
                    plain_ln(pi, xb, rx, u0[pi], r_u0[pi])
                    b.op("dve", "tensor_tensor", u0[pi][:], u0[pi][:], ga_bc[:], ALU.mult,
                         reads=[r_u0[pi], r_consts], writes=[r_u0[pi]])
                    b.op("dve", "tensor_tensor", u0[pi][:], u0[pi][:], ba_bc[:], ALU.add,
                         reads=[r_u0[pi], r_consts], writes=[r_u0[pi]])

                def a1_tr(tt):
                    pi = tt % 2
                    for half in range(2):
                        bank = 6 + half
                        for j in range(4):
                            dc = half * 4 + j
                            b.op("ten", "transpose", pb[bank][0:128, j * 128:(j + 1) * 128],
                                 u0[pi][:, dc * 128:(dc + 1) * 128], idn_f[:],
                                 reads=[r_u0[pi], r_consts], writes=[pres[bank]])
                        dst = u_fm[:, 0:8 * T].rearrange("p (dc t) -> p dc t", dc=8)[
                            :, half * 4:(half + 1) * 4, tt * 128:(tt + 1) * 128]
                        b.op("dve", "tensor_copy", dst,
                             pb[bank][0:128, 0:512].rearrange("p (dc t) -> p dc t", dc=4),
                             reads=[pres[bank]], writes=[r_ufm])

                a1_ln(0)
                for tt in range(16):
                    if tt + 1 < 16:
                        a1_ln(tt + 1)
                    a1_tr(tt)
                for oc in range(16):
                    wb = wqk_buf[oc % 2]; rw = r_wqk[oc % 2]
                    b.op("pool", "dma_start",
                         out=wb[:].rearrange("p (dc c) -> p dc c", dc=8),
                         in_=wqk_scr[oc].rearrange("dc p c -> p dc c"),
                         reads=[r_wqk_scr], writes=[rw])
                    stg = qk_stage[oc % 2]; rstg = r_qks[oc % 2]
                    for ncn in range(4):
                        bank = ncn % 3
                        for dc in range(8):
                            b.op("ten", "matmul", pb[bank][0:128, 0:512],
                                 wb[:, dc * 128:(dc + 1) * 128],
                                 u_fm[:, dc * T + ncn * 512: dc * T + (ncn + 1) * 512],
                                 start=(dc == 0), stop=(dc == 7),
                                 reads=[rw, r_ufm], writes=[pres[bank]])
                        for hh in range(2):
                            b.op("act", "activation",
                                 stg[64 * hh:64 * (hh + 1), ncn * 512:(ncn + 1) * 512],
                                 pb[bank][64 * hh:64 * (hh + 1), 0:512], AF.Identity,
                                 bias=bq_col[64 * hh:64 * (hh + 1), oc:oc + 1],
                                 reads=[pres[bank], r_consts], writes=[rstg])
                    b.op("sp", "dma_start", out=qk_park[oc // 8, oc % 8], in_=stg[:],
                         reads=[rstg], writes=[r_qkpark])
                for tc in range(16):
                    for dh in range(2):
                        bank = 3 + dh
                        for dc in range(8):
                            b.op("ten", "matmul", pb[bank][0:128, 0:512],
                                 u_fm[:, dc * T + tc * 128: dc * T + (tc + 1) * 128],
                                 wvo_sb[:, dc * D + dh * 512: dc * D + (dh + 1) * 512],
                                 start=(dc == 0), stop=(dc == 7),
                                 reads=[r_ufm, r_wvo], writes=[pres[bank]])
                        for j in range(8):
                            h = dh * 8 + j
                            b.op("dve", "tensor_tensor",
                                 v_full[:, tc * H * 65 + h * 65: tc * H * 65 + h * 65 + 64],
                                 pb[bank][0:128, j * 64:(j + 1) * 64],
                                 bvr_bc[:, dh * 512 + j * 64: dh * 512 + (j + 1) * 64],
                                 ALU.add, reads=[pres[bank], r_consts], writes=[r_vfull])
                for dc in range(8):
                    b.op("pool", "dma_start", out=wvo_sb[:, dc * D:(dc + 1) * D],
                         in_=wout_scr[dc], reads=[r_wout_scr], writes=[r_wvo])

                if True:
                    _eper = E // P_SEQ
                    for e in range(_eper * s, _eper * (s + 1)):
                        for fc in range(32):
                            src = w1_in[e, :, fc * 128:(fc + 1) * 128].rearrange(
                                "(dc p) f -> dc p f", p=128)
                            b.op("pool", "dma_start", out=w1_scr[e, fc], in_=src,
                                 writes=[r_w1_scr[e]])
                        for fc in range(32):
                            src = w2_in[e, fc * 128:(fc + 1) * 128, :].rearrange(
                                "p (dh j) -> dh p j", j=512)
                            b.op("pool", "dma_start", out=w2_scr[e, :, fc], in_=src,
                                 writes=[r_w2_scr[e]])

                def prep_loads(qc, h):
                    pi = h % 2
                    qb = qa[pi]; kb = ka[pi]
                    b.op("sp", "dma_start", out=qb[0:64, :],
                         in_=qk_park[0, h // 2, 64 * (h % 2):64 * (h % 2) + 64,
                                     qc * 512:(qc + 1) * 512],
                         reads=[r_qkpark], writes=[r_qa_d[pi]])
                    b.op("sp", "dma_start", out=kb[0:64, :],
                         in_=qk_park[1, h // 2, 64 * (h % 2):64 * (h % 2) + 64, :],
                         reads=[r_qkpark], writes=[r_ka_d[pi]])
                    if s == 0 and qc == 0 and h < 2:
                        b.op("dve", "memset", kb[64:65, :], 1.0, writes=[r_ka_o[pi]],
                             late=True)
                    b.op("dve", "tensor_tensor", sqt[pi][:, :], qb[0:64, :], qb[0:64, :],
                         ALU.mult, reads=[r_qa_d[pi]], writes=[r_sqt[pi]])

                def prep_bound(qc, h):
                    pi = h % 2
                    qb = qa[pi]; kb = ka[pi]
                    nb = 5 + pi
                    if qc == 0:
                        b.op("dve", "memset", kmx[0:1, h:h + 1], -1e30, writes=[r_kmx])
                        for i in range(4):
                            b.op("dve", "tensor_tensor", sqt[pi][:, :],
                                 kb[0:64, i * 512:(i + 1) * 512],
                                 kb[0:64, i * 512:(i + 1) * 512], ALU.mult,
                                 reads=[r_ka_d[pi]], writes=[r_sqt[pi]])
                            b.op("ten", "matmul", pb[nb][0:1, 0:512], ones_c64[:],
                                 sqt[pi][:], start=True, stop=True,
                                 reads=[r_sqt[pi], r_consts], writes=[pres[nb]])
                            b.op("dve", "tensor_reduce", sc2[pi][:], pb[nb][0:1, 0:512],
                                 AX.X, ALU.max, reads=[pres[nb]], writes=[r_sc2[pi]])
                            b.op("dve", "tensor_tensor", kmx[0:1, h:h + 1],
                                 kmx[0:1, h:h + 1], sc2[pi][:], ALU.max,
                                 reads=[r_sc2[pi], r_kmx], writes=[r_kmx])
                        b.op("act", "activation", kmx[0:1, h:h + 1], kmx[0:1, h:h + 1],
                             AF.Ln, reads=[r_kmx], writes=[r_kmx])
                        b.op("dve", "tensor_scalar", kmx[0:1, h:h + 1],
                             kmx[0:1, h:h + 1], 0.5, None, ALU.mult,
                             reads=[r_kmx], writes=[r_kmx])
                        b.op("dve", "tensor_tensor", sqt[pi][:, :], qb[0:64, :], qb[0:64, :],
                             ALU.mult, reads=[r_qa_d[pi]], writes=[r_sqt[pi]])
                    b.op("ten", "matmul", pb[nb][0:1, 0:512], ones_c64[:], sqt[pi][:],
                         start=True, stop=True, reads=[r_sqt[pi], r_consts],
                         writes=[pres[nb]])
                    b.op("act", "activation", lnq[pi][:], pb[nb][0:1, 0:512], AF.Ln,
                         reads=[pres[nb]], writes=[r_lnq[pi]])
                    b.op("act", "activation", lnq[pi][:], lnq[pi][:], AF.Exp,
                         scale=0.5, bias=kmx[0:1, h:h + 1],
                         reads=[r_lnq[pi], r_kmx], writes=[r_lnq[pi]])
                    b.op("dve", "tensor_scalar", qb[64:65, :], lnq[pi][:], -1.0, None,
                         ALU.mult, reads=[r_lnq[pi]], writes=[r_qa_b[pi]])

                def core(qc, h, inject):
                    pi = h % 2
                    qb = qa[pi]; kb = ka[pi]
                    pvbank = 3 + pi

                    def plan_scores(kc):
                        bank = kc % 3
                        b.op("ten", "matmul", pb[bank][0:128, 0:512],
                             kb[:, kc * 128:(kc + 1) * 128], qb[:, :],
                             start=True, stop=True,
                             reads=[r_ka_d[pi], r_ka_o[pi], r_qa_d[pi], r_qa_b[pi]],
                             writes=[pres[bank]])

                    def plan_exp(kc):
                        bank = kc % 3
                        b.op("act", "activation", P_sb[:, kc * 512:(kc + 1) * 512],
                             pb[bank][0:128, 0:512], AF.Exp, scale=0.125,
                             reads=[pres[bank]], writes=[r_P[kc]])

                    def plan_pv(kc):
                        b.op("ten", "matmul", pb[pvbank][0:65, 0:512],
                             v_full[:, kc * H * 65 + h * 65: kc * H * 65 + (h + 1) * 65],
                             P_sb[:, kc * 512:(kc + 1) * 512],
                             start=(kc == 0), stop=(kc == 15),
                             reads=[r_vfull, r_P[kc]], writes=[pres[pvbank]])

                    plan_scores(0)
                    plan_scores(1)
                    plan_scores(2)
                    for kc in range(16):
                        plan_exp(kc)
                        if kc + 3 < 16:
                            plan_scores(kc + 3)
                        if kc == 6 and inject is not None:
                            inject()
                        plan_pv(kc)
                    b.op("dve", "reciprocal", rcp[pi][:], pb[pvbank][64:65, 0:512],
                         reads=[pres[pvbank]], writes=[r_rcp[pi]])

                def norm_tail(h):
                    pi = h % 2
                    pvbank = 3 + pi
                    b.op("ten", "matmul", pb[7][0:64, 0:512], ones_r64f[:], rcp[pi][:],
                         start=True, stop=True, reads=[r_rcp[pi], r_consts],
                         writes=[pres[7]])
                    b.op("act", "activation", bcn[pi][:], pb[7][0:64, 0:512], AF.Identity,
                         reads=[pres[7]], writes=[r_bcn[pi]])
                    if h % 2 == 0:
                        b.op("dve", "tensor_tensor",
                             att[0:64, (h // 2) * 512:(h // 2 + 1) * 512],
                             pb[pvbank][0:64, 0:512], bcn[pi][:], ALU.mult,
                             reads=[pres[pvbank], r_bcn[pi]], writes=[r_att])
                    else:
                        at = att_tmp[(h // 2) % 2]; rat = r_attmp[(h // 2) % 2]
                        b.op("dve", "tensor_tensor", at[:],
                             pb[pvbank][0:64, 0:512], bcn[pi][:], ALU.mult,
                             reads=[pres[pvbank], r_bcn[pi]], writes=[rat])
                        b.op("sp", "dma_start",
                             out=att[64:128, (h // 2) * 512:(h // 2 + 1) * 512],
                             in_=at[:], reads=[rat], writes=[r_att])

                for qc in range(4):
                    prep_loads(qc, 0)
                    prep_bound(qc, 0)
                    for h in range(H):
                        if h + 1 < H:
                            prep_loads(qc, h + 1)
                            inject = (lambda hh=h + 1, qq=qc: prep_bound(qq, hh))
                        else:
                            inject = None
                        core(qc, h, inject)
                        if h > 0:
                            norm_tail(h - 1)
                    norm_tail(H - 1)
                    def a3_mm(tt4):
                        tt = qc * 4 + tt4
                        p2 = tt4 % 2
                        xb = x_t[p2]; rx = r_xt[p2]
                        b.op("sp", "dma_start", out=xb[:],
                             in_=x_in[s, tt * 128:(tt + 1) * 128, :], writes=[rx])
                        for dh in range(2):
                            bank = 2 * (tt4 % 2) + dh
                            for dc in range(8):
                                b.op("ten", "matmul", pb[bank][0:128, 0:512],
                                     att[:, dc * 512 + tt4 * 128: dc * 512 + (tt4 + 1) * 128],
                                     wvo_sb[:, dc * D + dh * 512: dc * D + (dh + 1) * 512],
                                     start=(dc == 0), stop=(dc == 7),
                                     reads=[r_att, r_wvo], writes=[pres[bank]])

                    def a3_post(tt4):
                        tt = qc * 4 + tt4
                        gtok = s * T + tt * 128
                        p2 = tt4 % 2
                        xb = x_t[p2]; rx = r_xt[p2]
                        for dh in range(2):
                            bank = 2 * (tt4 % 2) + dh
                            b.op("act", "activation", zt[p2][:, dh * 512:(dh + 1) * 512],
                                 pb[bank][0:128, 0:512], AF.Identity,
                                 reads=[pres[bank]], writes=[r_zt[p2]])
                        b.op("dve", "tensor_tensor", zt[p2][:], zt[p2][:], bout_bc[:],
                             ALU.add, reads=[r_zt[p2], r_consts], writes=[r_zt[p2]])
                        b.op("dve", "scalar_tensor_tensor", zt[p2][:], xb[:], 2.0, zt[p2][:],
                             ALU.mult, ALU.add, reads=[rx, r_zt[p2]], writes=[r_zt[p2]])
                        plain_ln(p2, zt[p2], r_zt[p2], y1_t[p2], r_y1t[p2])
                        b.op("sp", "dma_start", out=y1_park[gtok:gtok + 128, :],
                             in_=y1_t[p2][:], reads=[r_y1t[p2]], writes=[r_y1park])
                        plain_ln(p2, y1_t[p2], r_y1t[p2], xn0[p2], r_xn0[p2])
                        b.op("dve", "tensor_tensor", xn0[p2][:], xn0[p2][:], gf_bc[:],
                             ALU.mult, reads=[r_xn0[p2], r_consts], writes=[r_xn0[p2]])
                        b.op("dve", "tensor_tensor", xn0[p2][:], xn0[p2][:], bf_bc[:],
                             ALU.add, reads=[r_xn0[p2], r_consts], writes=[r_xn0[p2]])
                        for half in range(2):
                            bank = 6 + half
                            for j in range(4):
                                dc = half * 4 + j
                                b.op("ten", "transpose",
                                     pb[bank][0:128, j * 128:(j + 1) * 128],
                                     xn0[p2][:, dc * 128:(dc + 1) * 128], idn_f[:],
                                     reads=[r_xn0[p2], r_consts], writes=[pres[bank]])
                            b.op("dve", "tensor_copy",
                                 xn_st[p2][:, half * 512:(half + 1) * 512].rearrange(
                                     "p (dc t) -> p dc t", dc=4),
                                 pb[bank][0:128, 0:512].rearrange("p (dc t) -> p dc t", dc=4),
                                 reads=[pres[bank]], writes=[r_xnst[p2]])
                        b.op("sp", "dma_start", out=xn_park[gtok // 128],
                             in_=xn_st[p2][:], reads=[r_xnst[p2]], writes=[r_xnpark])
                        for dc in range(8):
                            b.op("ten", "matmul", pb[5][0:8, 0:128],
                                 wg_col[:, dc * 8:(dc + 1) * 8],
                                 xn_st[p2][:, dc * 128:(dc + 1) * 128],
                                 start=(dc == 0), stop=(dc == 7),
                                 reads=[r_consts, r_xnst[p2]], writes=[pres[5]])
                        b.op("act", "activation", lg_t[p2][:], pb[5][0:8, 0:128], AF.Identity,
                             reads=[pres[5]], writes=[r_lgt[p2]])
                        b.op("ten", "transpose", pb[5][0:128, 0:8], lg_t[p2][:],
                             idn_f[0:8, 0:8], reads=[r_lgt[p2], r_consts], writes=[pres[5]])
                        b.op("act", "activation", gt0_all[:, tt * E:(tt + 1) * E],
                             pb[5][0:128, 0:8], AF.Identity,
                             reads=[pres[5]], writes=[r_gt0a])

                    a3_mm(0)
                    for tt4 in range(4):
                        if tt4 + 1 < 4:
                            a3_mm(tt4 + 1)
                        a3_post(tt4)
                b.op("dve", "tensor_reduce", gm1[:], gv(gt0_all), AX.X, ALU.max,
                     reads=[r_gt0a], writes=[r_gm1])
                bTT(gsub, r_gsub, gt0_all, r_gt0a, gm1, r_gm1, ALU.subtract)
                b.op("act", "activation", gp8[:], gsub[:], AF.Exp,
                     reads=[r_gsub], writes=[r_gp8])
                b.op("dve", "tensor_reduce", gsum[:], gv(gp8), AX.X, ALU.add,
                     reads=[r_gp8], writes=[r_gsum])
                b.op("dve", "reciprocal", grs[:], gsum[:], reads=[r_gsum], writes=[r_grs])
                bTT(gprob, r_gprob, gp8, r_gp8, grs, r_grs, ALU.mult)
                b.op("dve", "tensor_reduce", gt1[:], gv(gprob), AX.X, ALU.max,
                     reads=[r_gprob], writes=[r_gt1])
                b.op("dve", "memset", gpmx[:], -1e30, writes=[r_gpmx])
                for j in range(1, E):
                    b.op("dve", "tensor_tensor", gv(gpmx)[:, :, j:j + 1],
                         gv(gpmx)[:, :, j - 1:j], gv(gprob)[:, :, j - 1:j], ALU.max,
                         reads=[r_gpmx, r_gprob], writes=[r_gpmx], late=True)
                bTT(gam, r_gam, gprob, r_gprob, gt1, r_gt1, ALU.is_ge)
                bTT(gbm, r_gbm, gpmx, r_gpmx, gt1, r_gt1, ALU.is_lt)
                b.op("dve", "tensor_tensor", gis1[:], gam[:], gbm[:], ALU.mult,
                     reads=[r_gam, r_gbm], writes=[r_gis1])
                b.op("dve", "scalar_tensor_tensor", gmsk[:], gis1[:], -1e30, gprob[:],
                     ALU.mult, ALU.add, reads=[r_gis1, r_gprob], writes=[r_gmsk])
                b.op("dve", "tensor_reduce", gt2[:], gv(gmsk), AX.X, ALU.max,
                     reads=[r_gmsk], writes=[r_gt2])
                b.op("dve", "memset", gpmx[:], -1e30, writes=[r_gpmx])
                for j in range(1, E):
                    b.op("dve", "tensor_tensor", gv(gpmx)[:, :, j:j + 1],
                         gv(gpmx)[:, :, j - 1:j], gv(gmsk)[:, :, j - 1:j], ALU.max,
                         reads=[r_gpmx, r_gmsk], writes=[r_gpmx], late=True)
                bTT(gam, r_gam, gmsk, r_gmsk, gt2, r_gt2, ALU.is_ge)
                bTT(gbm, r_gbm, gpmx, r_gpmx, gt2, r_gt2, ALU.is_lt)
                b.op("dve", "tensor_tensor", gsub[:], gam[:], gbm[:], ALU.mult,
                     reads=[r_gam, r_gbm], writes=[r_gsub])
                b.op("dve", "tensor_tensor", gden[:], gt1[:], gt2[:], ALU.add,
                     reads=[r_gt1, r_gt2], writes=[r_gden])
                b.op("dve", "tensor_scalar", gden[:], gden[:], 1e-9, None, ALU.add,
                     reads=[r_gden], writes=[r_gden])
                b.op("dve", "reciprocal", grs[:], gden[:], reads=[r_gden], writes=[r_grs])
                b.op("dve", "tensor_tensor", gt1[:], gt1[:], grs[:], ALU.mult,
                     reads=[r_gt1, r_grs], writes=[r_gt1])
                b.op("dve", "tensor_tensor", gt2[:], gt2[:], grs[:], ALU.mult,
                     reads=[r_gt2, r_grs], writes=[r_gt2])
                bTT(gtmp, r_gtmp, gsub, r_gsub, gt2, r_gt2, ALU.mult)
                bTT(gam, r_gam, gis1, r_gis1, gt1, r_gt1, ALU.mult)
                b.op("dve", "tensor_tensor", gtm_st[:], gam[:], gtmp[:], ALU.add,
                     reads=[r_gam, r_gtmp], writes=[r_gtmst])
                b.op("sp", "dma_start", out=gtm_park[s], in_=gtm_st[:],
                     reads=[r_gtmst], writes=[r_gtmpark])
                for tt in range(16):
                    b.op("ten", "transpose", pb[5][0:8, 0:128],
                         gtm_st[:, tt * E:(tt + 1) * E], idn_f[:],
                         reads=[r_gtmst, r_consts], writes=[pres[5]])
                    b.op("dve", "tensor_copy", gfm_st[:, (tt % 4) * 128:(tt % 4 + 1) * 128],
                         pb[5][0:8, 0:128], reads=[pres[5]], writes=[r_gfmst])
                    if tt % 4 == 3:
                        b.op("sp", "dma_start",
                             out=gfm_park[s, :, (tt // 4) * 512:(tt // 4 + 1) * 512],
                             in_=gfm_st[:], reads=[r_gfmst], writes=[r_gfmpark])

            b.hard_barrier()
            attn_end = b.nidx
            b.finalize_range(pro_end, attn_end)
            b.emit_range(block, pro_end, attn_end)

        with ExitStack() as es_m:
            def sbm(name, shape, dtype):
                return es_m.enter_context(nc.sbuf_tensor(name, shape, dtype))

            xn_blk = [sbm(f"xnb{i}", [128, 8 * TB], BF16) for i in range(2)]
            r_xnb = [b.res() for _ in range(2)]
            h_sb = sbm("h_sb", [128, 32 * TB], BF16)
            r_h = [b.res() for _ in range(32)]
            acc = sbm("acc", [128, 8 * D], F32)
            acc_r = [b.res(f"acc{i}") for i in range(8)]
            w1_buf = [sbm(f"w1b{i}", [128, 4 * 8 * 128], BF16) for i in range(2)]
            r_w1b = [b.res() for _ in range(2)]
            w2_buf = [sbm(f"w2b{i}", [128, 4 * 512], BF16) for i in range(2)]
            r_w2b = [b.res() for _ in range(2)]
            gtm_sb = [sbm(f"gtmsb{i}", [128, 8 * E], F32) for i in range(2)]
            r_gtm = [b.res() for _ in range(2)]
            gfm_sb = [sbm(f"gfmsb{i}", [E, TB], F32) for i in range(2)]
            r_gfm = [b.res() for _ in range(2)]
            ytmp = [sbm(f"ytmp{i}", [128, 512], F32) for i in range(2)]
            r_yt = [b.res() for _ in range(2)]
            y1b = [sbm(f"y1b{i}", [128, D], F32) for i in range(2)]
            r_y1b = [b.res() for _ in range(2)]
            outb = [sbm(f"outb{i}", [128, D], F32) for i in range(2)]
            r_outb = [b.res() for _ in range(2)]
            bn6m = [sbm(f"bn6m{i}", [128, 12], F32) for i in range(2)]
            r_bn6m = [b.res() for _ in range(2)]
            mvm = [sbm(f"mvm{i}", [128, 2], F32) for i in range(2)]
            r_mvm = [b.res() for _ in range(2)]
            vpem = [sbm(f"vpem{i}", [128, 1], F32) for i in range(2)]
            r_vpem = [b.res() for _ in range(2)]
            ngmm = [sbm(f"ngmm{i}", [128, 1], F32) for i in range(2)]
            r_ngmm = [b.res() for _ in range(2)]
            rsdm = [sbm(f"rsdm{i}", [128, 1], F32) for i in range(2)]
            r_rsdm = [b.res() for _ in range(2)]

            ntc = TB // 128
            for tb in range(NTB):
                s = tb // NHALF
                half = tb % NHALF
                tpi = tb % 2
                xnb = xn_blk[tpi]; rxnb = r_xnb[tpi]
                xview = xnb[:, :].rearrange("p (ti dc j) -> p ti dc j", ti=8, dc=8)
                for ti in range(8):
                    b.op("sp", "dma_start", out=xnb[:, ti * 1024:(ti + 1) * 1024],
                         in_=xn_park[tb * 8 + ti], reads=[r_xnpark], writes=[rxnb])
                b.op("sp", "dma_start", out=gtm_sb[tpi][:],
                     in_=gtm_park[s][:, half * 8 * E:(half + 1) * 8 * E],
                     reads=[r_gtmpark], writes=[r_gtm[tpi]])
                b.op("sp", "dma_start", out=gfm_sb[tpi][:],
                     in_=gfm_park[s][:, half * TB:(half + 1) * TB],
                     reads=[r_gfmpark], writes=[r_gfm[tpi]])
                def plan_acc_init():
                    for dh in range(2):
                        for tc in range(ntc):
                            b.op("ten", "matmul", pb[tc][0:128, 0:512],
                                 gfm_sb[tpi][:, tc * 128:(tc + 1) * 128],
                                 b2_sb[:, dh * 512:(dh + 1) * 512], start=True, stop=True,
                                 reads=[r_gfm[tpi], r_consts], writes=[pres[tc]])
                            b.op("act", "activation",
                                 acc[:, tc * D + dh * 512: tc * D + (dh + 1) * 512],
                                 pb[tc][0:128, 0:512], AF.Identity,
                                 reads=[pres[tc]], writes=[acc_r[tc]])
                for e in range(E):
                    for fcg in range(8):
                        wb = w1_buf[fcg % 2]; rw = r_w1b[fcg % 2]
                        b.op("pool", "dma_start",
                             out=wb[:].rearrange("p (fi dc f) -> p fi dc f", fi=4, dc=8),
                             in_=w1_scr[e, fcg * 4:(fcg + 1) * 4].rearrange(
                                 "fi dc p f -> p fi dc f"),
                             reads=[r_w1_scr[e]], writes=[rw])
                        for fi in range(4):
                            fc = fcg * 4 + fi
                            for th in range(2):
                                bank = (fc * 2 + th) % 3
                                for dc in range(8):
                                    b.op("ten", "matmul", pb[bank][0:128, 0:512],
                                         wb[:, (fi * 8 + dc) * 128:(fi * 8 + dc + 1) * 128],
                                         xview[:, th * 4:(th + 1) * 4, dc, :],
                                         start=(dc == 0), stop=(dc == 7),
                                         reads=[rw, rxnb], writes=[pres[bank]])
                                b.op("act", "activation",
                                     h_sb[:, fc * TB + th * 512: fc * TB + (th + 1) * 512],
                                     pb[bank][0:128, 0:512], AF.Gelu_apprx_tanh,
                                     bias=b1_col[:, fc * 8 + e: fc * 8 + e + 1],
                                     reads=[pres[bank], r_consts], writes=[r_h[fc]])
                    if "h_dump" in dbg and tb == 0 and e == 0:
                        b.op("sp", "dma_start", out=h_dump[:, :], in_=h_sb[:, :],
                             reads=[r_h[0], r_h[31]], writes=[r_out])
                    if e == 0:
                        plan_acc_init()
                    for dh in range(2):
                        for fcg in range(8):
                            wb = w2_buf[fcg % 2]; rw = r_w2b[fcg % 2]
                            b.op("pool", "dma_start",
                                 out=wb[:].rearrange("p (fi j) -> p fi j", fi=4),
                                 in_=w2_scr[e, dh, fcg * 4:(fcg + 1) * 4].rearrange(
                                     "fi p j -> p fi j"),
                                 reads=[r_w2_scr[e]], writes=[rw])
                            for fi in range(4):
                                fc = fcg * 4 + fi
                                for tc in range(ntc):
                                    b.op("ten", "matmul", pb[tc][0:128, 0:512],
                                         h_sb[:, fc * TB + tc * 128: fc * TB + (tc + 1) * 128],
                                         wb[:, fi * 512:(fi + 1) * 512],
                                         start=(fc == 0), stop=(fc == 31),
                                         reads=[rw, r_h[fc]], writes=[pres[tc]])
                        for tc in range(ntc):
                            yb = ytmp[tc % 2]; ry = r_yt[tc % 2]
                            b.op("act", "activation", yb[:], pb[tc][0:128, 0:512], AF.Identity,
                                 scale=gtm_sb[tpi][:, tc * E + e: tc * E + e + 1],
                                 reads=[pres[tc], r_gtm[tpi]], writes=[ry])
                            b.op("dve", "tensor_tensor",
                                 acc[:, tc * D + dh * 512: tc * D + (dh + 1) * 512],
                                 acc[:, tc * D + dh * 512: tc * D + (dh + 1) * 512],
                                 yb[:], ALU.add, reads=[ry, acc_r[tc]], writes=[acc_r[tc]])
                if "acc_dump" in dbg:
                    b.op("sp", "dma_start", out=acc_dump[tb], in_=acc[:, :],
                         reads=[acc_r[0], acc_r[1], acc_r[2], acc_r[3], acc_r[4],
                                acc_r[5], acc_r[6], acc_r[7]], writes=[r_out])
                for tc in range(ntc):
                    gtok = tb * TB + tc * 128
                    p2 = tc % 2
                    yb = y1b[p2]; ry = r_y1b[p2]
                    ob = outb[p2]; ro = r_outb[p2]
                    b.op("sp", "dma_start", out=yb[:], in_=y1_park[gtok:gtok + 128, :],
                         reads=[r_y1park], writes=[ry])
                    b.op("dve", "scalar_tensor_tensor", ob[:], yb[:], 2.0,
                         acc[:, tc * D:(tc + 1) * D], ALU.mult, ALU.add,
                         reads=[ry, acc_r[tc]], writes=[ro])
                    b.op("dve", "bn_stats", bn6m[p2][:, 0:6], ob[:, 0:512],
                         reads=[ro], writes=[r_bn6m[p2]])
                    b.op("dve", "bn_stats", bn6m[p2][:, 6:12], ob[:, 512:1024],
                         reads=[ro], writes=[r_bn6m[p2]])
                    b.op("dve", "bn_aggr", mvm[p2][:], bn6m[p2][:, 0:12],
                         reads=[r_bn6m[p2]], writes=[r_mvm[p2]])
                    b.op("dve", "tensor_scalar", ngmm[p2][:], mvm[p2][:, 0:1], -1.0, None,
                         ALU.mult, reads=[r_mvm[p2]], writes=[r_ngmm[p2]])
                    b.op("dve", "tensor_scalar", vpem[p2][:], mvm[p2][:, 1:2], EPS, None,
                         ALU.add, reads=[r_mvm[p2]], writes=[r_vpem[p2]])
                    b.op("act", "activation", rsdm[p2][:], vpem[p2][:], AF.Ln,
                         reads=[r_vpem[p2]], writes=[r_rsdm[p2]])
                    b.op("act", "activation", rsdm[p2][:], rsdm[p2][:], AF.Exp, scale=-0.5,
                         reads=[r_rsdm[p2]], writes=[r_rsdm[p2]])
                    b.op("act", "activation", ob[:], ob[:], AF.Identity, bias=ngmm[p2][:],
                         reads=[ro, r_ngmm[p2]], writes=[ro])
                    b.op("act", "activation", ob[:], ob[:], AF.Identity, scale=rsdm[p2][:],
                         reads=[ro, r_rsdm[p2]], writes=[ro])
                    b.op("act", "dma_start",
                         out=out_d[s, half * TB + tc * 128: half * TB + (tc + 1) * 128, :],
                         in_=ob[:], reads=[ro], writes=[r_out])
            b.op("sp", "wait_only", reads=[r_out])

            moe_end = b.nidx
            b.finalize_range(attn_end, moe_end)
            b.emit_range(block, attn_end, moe_end)
    return nc


_NC_CACHE = {}


def _get_nc():
    if "nc" not in _NC_CACHE:
        _NC_CACHE["nc"] = build_program()
    return _NC_CACHE["nc"]


def kernel(**inputs):
    nc = _get_nc()
    x = np.ascontiguousarray(np.asarray(inputs["L0_attn_x"], dtype=np.float32))
    names = ["L0_attn_ln_gamma", "L0_attn_ln_beta", "L0_attn_w_qkv", "L0_attn_b_qkv",
             "L0_attn_w_out", "L0_attn_b_out", "L0_ffn_gamma", "L0_ffn_beta",
             "L0_ffn_Wg", "L0_ffn_W1", "L0_ffn_b1", "L0_ffn_W2", "L0_ffn_b2"]
    shared = {n: np.ascontiguousarray(np.asarray(inputs[n], dtype=np.float32)) for n in names}
    in_maps = []
    for c in range(NCORES):
        m = dict(shared)
        m["L0_attn_x"] = np.ascontiguousarray(x[c * 4:(c + 1) * 4])
        in_maps.append(m)
    res = run_bass_kernel_spmd(nc, in_maps, core_ids=list(range(NCORES)))
    out = np.concatenate([r["out"] for r in res.results], axis=0)
    return out


# version 05
# speedup vs baseline: 1.2659x; 1.0664x over previous
import numpy as np
from contextlib import ExitStack

import concourse.bass as bass
import concourse.mybir as mybir
from concourse.bass import broadcast_tensor_aps
from concourse.bass_utils import run_bass_kernel_spmd

F32 = mybir.dt.float32
BF16 = mybir.dt.bfloat16
AF = mybir.ActivationFunctionType
ALU = mybir.AluOpType
AX = mybir.AxisListType

T = 2048
D = 1024
H = 16
E = 8
FF = 4096
TB = 1024
EPS = 1e-5
NCORES = 8

ENG2BASS = {"ten": "tensor", "act": "scalar", "dve": "vector", "pool": "gpsimd", "sp": "sync"}
LATE_METHODS = {"bn_stats", "bn_aggr", "tensor_reduce", "reduce_max", "reduce_sum",
                "reciprocal", "dma_start", "memset"}


class Rec:
    __slots__ = ("eng", "method", "args", "kw", "waits", "need_inc", "inc_amt", "val", "late", "idx")

    def __init__(self, eng, method, args, kw, late, idx):
        self.eng = eng
        self.method = method
        self.args = args
        self.kw = kw
        self.waits = []
        self.need_inc = False
        self.inc_amt = 16 if method == "dma_start" else 1
        self.val = None
        self.late = late
        self.idx = idx


class Res:
    __slots__ = ("name", "w", "rs")

    def __init__(self, name="r"):
        self.name = name
        self.w = None
        self.rs = []


class B:
    def __init__(self, nc):
        self.nc = nc
        self.q = {e: [] for e in ENG2BASS}
        self.pending = {e: [] for e in ENG2BASS}
        self.nidx = 0
        self.ctr = {e: 0 for e in ENG2BASS}
        self.hwm = {e: {} for e in ENG2BASS}
        self.all_res = []
        self.sems = None

    def res(self, name="r"):
        r = Res(name)
        self.all_res.append(r)
        return r

    def op(self, eng, method, *args, reads=(), writes=(), late=None, **kw):
        if late is None:
            late = method in LATE_METHODS or eng in ("dve", "act", "pool")
        r = Rec(eng, method, args, kw, late, self.nidx)
        self.nidx += 1
        if method == "dma_start":
            r.need_inc = True
        deps = []
        for res in reads:
            if res.w is not None:
                deps.append(res.w)
        for res in writes:
            if res.w is not None:
                deps.append(res.w)
            deps.extend(res.rs)
        deps.extend(self.pending[eng])
        self.pending[eng] = []
        best = {}
        for d in deps:
            if d.eng == eng and not d.late:
                continue
            cur = best.get(d.eng)
            if cur is None or d.idx > cur.idx:
                best[d.eng] = d
        for d in best.values():
            d.need_inc = True
            r.waits.append(d)
        for res in writes:
            res.w = r
            res.rs = []
        for res in reads:
            res.rs.append(r)
        self.q[eng].append(r)
        return r

    def hard_barrier(self):
        lasts = [self.q[e][-1] for e in ENG2BASS if self.q[e]]
        for d in lasts:
            d.need_inc = True
        for e in ENG2BASS:
            self.pending[e] = list(lasts)
        for r in self.all_res:
            r.w = None
            r.rs = []

    def finalize_range(self, lo, hi):
        for e in ENG2BASS:
            for r in self.q[e]:
                if lo <= r.idx < hi and r.need_inc:
                    self.ctr[e] += r.inc_amt
                    r.val = self.ctr[e]

    def emit_range(self, block, lo, hi):
        sems = self.sems
        for e, bname in ENG2BASS.items():
            recs = [r for r in self.q[e] if lo <= r.idx < hi]
            if not recs:
                continue

            def emit(h, recs=recs, e=e):
                hw = self.hwm[e]
                for r in recs:
                    for d in r.waits:
                        assert d.val is not None, (r.method, d.method, d.eng)
                        if hw.get(d.eng, 0) >= d.val:
                            continue
                        hw[d.eng] = d.val
                        h.wait_ge(sems[d.eng], d.val)
                    if r.method == "wait_only":
                        continue
                    ins = getattr(h, r.method)(*r.args, **r.kw)
                    if r.need_inc:
                        ins.then_inc(sems[r.eng], r.inc_amt)

            getattr(block, bname)(emit)


def build_program(P_SEQ=4, dbg=()):
    NTOK = P_SEQ * T
    NTB = NTOK // TB
    NHALF = T // TB
    nc = bass.Bass("TRN2", target_bir_lowering=False)

    def dram(name, shape, dtype, kind):
        if name in dbg:
            kind = "ExternalOutput"
        return nc.dram_tensor(name, shape, dtype, kind=kind).ap()

    x_in = dram("L0_attn_x", [P_SEQ, T, D], F32, "ExternalInput")
    ga_in = dram("L0_attn_ln_gamma", [D], F32, "ExternalInput")
    ba_in = dram("L0_attn_ln_beta", [D], F32, "ExternalInput")
    wqkv_in = dram("L0_attn_w_qkv", [D, 3 * D], F32, "ExternalInput")
    bqkv_in = dram("L0_attn_b_qkv", [3 * D], F32, "ExternalInput")
    wout_in = dram("L0_attn_w_out", [D, D], F32, "ExternalInput")
    bout_in = dram("L0_attn_b_out", [D], F32, "ExternalInput")
    gf_in = dram("L0_ffn_gamma", [D], F32, "ExternalInput")
    bf_in = dram("L0_ffn_beta", [D], F32, "ExternalInput")
    wg_in = dram("L0_ffn_Wg", [E, D], F32, "ExternalInput")
    w1_in = dram("L0_ffn_W1", [E, D, FF], F32, "ExternalInput")
    b1_in = dram("L0_ffn_b1", [E, FF], F32, "ExternalInput")
    w2_in = dram("L0_ffn_W2", [E, FF, D], F32, "ExternalInput")
    b2_in = dram("L0_ffn_b2", [E, D], F32, "ExternalInput")
    out_d = dram("out", [P_SEQ, T, D], F32, "ExternalOutput")

    wqk_scr = dram("wqk_scr", [16, 8, 128, 128], BF16, "Internal")
    wv_scr = dram("wv_scr", [8, 128, D], BF16, "Internal")
    wout_scr = dram("wout_scr", [8, 128, D], BF16, "Internal")
    w1_scr = dram("w1_scr", [E, 32, 8, 128, 128], BF16, "Internal")
    w2_scr = dram("w2_scr", [E, 2, 32, 128, 512], BF16, "Internal")
    qk_park = dram("qk_park", [2, 8, 128, T], BF16, "Internal")
    y1_park = dram("y1_park", [NTOK, D], F32, "Internal")
    xn_park = dram("xn_park", [NTOK // 128, 128, 8 * 128], BF16, "Internal")
    gtm_park = dram("gtm_park", [P_SEQ, 128, 16 * E], F32, "Internal")
    acc_dump = dram("acc_dump", [NTOK // TB, 128, 8 * D], F32, "Internal")
    h_dump = dram("h_dump", [128, 32 * TB], BF16, "Internal")
    gfm_park = dram("gfm_park", [P_SEQ, E, T], F32, "Internal")

    b = B(nc)

    with ExitStack() as es:
        b.sems = {e: es.enter_context(nc.semaphore(f"sem_{e}")) for e in ENG2BASS}

        def sb(name, shape, dtype):
            return es.enter_context(nc.sbuf_tensor(name, shape, dtype))

        pb = [es.enter_context(nc.psum_tensor(f"pb{i}", [128, 512], F32)) for i in range(8)]
        pres = [b.res(f"pb{i}") for i in range(8)]

        idn_f = sb("idn_f", [128, 128], F32)
        ones_c64 = sb("ones_c64", [64, 1], BF16)
        ones_r128 = sb("ones_r128", [1, 128], F32)
        ones_r64f = sb("ones_r64f", [1, 64], F32)
        b1_col = sb("b1_col", [128, 32 * E], F32)
        b2_sb = sb("b2_sb", [E, D], F32)
        wg_col = sb("wg_col", [128, 8 * E], BF16)
        bq_col = sb("bq_col", [128, 16], F32)
        ga_bc = sb("ga_bc", [128, D], BF16)
        ba_bc = sb("ba_bc", [128, D], BF16)
        gf_bc = sb("gf_bc", [128, D], BF16)
        bf_bc = sb("bf_bc", [128, D], BF16)
        bout_bc = sb("bout_bc", [128, D], BF16)
        bvr_bc = sb("bvr_bc", [128, D], BF16)

        r_consts = b.res("consts")
        r_wqk_scr = b.res(); r_wv_scr = b.res(); r_wout_scr = b.res()
        r_w1_scr = [b.res() for _ in range(E)]
        r_w2_scr = [b.res() for _ in range(E)]
        r_qkpark = b.res()
        r_y1park = b.res(); r_xnpark = b.res(); r_gtmpark = b.res(); r_gfmpark = b.res()
        r_out = b.res()

        block = es.enter_context(nc.Block())

        for oc in range(16):
            src = wqkv_in[:, oc * 128:(oc + 1) * 128].rearrange("(dc p) c -> dc p c", p=128)
            b.op("pool", "dma_start", out=wqk_scr[oc], in_=src, writes=[r_wqk_scr])
        for dc in range(8):
            b.op("pool", "dma_start", out=wv_scr[dc],
                 in_=wqkv_in[dc * 128:(dc + 1) * 128, 2 * D:3 * D], writes=[r_wv_scr])
            b.op("pool", "dma_start", out=wout_scr[dc],
                 in_=wout_in[dc * 128:(dc + 1) * 128, :], writes=[r_wout_scr])

        b.op("pool", "memset", idn_f[:], 0.0, writes=[r_consts])
        b.op("pool", "affine_select", out=idn_f[:], in_=idn_f[:], compare_op=ALU.not_equal,
             fill=1.0, base=0, pattern=[[-1, 128]], channel_multiplier=1,
             reads=[r_consts], writes=[r_consts])
        b.op("pool", "memset", ones_c64[:], 1.0, writes=[r_consts])
        b.op("pool", "memset", ones_r128[:], 1.0, writes=[r_consts])
        b.op("pool", "memset", ones_r64f[:], 1.0, writes=[r_consts])

        with ExitStack() as es_pro:
            rowbuf = es_pro.enter_context(nc.sbuf_tensor("rowbuf", [1, 3 * D], F32))
            b1_sb8 = es_pro.enter_context(nc.sbuf_tensor("b1_sb8", [E, FF], F32))
            wg_sb8 = es_pro.enter_context(nc.sbuf_tensor("wg_sb8", [E, D], F32))
            row_r = b.res(); s8_r = b.res()

            def bcast_row(dst, ofs, n):
                for j0 in range(0, n, 512):
                    w = min(512, n - j0)
                    b.op("ten", "matmul", pb[7][0:128, 0:w], ones_r128[:, :],
                         rowbuf[0:1, ofs + j0: ofs + j0 + w], start=True, stop=True,
                         reads=[r_consts, row_r], writes=[pres[7]])
                    b.op("act", "activation", dst[:, j0:j0 + w], pb[7][0:128, 0:w], AF.Identity,
                         reads=[pres[7]], writes=[r_consts])

            for src_t, dst in [(ga_in, ga_bc), (ba_in, ba_bc), (gf_in, gf_bc),
                               (bf_in, bf_bc), (bout_in, bout_bc)]:
                b.op("sp", "dma_start", out=rowbuf[0:1, 0:D],
                     in_=src_t.rearrange("(a d) -> a d", a=1), writes=[row_r])
                bcast_row(dst, 0, D)
            b.op("sp", "dma_start", out=rowbuf[0:1, 0:3 * D],
                 in_=bqkv_in.rearrange("(a d) -> a d", a=1), writes=[row_r])
            for oc in range(16):
                b.op("ten", "transpose", pb[6][0:128, 0:1],
                     rowbuf[0:1, oc * 128:(oc + 1) * 128], idn_f[0:1, 0:1],
                     reads=[row_r, r_consts], writes=[pres[6]])
                b.op("act", "activation", bq_col[:, oc:oc + 1], pb[6][0:128, 0:1], AF.Identity,
                     reads=[pres[6]], writes=[r_consts])
            bcast_row(bvr_bc, 2 * D, D)
            b.op("sp", "dma_start", out=b1_sb8[:, :], in_=b1_in[:, :], writes=[s8_r])
            for fc in range(32):
                b.op("ten", "transpose", pb[6][0:128, 0:8],
                     b1_sb8[:, fc * 128:(fc + 1) * 128], idn_f[0:8, 0:8],
                     reads=[s8_r, r_consts], writes=[pres[6]])
                b.op("act", "activation", b1_col[:, fc * 8:(fc + 1) * 8], pb[6][0:128, 0:8],
                     AF.Identity, reads=[pres[6]], writes=[r_consts])
            b.op("sp", "dma_start", out=wg_sb8[:, :], in_=wg_in[:, :], writes=[s8_r])
            for dc in range(8):
                b.op("ten", "transpose", pb[6][0:128, 0:8],
                     wg_sb8[:, dc * 128:(dc + 1) * 128], idn_f[0:8, 0:8],
                     reads=[s8_r, r_consts], writes=[pres[6]])
                b.op("dve", "tensor_copy", wg_col[:, dc * 8:(dc + 1) * 8], pb[6][0:128, 0:8],
                     reads=[pres[6]], writes=[r_consts])
            b.op("sp", "dma_start", out=b2_sb[:, :], in_=b2_in[:, :], writes=[r_consts])

            b.hard_barrier()
            pro_end = b.nidx
            b.finalize_range(0, pro_end)
            b.emit_range(block, 0, pro_end)

        with ExitStack() as es_a:
            def sba(name, shape, dtype):
                return es_a.enter_context(nc.sbuf_tensor(name, shape, dtype))

            u_fm = sba("u_fm", [128, 8 * T], BF16); r_ufm = b.res("ufm")
            x_t = [sba(f"x_t{i}", [128, D], F32) for i in range(2)]
            r_xt = [b.res() for _ in range(2)]
            u0 = [sba(f"u0_{i}", [128, D], F32) for i in range(2)]
            r_u0 = [b.res() for _ in range(2)]
            bn6 = [sba(f"bn6_{i}", [128, 12], F32) for i in range(2)]
            r_bn6 = [b.res() for _ in range(2)]
            mv = [sba(f"mv_{i}", [128, 2], F32) for i in range(2)]
            r_mv = [b.res() for _ in range(2)]
            vpe = [sba(f"vpe_{i}", [128, 1], F32) for i in range(2)]
            r_vpe = [b.res() for _ in range(2)]
            ngm = [sba(f"ngm_{i}", [128, 1], F32) for i in range(2)]
            r_ngm = [b.res() for _ in range(2)]
            rsd = [sba(f"rsd_{i}", [128, 1], F32) for i in range(2)]
            r_rsd = [b.res() for _ in range(2)]
            wqk_buf = [sba(f"wqk{i}", [128, 8 * 128], BF16) for i in range(2)]
            r_wqk = [b.res() for _ in range(2)]
            wvo_sb = sba("wvo_sb", [128, 8 * D], BF16); r_wvo = b.res()
            qk_stage = [sba("qks0", [128, T], BF16)] * 2
            _rq = b.res()
            r_qks = [_rq, _rq]
            v_full = sba("v_full", [128, 16 * H * 65], BF16); r_vfull = b.res("vfull")
            qa = [sba(f"qa{i}", [65, 512], BF16) for i in range(2)]
            r_qa_d = [b.res() for _ in range(2)]; r_qa_b = [b.res() for _ in range(2)]
            ka = [sba(f"ka{i}", [65, T], BF16) for i in range(2)]
            r_ka_d = [b.res() for _ in range(2)]; r_ka_o = [b.res() for _ in range(2)]
            sqt = [sba(f"sqt{i}", [64, 512], BF16) for i in range(2)]
            r_sqt = [b.res() for _ in range(2)]
            kmx = sba("kmx", [1, 16], F32); r_kmx = b.res()
            sc2 = [sba(f"sc2_{i}", [1, 1], F32) for i in range(2)]
            r_sc2 = [b.res() for _ in range(2)]
            lnq = [sba(f"lnq{i}", [1, 512], F32) for i in range(2)]
            r_lnq = [b.res() for _ in range(2)]
            P_sb = sba("P_sb", [128, 16 * 512], BF16)
            r_P = [b.res() for _ in range(16)]
            rcp = [sba(f"rcp{i}", [1, 512], F32) for i in range(2)]
            r_rcp = [b.res() for _ in range(2)]
            bcn = [sba(f"bcn{i}", [64, 512], F32) for i in range(2)]
            r_bcn = [b.res() for _ in range(2)]
            att = sba("att", [128, 8 * 512], BF16); r_att = b.res()
            _at = sba("attmp0", [64, 512], BF16)
            att_tmp = [_at, _at]
            _ra = b.res()
            r_attmp = [_ra, _ra]
            zt = [sba(f"zt{i}", [128, D], F32) for i in range(2)]
            r_zt = [b.res() for _ in range(2)]
            y1_t = [sba(f"y1t{i}", [128, D], F32) for i in range(2)]
            r_y1t = [b.res() for _ in range(2)]
            xn0 = [sba(f"xn0_{i}", [128, D], F32) for i in range(2)]
            r_xn0 = [b.res() for _ in range(2)]
            xn_st = [sba(f"xnst{i}", [128, 8 * 128], BF16) for i in range(2)]
            r_xnst = [b.res() for _ in range(2)]
            lg_t = [sba(f"lgt{i}", [8, 128], F32) for i in range(2)]
            r_lgt = [b.res() for _ in range(2)]
            gt0_all = sba("gt0_all", [128, 16 * E], F32); r_gt0a = b.res()
            gsub = sba("gsub", [128, 16 * E], F32); r_gsub = b.res()
            gp8 = sba("gp8", [128, 16 * E], F32); r_gp8 = b.res()
            gprob = sba("gprob", [128, 16 * E], F32); r_gprob = b.res()
            gam = sba("gam", [128, 16 * E], F32); r_gam = b.res()
            gbm = sba("gbm", [128, 16 * E], F32); r_gbm = b.res()
            gis1 = sba("gis1", [128, 16 * E], F32); r_gis1 = b.res()
            gpmx, r_gpmx = gp8, r_gp8
            gmsk, r_gmsk = gt0_all, r_gt0a
            gtmp, r_gtmp = gbm, r_gbm
            gm1 = sba("gm1", [128, 16], F32); r_gm1 = b.res()
            gsum = sba("gsum", [128, 16], F32); r_gsum = b.res()
            grs = sba("grs", [128, 16], F32); r_grs = b.res()
            gt1 = sba("gt1", [128, 16], F32); r_gt1 = b.res()
            gt2 = sba("gt2", [128, 16], F32); r_gt2 = b.res()
            gden = sba("gden", [128, 16], F32); r_gden = b.res()
            gtm_st, r_gtmst = gprob, r_gprob
            gfm_st = sba("gfm_st", [8, 512], F32); r_gfmst = b.res()

            gv = lambda t: t[:, :].rearrange("p (g e) -> p g e", e=E)
            g1 = lambda t: t[:, :].rearrange("p (g o) -> p g o", o=1)

            def bTT(out_t, rout, a_t, ra, b_t, rb, op):
                a2, b2 = broadcast_tensor_aps(gv(a_t), g1(b_t))
                b.op("dve", "tensor_tensor", gv(out_t), a2, b2, op,
                     reads=[ra, rb], writes=[rout])

            def plain_ln(pi, src, rsrc, dst, rdst):
                b.op("dve", "bn_stats", bn6[pi][:, 0:6], src[:, 0:512],
                     reads=[rsrc], writes=[r_bn6[pi]])
                b.op("dve", "bn_stats", bn6[pi][:, 6:12], src[:, 512:1024],
                     reads=[rsrc], writes=[r_bn6[pi]])
                b.op("dve", "bn_aggr", mv[pi][:], bn6[pi][:, 0:12],
                     reads=[r_bn6[pi]], writes=[r_mv[pi]])
                b.op("dve", "tensor_scalar", vpe[pi][:], mv[pi][:, 1:2], EPS, None, ALU.add,
                     reads=[r_mv[pi]], writes=[r_vpe[pi]])
                b.op("act", "activation", rsd[pi][:], vpe[pi][:], AF.Ln,
                     reads=[r_vpe[pi]], writes=[r_rsd[pi]])
                b.op("act", "activation", rsd[pi][:], rsd[pi][:], AF.Exp, scale=-0.5,
                     reads=[r_rsd[pi]], writes=[r_rsd[pi]])
                b.op("dve", "scalar_tensor_tensor", ngm[pi][:], mv[pi][:, 0:1], -1.0,
                     rsd[pi][:], ALU.mult, ALU.mult,
                     reads=[r_mv[pi], r_rsd[pi]], writes=[r_ngm[pi]])
                b.op("dve", "tensor_scalar", dst[:], src[:], rsd[pi][:], ngm[pi][:],
                     ALU.mult, ALU.add,
                     reads=[rsrc, r_rsd[pi], r_ngm[pi]], writes=[rdst])

            for s in range(P_SEQ):
                for dc in range(8):
                    b.op("pool", "dma_start", out=wvo_sb[:, dc * D:(dc + 1) * D],
                         in_=wv_scr[dc], reads=[r_wv_scr], writes=[r_wvo])
                if s == 0:
                    for tc in range(16):
                        b.op("dve", "memset",
                             v_full[:, tc * H * 65:(tc + 1) * H * 65].rearrange(
                                 "p (h c) -> p h c", c=65)[:, :, 64:65],
                             1.0, writes=[r_vfull], late=True)
                def a1_ln(tt):
                    pi = tt % 2
                    xb = x_t[pi]; rx = r_xt[pi]
                    b.op("sp", "dma_start", out=xb[:],
                         in_=x_in[s, tt * 128:(tt + 1) * 128, :], writes=[rx])
                    plain_ln(pi, xb, rx, u0[pi], r_u0[pi])
                    b.op("dve", "tensor_tensor", u0[pi][:], u0[pi][:], ga_bc[:], ALU.mult,
                         reads=[r_u0[pi], r_consts], writes=[r_u0[pi]])
                    b.op("dve", "tensor_tensor", u0[pi][:], u0[pi][:], ba_bc[:], ALU.add,
                         reads=[r_u0[pi], r_consts], writes=[r_u0[pi]])

                def a1_tr(tt):
                    pi = tt % 2
                    for half in range(2):
                        bank = 6 + half
                        for j in range(4):
                            dc = half * 4 + j
                            b.op("ten", "transpose", pb[bank][0:128, j * 128:(j + 1) * 128],
                                 u0[pi][:, dc * 128:(dc + 1) * 128], idn_f[:],
                                 reads=[r_u0[pi], r_consts], writes=[pres[bank]])
                        dst = u_fm[:, 0:8 * T].rearrange("p (dc t) -> p dc t", dc=8)[
                            :, half * 4:(half + 1) * 4, tt * 128:(tt + 1) * 128]
                        b.op("dve", "tensor_copy", dst,
                             pb[bank][0:128, 0:512].rearrange("p (dc t) -> p dc t", dc=4),
                             reads=[pres[bank]], writes=[r_ufm])

                a1_ln(0)
                for tt in range(16):
                    if tt + 1 < 16:
                        a1_ln(tt + 1)
                    a1_tr(tt)
                for oc in range(16):
                    wb = wqk_buf[oc % 2]; rw = r_wqk[oc % 2]
                    b.op("pool", "dma_start",
                         out=wb[:].rearrange("p (dc c) -> p dc c", dc=8),
                         in_=wqk_scr[oc].rearrange("dc p c -> p dc c"),
                         reads=[r_wqk_scr], writes=[rw])
                    stg = qk_stage[oc % 2]; rstg = r_qks[oc % 2]
                    for ncn in range(4):
                        bank = ncn % 3
                        for dc in range(8):
                            b.op("ten", "matmul", pb[bank][0:128, 0:512],
                                 wb[:, dc * 128:(dc + 1) * 128],
                                 u_fm[:, dc * T + ncn * 512: dc * T + (ncn + 1) * 512],
                                 start=(dc == 0), stop=(dc == 7),
                                 reads=[rw, r_ufm], writes=[pres[bank]])
                        for hh in range(2):
                            b.op("act", "activation",
                                 stg[64 * hh:64 * (hh + 1), ncn * 512:(ncn + 1) * 512],
                                 pb[bank][64 * hh:64 * (hh + 1), 0:512], AF.Identity,
                                 bias=bq_col[64 * hh:64 * (hh + 1), oc:oc + 1],
                                 reads=[pres[bank], r_consts], writes=[rstg])
                    b.op("sp", "dma_start", out=qk_park[oc // 8, oc % 8], in_=stg[:],
                         reads=[rstg], writes=[r_qkpark])
                for tc in range(16):
                    for dh in range(2):
                        bank = 3 + dh
                        for dc in range(8):
                            b.op("ten", "matmul", pb[bank][0:128, 0:512],
                                 u_fm[:, dc * T + tc * 128: dc * T + (tc + 1) * 128],
                                 wvo_sb[:, dc * D + dh * 512: dc * D + (dh + 1) * 512],
                                 start=(dc == 0), stop=(dc == 7),
                                 reads=[r_ufm, r_wvo], writes=[pres[bank]])
                        for j in range(8):
                            h = dh * 8 + j
                            b.op("dve", "tensor_tensor",
                                 v_full[:, tc * H * 65 + h * 65: tc * H * 65 + h * 65 + 64],
                                 pb[bank][0:128, j * 64:(j + 1) * 64],
                                 bvr_bc[:, dh * 512 + j * 64: dh * 512 + (j + 1) * 64],
                                 ALU.add, reads=[pres[bank], r_consts], writes=[r_vfull])
                for dc in range(8):
                    b.op("pool", "dma_start", out=wvo_sb[:, dc * D:(dc + 1) * D],
                         in_=wout_scr[dc], reads=[r_wout_scr], writes=[r_wvo])

                if True:
                    _eper = E // P_SEQ
                    for e in range(_eper * s, _eper * (s + 1)):
                        for fc in range(32):
                            src = w1_in[e, :, fc * 128:(fc + 1) * 128].rearrange(
                                "(dc p) f -> dc p f", p=128)
                            b.op("pool", "dma_start", out=w1_scr[e, fc], in_=src,
                                 writes=[r_w1_scr[e]])
                        for fc in range(32):
                            src = w2_in[e, fc * 128:(fc + 1) * 128, :].rearrange(
                                "p (dh j) -> dh p j", j=512)
                            b.op("pool", "dma_start", out=w2_scr[e, :, fc], in_=src,
                                 writes=[r_w2_scr[e]])

                def prep_loads(qc, h):
                    pi = h % 2
                    qb = qa[pi]; kb = ka[pi]
                    b.op("sp", "dma_start", out=qb[0:64, :],
                         in_=qk_park[0, h // 2, 64 * (h % 2):64 * (h % 2) + 64,
                                     qc * 512:(qc + 1) * 512],
                         reads=[r_qkpark], writes=[r_qa_d[pi]])
                    b.op("sp", "dma_start", out=kb[0:64, :],
                         in_=qk_park[1, h // 2, 64 * (h % 2):64 * (h % 2) + 64, :],
                         reads=[r_qkpark], writes=[r_ka_d[pi]])
                    if s == 0 and qc == 0 and h < 2:
                        b.op("dve", "memset", kb[64:65, :], 1.0, writes=[r_ka_o[pi]],
                             late=True)
                    b.op("dve", "tensor_tensor", sqt[pi][:, :], qb[0:64, :], qb[0:64, :],
                         ALU.mult, reads=[r_qa_d[pi]], writes=[r_sqt[pi]])

                def prep_bound(qc, h):
                    pi = h % 2
                    qb = qa[pi]; kb = ka[pi]
                    nb = 5 + pi
                    if qc == 0:
                        b.op("dve", "memset", kmx[0:1, h:h + 1], -1e30, writes=[r_kmx])
                        for i in range(4):
                            b.op("dve", "tensor_tensor", sqt[pi][:, :],
                                 kb[0:64, i * 512:(i + 1) * 512],
                                 kb[0:64, i * 512:(i + 1) * 512], ALU.mult,
                                 reads=[r_ka_d[pi]], writes=[r_sqt[pi]])
                            b.op("ten", "matmul", pb[nb][0:1, 0:512], ones_c64[:],
                                 sqt[pi][:], start=True, stop=True,
                                 reads=[r_sqt[pi], r_consts], writes=[pres[nb]])
                            b.op("dve", "tensor_reduce", sc2[pi][:], pb[nb][0:1, 0:512],
                                 AX.X, ALU.max, reads=[pres[nb]], writes=[r_sc2[pi]])
                            b.op("dve", "tensor_tensor", kmx[0:1, h:h + 1],
                                 kmx[0:1, h:h + 1], sc2[pi][:], ALU.max,
                                 reads=[r_sc2[pi], r_kmx], writes=[r_kmx])
                        b.op("act", "activation", kmx[0:1, h:h + 1], kmx[0:1, h:h + 1],
                             AF.Ln, reads=[r_kmx], writes=[r_kmx])
                        b.op("dve", "tensor_scalar", kmx[0:1, h:h + 1],
                             kmx[0:1, h:h + 1], 0.5, None, ALU.mult,
                             reads=[r_kmx], writes=[r_kmx])
                        b.op("dve", "tensor_tensor", sqt[pi][:, :], qb[0:64, :], qb[0:64, :],
                             ALU.mult, reads=[r_qa_d[pi]], writes=[r_sqt[pi]])
                    b.op("ten", "matmul", pb[nb][0:1, 0:512], ones_c64[:], sqt[pi][:],
                         start=True, stop=True, reads=[r_sqt[pi], r_consts],
                         writes=[pres[nb]])
                    b.op("act", "activation", lnq[pi][:], pb[nb][0:1, 0:512], AF.Ln,
                         reads=[pres[nb]], writes=[r_lnq[pi]])
                    b.op("act", "activation", lnq[pi][:], lnq[pi][:], AF.Exp,
                         scale=0.5, bias=kmx[0:1, h:h + 1],
                         reads=[r_lnq[pi], r_kmx], writes=[r_lnq[pi]])
                    b.op("dve", "tensor_scalar", qb[64:65, :], lnq[pi][:], -1.0, None,
                         ALU.mult, reads=[r_lnq[pi]], writes=[r_qa_b[pi]])

                def core(qc, h, inject):
                    pi = h % 2
                    qb = qa[pi]; kb = ka[pi]
                    pvbank = 3 + pi

                    def plan_scores(kc):
                        bank = kc % 3
                        b.op("ten", "matmul", pb[bank][0:128, 0:512],
                             kb[:, kc * 128:(kc + 1) * 128], qb[:, :],
                             start=True, stop=True,
                             reads=[r_ka_d[pi], r_ka_o[pi], r_qa_d[pi], r_qa_b[pi]],
                             writes=[pres[bank]])

                    def plan_exp(kc):
                        bank = kc % 3
                        b.op("act", "activation", P_sb[:, kc * 512:(kc + 1) * 512],
                             pb[bank][0:128, 0:512], AF.Exp, scale=0.125,
                             reads=[pres[bank]], writes=[r_P[kc]])

                    def plan_pv(kc):
                        b.op("ten", "matmul", pb[pvbank][0:65, 0:512],
                             v_full[:, kc * H * 65 + h * 65: kc * H * 65 + (h + 1) * 65],
                             P_sb[:, kc * 512:(kc + 1) * 512],
                             start=(kc == 0), stop=(kc == 15),
                             reads=[r_vfull, r_P[kc]], writes=[pres[pvbank]])

                    plan_scores(0)
                    plan_scores(1)
                    plan_scores(2)
                    for kc in range(16):
                        plan_exp(kc)
                        if kc + 3 < 16:
                            plan_scores(kc + 3)
                        if kc == 10 and inject is not None:
                            inject()
                        plan_pv(kc)
                    b.op("dve", "reciprocal", rcp[pi][:], pb[pvbank][64:65, 0:512],
                         reads=[pres[pvbank]], writes=[r_rcp[pi]])

                def norm_tail(h):
                    pi = h % 2
                    pvbank = 3 + pi
                    b.op("ten", "matmul", pb[7][0:64, 0:512], ones_r64f[:], rcp[pi][:],
                         start=True, stop=True, reads=[r_rcp[pi], r_consts],
                         writes=[pres[7]])
                    b.op("act", "activation", bcn[pi][:], pb[7][0:64, 0:512], AF.Identity,
                         reads=[pres[7]], writes=[r_bcn[pi]])
                    if h % 2 == 0:
                        b.op("dve", "tensor_tensor",
                             att[0:64, (h // 2) * 512:(h // 2 + 1) * 512],
                             pb[pvbank][0:64, 0:512], bcn[pi][:], ALU.mult,
                             reads=[pres[pvbank], r_bcn[pi]], writes=[r_att])
                    else:
                        at = att_tmp[(h // 2) % 2]; rat = r_attmp[(h // 2) % 2]
                        b.op("dve", "tensor_tensor", at[:],
                             pb[pvbank][0:64, 0:512], bcn[pi][:], ALU.mult,
                             reads=[pres[pvbank], r_bcn[pi]], writes=[rat])
                        b.op("sp", "dma_start",
                             out=att[64:128, (h // 2) * 512:(h // 2 + 1) * 512],
                             in_=at[:], reads=[rat], writes=[r_att])

                for qc in range(4):
                    prep_loads(qc, 0)
                    prep_bound(qc, 0)
                    for h in range(H):
                        if h + 1 < H:
                            prep_loads(qc, h + 1)
                            inject = (lambda hh=h + 1, qq=qc: prep_bound(qq, hh))
                        else:
                            inject = None
                        core(qc, h, inject)
                        if h > 0:
                            norm_tail(h - 1)
                    norm_tail(H - 1)
                    def a3_mm(tt4):
                        tt = qc * 4 + tt4
                        p2 = tt4 % 2
                        xb = x_t[p2]; rx = r_xt[p2]
                        b.op("sp", "dma_start", out=xb[:],
                             in_=x_in[s, tt * 128:(tt + 1) * 128, :], writes=[rx])
                        for dh in range(2):
                            bank = 2 * (tt4 % 2) + dh
                            for dc in range(8):
                                b.op("ten", "matmul", pb[bank][0:128, 0:512],
                                     att[:, dc * 512 + tt4 * 128: dc * 512 + (tt4 + 1) * 128],
                                     wvo_sb[:, dc * D + dh * 512: dc * D + (dh + 1) * 512],
                                     start=(dc == 0), stop=(dc == 7),
                                     reads=[r_att, r_wvo], writes=[pres[bank]])

                    def a3_post(tt4):
                        tt = qc * 4 + tt4
                        gtok = s * T + tt * 128
                        p2 = tt4 % 2
                        xb = x_t[p2]; rx = r_xt[p2]
                        for dh in range(2):
                            bank = 2 * (tt4 % 2) + dh
                            b.op("act", "activation", zt[p2][:, dh * 512:(dh + 1) * 512],
                                 pb[bank][0:128, 0:512], AF.Identity,
                                 reads=[pres[bank]], writes=[r_zt[p2]])
                        b.op("dve", "tensor_tensor", zt[p2][:], zt[p2][:], bout_bc[:],
                             ALU.add, reads=[r_zt[p2], r_consts], writes=[r_zt[p2]])
                        b.op("dve", "scalar_tensor_tensor", zt[p2][:], xb[:], 2.0, zt[p2][:],
                             ALU.mult, ALU.add, reads=[rx, r_zt[p2]], writes=[r_zt[p2]])
                        plain_ln(p2, zt[p2], r_zt[p2], y1_t[p2], r_y1t[p2])
                        b.op("sp", "dma_start", out=y1_park[gtok:gtok + 128, :],
                             in_=y1_t[p2][:], reads=[r_y1t[p2]], writes=[r_y1park])
                        plain_ln(p2, y1_t[p2], r_y1t[p2], xn0[p2], r_xn0[p2])
                        b.op("dve", "tensor_tensor", xn0[p2][:], xn0[p2][:], gf_bc[:],
                             ALU.mult, reads=[r_xn0[p2], r_consts], writes=[r_xn0[p2]])
                        b.op("dve", "tensor_tensor", xn0[p2][:], xn0[p2][:], bf_bc[:],
                             ALU.add, reads=[r_xn0[p2], r_consts], writes=[r_xn0[p2]])
                        for half in range(2):
                            bank = 6 + half
                            for j in range(4):
                                dc = half * 4 + j
                                b.op("ten", "transpose",
                                     pb[bank][0:128, j * 128:(j + 1) * 128],
                                     xn0[p2][:, dc * 128:(dc + 1) * 128], idn_f[:],
                                     reads=[r_xn0[p2], r_consts], writes=[pres[bank]])
                            b.op("dve", "tensor_copy",
                                 xn_st[p2][:, half * 512:(half + 1) * 512].rearrange(
                                     "p (dc t) -> p dc t", dc=4),
                                 pb[bank][0:128, 0:512].rearrange("p (dc t) -> p dc t", dc=4),
                                 reads=[pres[bank]], writes=[r_xnst[p2]])
                        b.op("sp", "dma_start", out=xn_park[gtok // 128],
                             in_=xn_st[p2][:], reads=[r_xnst[p2]], writes=[r_xnpark])
                        for dc in range(8):
                            b.op("ten", "matmul", pb[5][0:8, 0:128],
                                 wg_col[:, dc * 8:(dc + 1) * 8],
                                 xn_st[p2][:, dc * 128:(dc + 1) * 128],
                                 start=(dc == 0), stop=(dc == 7),
                                 reads=[r_consts, r_xnst[p2]], writes=[pres[5]])
                        b.op("act", "activation", lg_t[p2][:], pb[5][0:8, 0:128], AF.Identity,
                             reads=[pres[5]], writes=[r_lgt[p2]])
                        b.op("ten", "transpose", pb[5][0:128, 0:8], lg_t[p2][:],
                             idn_f[0:8, 0:8], reads=[r_lgt[p2], r_consts], writes=[pres[5]])
                        b.op("act", "activation", gt0_all[:, tt * E:(tt + 1) * E],
                             pb[5][0:128, 0:8], AF.Identity,
                             reads=[pres[5]], writes=[r_gt0a])

                    a3_mm(0)
                    for tt4 in range(4):
                        if tt4 + 1 < 4:
                            a3_mm(tt4 + 1)
                        a3_post(tt4)
                b.op("dve", "tensor_reduce", gm1[:], gv(gt0_all), AX.X, ALU.max,
                     reads=[r_gt0a], writes=[r_gm1])
                bTT(gsub, r_gsub, gt0_all, r_gt0a, gm1, r_gm1, ALU.subtract)
                b.op("act", "activation", gp8[:], gsub[:], AF.Exp,
                     reads=[r_gsub], writes=[r_gp8])
                b.op("dve", "tensor_reduce", gsum[:], gv(gp8), AX.X, ALU.add,
                     reads=[r_gp8], writes=[r_gsum])
                b.op("dve", "reciprocal", grs[:], gsum[:], reads=[r_gsum], writes=[r_grs])
                bTT(gprob, r_gprob, gp8, r_gp8, grs, r_grs, ALU.mult)
                b.op("dve", "tensor_reduce", gt1[:], gv(gprob), AX.X, ALU.max,
                     reads=[r_gprob], writes=[r_gt1])
                b.op("dve", "memset", gpmx[:], -1e30, writes=[r_gpmx])
                for j in range(1, E):
                    b.op("dve", "tensor_tensor", gv(gpmx)[:, :, j:j + 1],
                         gv(gpmx)[:, :, j - 1:j], gv(gprob)[:, :, j - 1:j], ALU.max,
                         reads=[r_gpmx, r_gprob], writes=[r_gpmx], late=True)
                bTT(gam, r_gam, gprob, r_gprob, gt1, r_gt1, ALU.is_ge)
                bTT(gbm, r_gbm, gpmx, r_gpmx, gt1, r_gt1, ALU.is_lt)
                b.op("dve", "tensor_tensor", gis1[:], gam[:], gbm[:], ALU.mult,
                     reads=[r_gam, r_gbm], writes=[r_gis1])
                b.op("dve", "scalar_tensor_tensor", gmsk[:], gis1[:], -1e30, gprob[:],
                     ALU.mult, ALU.add, reads=[r_gis1, r_gprob], writes=[r_gmsk])
                b.op("dve", "tensor_reduce", gt2[:], gv(gmsk), AX.X, ALU.max,
                     reads=[r_gmsk], writes=[r_gt2])
                b.op("dve", "memset", gpmx[:], -1e30, writes=[r_gpmx])
                for j in range(1, E):
                    b.op("dve", "tensor_tensor", gv(gpmx)[:, :, j:j + 1],
                         gv(gpmx)[:, :, j - 1:j], gv(gmsk)[:, :, j - 1:j], ALU.max,
                         reads=[r_gpmx, r_gmsk], writes=[r_gpmx], late=True)
                bTT(gam, r_gam, gmsk, r_gmsk, gt2, r_gt2, ALU.is_ge)
                bTT(gbm, r_gbm, gpmx, r_gpmx, gt2, r_gt2, ALU.is_lt)
                b.op("dve", "tensor_tensor", gsub[:], gam[:], gbm[:], ALU.mult,
                     reads=[r_gam, r_gbm], writes=[r_gsub])
                b.op("dve", "tensor_tensor", gden[:], gt1[:], gt2[:], ALU.add,
                     reads=[r_gt1, r_gt2], writes=[r_gden])
                b.op("dve", "tensor_scalar", gden[:], gden[:], 1e-9, None, ALU.add,
                     reads=[r_gden], writes=[r_gden])
                b.op("dve", "reciprocal", grs[:], gden[:], reads=[r_gden], writes=[r_grs])
                b.op("dve", "tensor_tensor", gt1[:], gt1[:], grs[:], ALU.mult,
                     reads=[r_gt1, r_grs], writes=[r_gt1])
                b.op("dve", "tensor_tensor", gt2[:], gt2[:], grs[:], ALU.mult,
                     reads=[r_gt2, r_grs], writes=[r_gt2])
                bTT(gtmp, r_gtmp, gsub, r_gsub, gt2, r_gt2, ALU.mult)
                bTT(gam, r_gam, gis1, r_gis1, gt1, r_gt1, ALU.mult)
                b.op("dve", "tensor_tensor", gtm_st[:], gam[:], gtmp[:], ALU.add,
                     reads=[r_gam, r_gtmp], writes=[r_gtmst])
                b.op("sp", "dma_start", out=gtm_park[s], in_=gtm_st[:],
                     reads=[r_gtmst], writes=[r_gtmpark])
                for tt in range(16):
                    b.op("ten", "transpose", pb[5][0:8, 0:128],
                         gtm_st[:, tt * E:(tt + 1) * E], idn_f[:],
                         reads=[r_gtmst, r_consts], writes=[pres[5]])
                    b.op("dve", "tensor_copy", gfm_st[:, (tt % 4) * 128:(tt % 4 + 1) * 128],
                         pb[5][0:8, 0:128], reads=[pres[5]], writes=[r_gfmst])
                    if tt % 4 == 3:
                        b.op("sp", "dma_start",
                             out=gfm_park[s, :, (tt // 4) * 512:(tt // 4 + 1) * 512],
                             in_=gfm_st[:], reads=[r_gfmst], writes=[r_gfmpark])

            b.hard_barrier()
            attn_end = b.nidx
            b.finalize_range(pro_end, attn_end)
            b.emit_range(block, pro_end, attn_end)

        with ExitStack() as es_m:
            def sbm(name, shape, dtype):
                return es_m.enter_context(nc.sbuf_tensor(name, shape, dtype))

            xn_blk = [sbm(f"xnb{i}", [128, 8 * TB], BF16) for i in range(2)]
            r_xnb = [b.res() for _ in range(2)]
            h_sb = sbm("h_sb", [128, 32 * TB], BF16)
            r_h = [b.res() for _ in range(32)]
            acc = sbm("acc", [128, 8 * D], F32)
            acc_r = [b.res(f"acc{i}") for i in range(8)]
            w1_buf = [sbm(f"w1b{i}", [128, 4 * 8 * 128], BF16) for i in range(2)]
            r_w1b = [b.res() for _ in range(2)]
            w2_buf = [sbm(f"w2b{i}", [128, 4 * 512], BF16) for i in range(2)]
            r_w2b = [b.res() for _ in range(2)]
            gtm_sb = [sbm(f"gtmsb{i}", [128, 8 * E], F32) for i in range(2)]
            r_gtm = [b.res() for _ in range(2)]
            gfm_sb = [sbm(f"gfmsb{i}", [E, TB], F32) for i in range(2)]
            r_gfm = [b.res() for _ in range(2)]
            ytmp = [sbm(f"ytmp{i}", [128, 512], F32) for i in range(2)]
            r_yt = [b.res() for _ in range(2)]
            y1b = [sbm(f"y1b{i}", [128, D], F32) for i in range(2)]
            r_y1b = [b.res() for _ in range(2)]
            outb = [sbm(f"outb{i}", [128, D], F32) for i in range(2)]
            r_outb = [b.res() for _ in range(2)]
            bn6m = [sbm(f"bn6m{i}", [128, 12], F32) for i in range(2)]
            r_bn6m = [b.res() for _ in range(2)]
            mvm = [sbm(f"mvm{i}", [128, 2], F32) for i in range(2)]
            r_mvm = [b.res() for _ in range(2)]
            vpem = [sbm(f"vpem{i}", [128, 1], F32) for i in range(2)]
            r_vpem = [b.res() for _ in range(2)]
            ngmm = [sbm(f"ngmm{i}", [128, 1], F32) for i in range(2)]
            r_ngmm = [b.res() for _ in range(2)]
            rsdm = [sbm(f"rsdm{i}", [128, 1], F32) for i in range(2)]
            r_rsdm = [b.res() for _ in range(2)]

            ntc = TB // 128
            for tb in range(NTB):
                s = tb // NHALF
                half = tb % NHALF
                tpi = tb % 2
                xnb = xn_blk[tpi]; rxnb = r_xnb[tpi]
                xview = xnb[:, :].rearrange("p (ti dc j) -> p ti dc j", ti=8, dc=8)
                for ti in range(8):
                    b.op("sp", "dma_start", out=xnb[:, ti * 1024:(ti + 1) * 1024],
                         in_=xn_park[tb * 8 + ti], reads=[r_xnpark], writes=[rxnb])
                b.op("sp", "dma_start", out=gtm_sb[tpi][:],
                     in_=gtm_park[s][:, half * 8 * E:(half + 1) * 8 * E],
                     reads=[r_gtmpark], writes=[r_gtm[tpi]])
                b.op("sp", "dma_start", out=gfm_sb[tpi][:],
                     in_=gfm_park[s][:, half * TB:(half + 1) * TB],
                     reads=[r_gfmpark], writes=[r_gfm[tpi]])
                def plan_acc_init():
                    for dh in range(2):
                        for tc in range(ntc):
                            b.op("ten", "matmul", pb[tc][0:128, 0:512],
                                 gfm_sb[tpi][:, tc * 128:(tc + 1) * 128],
                                 b2_sb[:, dh * 512:(dh + 1) * 512], start=True, stop=True,
                                 reads=[r_gfm[tpi], r_consts], writes=[pres[tc]])
                            b.op("act", "activation",
                                 acc[:, tc * D + dh * 512: tc * D + (dh + 1) * 512],
                                 pb[tc][0:128, 0:512], AF.Identity,
                                 reads=[pres[tc]], writes=[acc_r[tc]])
                for e in range(E):
                    for fcg in range(8):
                        wb = w1_buf[fcg % 2]; rw = r_w1b[fcg % 2]
                        b.op("pool", "dma_start",
                             out=wb[:].rearrange("p (fi dc f) -> p fi dc f", fi=4, dc=8),
                             in_=w1_scr[e, fcg * 4:(fcg + 1) * 4].rearrange(
                                 "fi dc p f -> p fi dc f"),
                             reads=[r_w1_scr[e]], writes=[rw])
                        for fi in range(4):
                            fc = fcg * 4 + fi
                            for th in range(2):
                                bank = (fc * 2 + th) % 3
                                for dc in range(8):
                                    b.op("ten", "matmul", pb[bank][0:128, 0:512],
                                         wb[:, (fi * 8 + dc) * 128:(fi * 8 + dc + 1) * 128],
                                         xview[:, th * 4:(th + 1) * 4, dc, :],
                                         start=(dc == 0), stop=(dc == 7),
                                         reads=[rw, rxnb], writes=[pres[bank]])
                                b.op("act", "activation",
                                     h_sb[:, fc * TB + th * 512: fc * TB + (th + 1) * 512],
                                     pb[bank][0:128, 0:512], AF.Gelu_apprx_tanh,
                                     bias=b1_col[:, fc * 8 + e: fc * 8 + e + 1],
                                     reads=[pres[bank], r_consts], writes=[r_h[fc]])
                    if "h_dump" in dbg and tb == 0 and e == 0:
                        b.op("sp", "dma_start", out=h_dump[:, :], in_=h_sb[:, :],
                             reads=[r_h[0], r_h[31]], writes=[r_out])
                    if e == 0:
                        plan_acc_init()
                    for dh in range(2):
                        for fcg in range(8):
                            wb = w2_buf[fcg % 2]; rw = r_w2b[fcg % 2]
                            b.op("pool", "dma_start",
                                 out=wb[:].rearrange("p (fi j) -> p fi j", fi=4),
                                 in_=w2_scr[e, dh, fcg * 4:(fcg + 1) * 4].rearrange(
                                     "fi p j -> p fi j"),
                                 reads=[r_w2_scr[e]], writes=[rw])
                            for fi in range(4):
                                fc = fcg * 4 + fi
                                for tc in range(ntc):
                                    b.op("ten", "matmul", pb[tc][0:128, 0:512],
                                         h_sb[:, fc * TB + tc * 128: fc * TB + (tc + 1) * 128],
                                         wb[:, fi * 512:(fi + 1) * 512],
                                         start=(fc == 0), stop=(fc == 31),
                                         reads=[rw, r_h[fc]], writes=[pres[tc]])
                        for tc in range(ntc):
                            yb = ytmp[tc % 2]; ry = r_yt[tc % 2]
                            b.op("act", "activation", yb[:], pb[tc][0:128, 0:512], AF.Identity,
                                 scale=gtm_sb[tpi][:, tc * E + e: tc * E + e + 1],
                                 reads=[pres[tc], r_gtm[tpi]], writes=[ry])
                            b.op("dve", "tensor_tensor",
                                 acc[:, tc * D + dh * 512: tc * D + (dh + 1) * 512],
                                 acc[:, tc * D + dh * 512: tc * D + (dh + 1) * 512],
                                 yb[:], ALU.add, reads=[ry, acc_r[tc]], writes=[acc_r[tc]])
                if "acc_dump" in dbg:
                    b.op("sp", "dma_start", out=acc_dump[tb], in_=acc[:, :],
                         reads=[acc_r[0], acc_r[1], acc_r[2], acc_r[3], acc_r[4],
                                acc_r[5], acc_r[6], acc_r[7]], writes=[r_out])
                for tc in range(ntc):
                    gtok = tb * TB + tc * 128
                    p2 = tc % 2
                    yb = y1b[p2]; ry = r_y1b[p2]
                    ob = outb[p2]; ro = r_outb[p2]
                    b.op("sp", "dma_start", out=yb[:], in_=y1_park[gtok:gtok + 128, :],
                         reads=[r_y1park], writes=[ry])
                    b.op("dve", "scalar_tensor_tensor", ob[:], yb[:], 2.0,
                         acc[:, tc * D:(tc + 1) * D], ALU.mult, ALU.add,
                         reads=[ry, acc_r[tc]], writes=[ro])
                    b.op("dve", "bn_stats", bn6m[p2][:, 0:6], ob[:, 0:512],
                         reads=[ro], writes=[r_bn6m[p2]])
                    b.op("dve", "bn_stats", bn6m[p2][:, 6:12], ob[:, 512:1024],
                         reads=[ro], writes=[r_bn6m[p2]])
                    b.op("dve", "bn_aggr", mvm[p2][:], bn6m[p2][:, 0:12],
                         reads=[r_bn6m[p2]], writes=[r_mvm[p2]])
                    b.op("dve", "tensor_scalar", vpem[p2][:], mvm[p2][:, 1:2], EPS, None,
                         ALU.add, reads=[r_mvm[p2]], writes=[r_vpem[p2]])
                    b.op("act", "activation", rsdm[p2][:], vpem[p2][:], AF.Ln,
                         reads=[r_vpem[p2]], writes=[r_rsdm[p2]])
                    b.op("act", "activation", rsdm[p2][:], rsdm[p2][:], AF.Exp, scale=-0.5,
                         reads=[r_rsdm[p2]], writes=[r_rsdm[p2]])
                    b.op("dve", "scalar_tensor_tensor", ngmm[p2][:], mvm[p2][:, 0:1], -1.0,
                         rsdm[p2][:], ALU.mult, ALU.mult,
                         reads=[r_mvm[p2], r_rsdm[p2]], writes=[r_ngmm[p2]])
                    b.op("dve", "tensor_scalar", ob[:], ob[:], rsdm[p2][:], ngmm[p2][:],
                         ALU.mult, ALU.add,
                         reads=[ro, r_rsdm[p2], r_ngmm[p2]], writes=[ro])
                    b.op("act", "dma_start",
                         out=out_d[s, half * TB + tc * 128: half * TB + (tc + 1) * 128, :],
                         in_=ob[:], reads=[ro], writes=[r_out])
            b.op("sp", "wait_only", reads=[r_out])

            moe_end = b.nidx
            b.finalize_range(attn_end, moe_end)
            b.emit_range(block, attn_end, moe_end)
    return nc


_NC_CACHE = {}


def _get_nc():
    if "nc" not in _NC_CACHE:
        _NC_CACHE["nc"] = build_program()
    return _NC_CACHE["nc"]


def kernel(**inputs):
    nc = _get_nc()
    x = np.ascontiguousarray(np.asarray(inputs["L0_attn_x"], dtype=np.float32))
    names = ["L0_attn_ln_gamma", "L0_attn_ln_beta", "L0_attn_w_qkv", "L0_attn_b_qkv",
             "L0_attn_w_out", "L0_attn_b_out", "L0_ffn_gamma", "L0_ffn_beta",
             "L0_ffn_Wg", "L0_ffn_W1", "L0_ffn_b1", "L0_ffn_W2", "L0_ffn_b2"]
    shared = {n: np.ascontiguousarray(np.asarray(inputs[n], dtype=np.float32)) for n in names}
    in_maps = []
    for c in range(NCORES):
        m = dict(shared)
        m["L0_attn_x"] = np.ascontiguousarray(x[c * 4:(c + 1) * 4])
        in_maps.append(m)
    res = run_bass_kernel_spmd(nc, in_maps, core_ids=list(range(NCORES)))
    out = np.concatenate([r["out"] for r in res.results], axis=0)
    return out
